# version 12
# baseline (speedup 1.0000x reference)
import numpy as np
import ml_dtypes
from contextlib import ExitStack

import concourse.bass as bass
import concourse.mybir as mybir
from concourse.bass_utils import run_bass_kernel_spmd
from concourse.bass import IndirectOffsetOnAxis

F32 = mybir.dt.float32
BF16 = mybir.dt.bfloat16
U32 = mybir.dt.uint32
AF = mybir.ActivationFunctionType
ALU = mybir.AluOpType
AX = mybir.AxisListType.X

BF = ml_dtypes.bfloat16

B, T, D = 2, 8192, 1024
NH, DH, WIN = 16, 64, 256
DF = 4096
NCORE = 8
ROWS = 2048
HALO = 256
TK = ROWS + HALO


ENGS = ("t", "sc", "v", "g", "s")


class Plan:

    def __init__(self, nc, stack):
        self.nc = nc
        self.stack = stack
        self.streams = {e: [] for e in ENGS}
        self.sem = {}
        self.free = []
        self.order = []

    def new_sem(self, name):
        assert name not in self.sem, name
        if self.free:
            h, n = self.free.pop()
            self.sem[name] = [h, 0]
        else:
            h = self.stack.enter_context(self.nc.semaphore(name))
            self.sem[name] = [h, 0]
        self.order.append(name)
        return name

    def new_dma_sem(self, name, kind="g"):
        assert name not in self.sem, name
        if not hasattr(self, "dma_pools"):
            self.dma_pools = {"g": [], "s": []}
            self.dma_cursor = {"g": 0, "s": 0}
        pool = self.dma_pools[kind]
        i = self.dma_cursor[kind]
        self.dma_cursor[kind] = i + 1
        if i >= len(pool):
            h = self.stack.enter_context(self.nc.semaphore(name))
            pool.append([h, 0])
        self.sem[name] = [pool[i][0], pool[i][1]]
        pool[i].append(name)
        return name

    def phase_reset(self):
        if hasattr(self, "dma_pools"):
            for kind in ("g", "s"):
                for ent in self.dma_pools[kind]:
                    while len(ent) > 2:
                        alias = ent.pop()
                        if alias in self.sem:
                            ent[1] = self.sem[alias][1]
                            del self.sem[alias]
                self.dma_cursor[kind] = 0

    def mark(self):
        return len(self.order)

    def release_since(self, mk):
        released = []
        for name in self.order[mk:]:
            ent = self.sem[name]
            released.append((ent[0], ent[1]))
            self.free.append([ent[0], 0])
            del self.sem[name]
        del self.order[mk:]
        return released

    def n(self, name):
        return self.sem[name][1]

    def emit(self, eng, fn):
        self.streams[eng].append(fn)

    def wait(self, eng, name, val=None):
        h, cnt = self.sem[name]
        v = cnt if val is None else val
        if v <= 0:
            return
        self.emit(eng, lambda e, h=h, v=v: e.wait_ge(h, v))

    def step(self, eng, name, fn, k=1):
        h, _ = self.sem[name]
        self.emit(eng, lambda e, h=h, k=k, fn=fn: fn(e).then_inc(h, k))
        self.sem[name][1] += k
        return self.sem[name][1]

    def realize(self, block):
        for ename, attr in (("t", "tensor"), ("sc", "scalar"), ("v", "vector"),
                            ("g", "gpsimd"), ("s", "sync")):
            stream = self.streams[ename]
            if not stream:
                continue
            deco = getattr(block, attr)

            def body(eng, stream=stream):
                for fn in stream:
                    fn(eng)
            deco(body)


class Arena:

    def __init__(self, nc, stack, name, n_bf16):
        self.t = stack.enter_context(nc.sbuf_tensor(name, [128, n_bf16], BF16))
        self.n = n_bf16
        self.off = 0
        self.marks = []

    def save(self):
        self.marks.append(self.off)

    def restore(self):
        self.off = self.marks.pop()

    def bf16(self, cols):
        o = self.off
        assert o + cols <= self.n, f"arena overflow {o}+{cols}>{self.n}"
        self.off = o + cols
        return self.t[:, o:o + cols]

    def f32(self, cols):
        if self.off % 2:
            self.off += 1
        ap = self.bf16(2 * cols)
        return ap.bitcast(F32)



def _rotary_tables(pos):
    inv_freq = (1.0 / 10000.0 ** (np.arange(0, DH, 2) / DH)).astype(np.float64)
    p = np.arange(128)
    freq = inv_freq[p % 32]
    ang = freq[:, None] * pos[None, :].astype(np.float64)
    cos = np.cos(ang).astype(np.float32)
    sin = np.sin(ang).astype(np.float32)
    sign = np.where(p % 64 < 32, -1.0, 1.0).astype(np.float32)
    return cos, (sin * sign[:, None]).astype(np.float32)


def _perm_swap32():
    Pm = np.zeros((128, 128), dtype=np.float32)
    for p in range(128):
        k = (p & ~63) | ((p + 32) % 64)
        Pm[k, p] = 1.0
    return Pm.astype(BF)


def _l0_masks(cp):
    out = np.zeros((128, 12 * 512), dtype=np.float32)
    kl = np.arange(128)[:, None]
    ql = np.arange(512)[None, :]
    for s in range(2):
        for ci in range(6):
            d = ql + 256 - 128 * ci - kl
            ok = (d >= 0) & (d < WIN)
            if cp == 0 and s == 0:
                kv_abs = -256 + 128 * ci + kl + 0 * ql
                ok = ok & (kv_abs >= 0)
            out[:, (s * 6 + ci) * 512:(s * 6 + ci + 1) * 512] = ok
    return out.astype(BF)


L0_RANGES = [(max(0, -256 + 128 * ci), min(512, -256 + 128 * ci + 384)) for ci in range(6)]


def _pretile2(w, blk=128):
    i, o = w.shape
    t = np.ascontiguousarray(w.reshape(i, o // blk, blk).transpose(1, 0, 2))
    return t.reshape((o // blk) * i, blk)


def stage_p0(inputs):
    x = inputs["L0_attn_x"]
    wqkvT = np.ascontiguousarray(inputs["L0_attn_w_qkv"].T)
    woutT = np.ascontiguousarray(inputs["L0_attn_w_out"].T)
    wqkT = _pretile2(wqkvT[:, :2048])
    wvT = np.ascontiguousarray(wqkvT[:, 2048:])
    woutTt = _pretile2(woutT)
    sel = np.zeros((1, 256), dtype=np.float32)
    sel[0, 0:64] = 1.0
    sel[0, 192:256] = 1.0
    ones_row = np.ones((1, 128), dtype=np.float32)
    statcol = np.full((128, 1), 1.0 / D, dtype=np.float32).astype(BF)
    onecol = np.ones((128, 1), dtype=np.float32).astype(BF)
    perm = _perm_swap32()

    maps = []
    for c in range(NCORE):
        b, cp = c // 4, c % 4
        q0 = 2048 * cp
        xT = np.zeros((D, TK), dtype=np.float32)
        lo = q0 - HALO
        src_lo = max(0, lo)
        xT[:, src_lo - lo:] = x[b, src_lo:q0 + ROWS].T
        pos = np.maximum(np.arange(lo, q0 + ROWS), 0)
        cosf, sinf = _rotary_tables(pos)
        maps.append({
            "p0_xT": np.ascontiguousarray(xT),
            "p0_wqkT": wqkT, "p0_wvT": wvT, "p0_woutT": woutTt,
            "p0_cos": cosf, "p0_sin": sinf,
            "p0_perm": perm, "p0_mask": _l0_masks(cp),
            "c_sel": sel, "c_ones_row": ones_row,
            "c_statcol": statcol, "c_onecol": onecol,
            "c_eps": np.full((1, 1), 1e-5, dtype=np.float32),
        })
    return maps



def emit_p0(P, nc, A, pool, dram, out_xA):
    st = P.stack

    gl_bulk = P.new_dma_sem("p0glb", "g")
    sl_bulk = P.new_dma_sem("p0slb", "s")
    gl_slab = P.new_dma_sem("p0gls", "g")
    gl_wout = P.new_dma_sem("p0glw", "g")
    sl_x0 = [P.new_dma_sem("p0slx0", "s"), P.new_dma_sem("p0slx1", "s")]
    so_p = [P.new_dma_sem("p0soa", "s"), P.new_dma_sem("p0sob", "s")]
    tA = P.new_sem("p0tA")
    tB = P.new_sem("p0tB")
    tC = P.new_sem("p0tC")
    tD = P.new_sem("p0tD")
    tE = P.new_sem("p0tE")
    tF = P.new_sem("p0tF")
    tG = P.new_sem("p0tG")
    tH = P.new_sem("p0tH")
    tV = P.new_sem("p0tV")
    tZ = P.new_sem("p0tZ")
    scA = P.new_sem("p0scA")
    scB = P.new_sem("p0scB")
    scC = P.new_sem("p0scC")
    scD = P.new_sem("p0scD")
    scE = P.new_sem("p0scE")
    scV = P.new_sem("p0scV")
    vA = P.new_sem("p0vA")
    vB = P.new_sem("p0vB")
    vC = P.new_sem("p0vC")
    vD = P.new_sem("p0vD")
    vE = P.new_sem("p0vE")
    vF = P.new_sem("p0vF")
    vG = P.new_sem("p0vG")
    vH = P.new_sem("p0vH")
    vS = P.new_sem("p0vS")
    scS = P.new_sem("p0scS")


    def slot(i, w=1):
        return pool[:, 512 * i:512 * (i + w)]
    pj = [slot(0), slot(1)]
    pv = slot(0, 2)
    psw = [slot(2), slot(3)]
    psc = [slot(4, 2), slot(6, 2)]
    pav = [slot(2), slot(3)]
    plp_e = pool[0:1, 0:512]
    plp_o = pool[0:1, 512:1024]
    plb = pool[:, 2048:2560]
    pz = slot(0)
    pst_m = pool[0:1, 0:512]
    pst_q = pool[0:1, 512:1024]
    pbr = slot(2, 2)

    attnf = A.bf16(8 * ROWS)
    maskb = A.bf16(12 * 512)
    permb = A.bf16(128)
    selb = A.f32(256)
    onesrow = A.f32(128)
    statcol = A.bf16(2)
    onecol = A.bf16(2)
    epsr = A.f32(1)
    linv = A.f32(1024)
    lbs = A.f32(512)

    A.save()
    xbf = A.bf16(8 * TK)
    wvb = A.bf16(8 * 1024)
    vbuf = A.bf16(18 * 1024)
    qkslab = [A.bf16(8 * 128) for _ in range(2)]
    qp = A.bf16(ROWS)
    kp = A.bf16(TK)
    qc = [A.bf16(512) for _ in range(2)]
    rt1 = [A.f32(512) for _ in range(2)]
    rt2 = [A.f32(512) for _ in range(2)]
    probs = A.bf16(12 * 512)
    cosb = A.f32(TK)
    sinb = A.f32(TK)

    for i in range(8):
        P.step("g", gl_bulk, lambda g, i=i: g.dma_start(
            out=xbf[:, i * TK:(i + 1) * TK], in_=dram["p0_xT"][i * 128:(i + 1) * 128, :]), 16)
    for i in range(8):
        P.step("g", gl_bulk, lambda g, i=i: g.dma_start(
            out=wvb[:, i * 1024:(i + 1) * 1024], in_=dram["p0_wvT"][i * 128:(i + 1) * 128, :]), 16)
    P.step("g", gl_bulk, lambda g: g.dma_start(out=permb[:, :], in_=dram["p0_perm"][:, :]), 16)
    P.step("g", gl_bulk, lambda g: g.dma_start(out=maskb[:, :], in_=dram["p0_mask"][:, :]), 16)
    P.step("g", gl_bulk, lambda g: g.dma_start(out=statcol[:, 0:1], in_=dram["c_statcol"][:, :]), 16)
    P.step("g", gl_bulk, lambda g: g.dma_start(out=onecol[:, 0:1], in_=dram["c_onecol"][:, :]), 16)
    P.step("s", sl_bulk, lambda s: s.dma_start(out=cosb[:, :], in_=dram["p0_cos"][:, :]), 16)
    P.step("s", sl_bulk, lambda s: s.dma_start(out=sinb[:, :], in_=dram["p0_sin"][:, :]), 16)
    P.step("s", sl_bulk, lambda s: s.dma_start(out=selb[0:1, :], in_=dram["c_sel"][:, :]), 16)
    P.step("s", sl_bulk, lambda s: s.dma_start(out=onesrow[0:1, :], in_=dram["c_ones_row"][:, :]), 16)
    P.step("s", sl_bulk, lambda s: s.dma_start(out=epsr[0:1, :], in_=dram["c_eps"][:, :]), 16)
    for e in ("t", "v", "sc"):
        P.wait(e, gl_bulk)
        P.wait(e, sl_bulk)

    P.step("v", vS, lambda v: v.memset(probs[:, :], 0.0))

    for tc in range(18):
        if tc >= 1:
            P.wait("t", scV, tc)
        for ic in range(8):
            for h in range(2):
                fn = lambda t, tc=tc, ic=ic, h=h: t.matmul(
                    pv[:, h * 512:(h + 1) * 512],
                    xbf[:, ic * TK + tc * 128: ic * TK + tc * 128 + 128],
                    wvb[:, ic * 1024 + h * 512: ic * 1024 + (h + 1) * 512],
                    start=(ic == 0), stop=(ic == 7))
                if ic == 7 and h == 1:
                    P.step("t", tV, fn)
                else:
                    P.emit("t", fn)
        P.wait("sc", tV, P.n(tV))
        P.step("sc", scV, lambda sc, tc=tc: sc.copy(
            out=vbuf[:, tc * 1024:(tc + 1) * 1024], in_=pv[:, :]))

    P.wait("t", scV, P.n(scV))
    tile_g = 0
    chunk_g = 0
    for hp in range(8):
        if hp >= 1:
            P.wait("g", tA, 9 * hp)
        P.step("g", gl_slab, lambda g, hp=hp: g.dma_start(
            out=qkslab[0].rearrange("p (c o) -> p c o", c=8),
            in_=dram["p0_wqkT"][hp * 1024:(hp + 1) * 1024, :].rearrange("(c p) o -> p c o", p=128)), 16)
        P.step("g", gl_slab, lambda g, hp=hp: g.dma_start(
            out=qkslab[1].rearrange("p (c o) -> p c o", c=8),
            in_=dram["p0_wqkT"][(hp + 8) * 1024:(hp + 9) * 1024, :].rearrange("(c p) o -> p c o", p=128)), 16)
        P.wait("t", gl_slab)

        P.wait("t", vD, P.n(vD))
        P.wait("t", vC, P.n(vC))
        tiles = []
        for i in range(4):
            tiles.append(("q", qp, 512 * i, 512, HALO + 512 * i))
        for i in range(5):
            w = 512 if i < 4 else 256
            tiles.append(("k", kp, 512 * i, w, 512 * i))
        for (kind, dest, dcol, w, xcol) in tiles:
            par = tile_g % 2
            slabi = 0 if kind == "q" else 1
            if tile_g >= 2:
                P.wait("t", vA, tile_g - 1)
            for ic in range(8):
                fn = lambda t, par=par, slabi=slabi, ic=ic, xcol=xcol, w=w: t.matmul(
                    pj[par][:, 0:w],
                    qkslab[slabi][:, ic * 128:(ic + 1) * 128],
                    xbf[:, ic * TK + xcol: ic * TK + xcol + w],
                    start=(ic == 0), stop=(ic == 7))
                if ic == 7:
                    P.step("t", tA, fn)
                else:
                    P.emit("t", fn)
            P.wait("sc", tA, P.n(tA))
            P.step("sc", scA, lambda sc, par=par, w=w: sc.copy(
                out=qc[par][:, 0:w], in_=pj[par][:, 0:w]))
            P.wait("t", scA, P.n(scA))
            P.step("t", tB, lambda t, par=par, w=w: t.matmul(
                psw[par][:, 0:w], permb[:, :], qc[par][:, 0:w], start=True, stop=True))
            P.wait("v", tA, P.n(tA))
            P.wait("v", tB, P.n(tB))
            if tile_g >= 2:
                P.wait("v", vA, tile_g - 1)
            P.step("v", vS, lambda v, par=par, w=w, xcol=xcol: v.tensor_mul(
                rt1[par][:, 0:w], pj[par][:, 0:w], cosb[:, xcol:xcol + w]))
            P.step("v", vS, lambda v, par=par, w=w, xcol=xcol: v.tensor_mul(
                rt2[par][:, 0:w], psw[par][:, 0:w], sinb[:, xcol:xcol + w]))
            P.wait("v", vS, P.n(vS))
            P.step("v", vA, lambda v, par=par, w=w, dest=dest, dcol=dcol: v.tensor_add(
                dest[:, dcol:dcol + w], rt1[par][:, 0:w], rt2[par][:, 0:w]))
            tile_g += 1

        rot_done = P.n(vA)
        P.wait("t", vA, rot_done)

        for qt in range(4):
            P.wait("t", vD, P.n(vD) - 1)
            mset = 0 if qt == 0 else 1
            for ci in range(6):
                lo, hi = L0_RANGES[ci]
                w = hi - lo
                kcol = 512 * qt + 128 * ci
                cp2 = chunk_g % 2
                if chunk_g >= 2:
                    P.wait("t", scB, 2 * (chunk_g - 1))
                if ci == 0:
                    P.wait("t", scC, P.n(scC))
                for par in range(2):
                    fn = lambda t, cp2=cp2, par=par, lo=lo, w=w, kcol=kcol, qt=qt: t.matmul(
                        psc[cp2][:, par * 512 + lo: par * 512 + lo + w],
                        kp[64 * par:64 * par + 64, kcol:kcol + 128],
                        qp[64 * par:64 * par + 64, 512 * qt + lo: 512 * qt + lo + w],
                        start=True, stop=True)
                    if par == 1:
                        P.step("t", tC, fn)
                    else:
                        P.emit("t", fn)
                P.wait("sc", tC, P.n(tC))
                P.step("sc", scB, lambda sc, cp2=cp2, ci=ci, lo=lo, w=w: sc.activation(
                    out=probs[:, (0 * 6 + ci) * 512 + lo: (0 * 6 + ci) * 512 + lo + w],
                    in_=psc[cp2][:, 0 * 512 + lo: 0 * 512 + lo + w], func=AF.Exp, scale=0.125))
                P.step("sc", scB, lambda sc, cp2=cp2, ci=ci, lo=lo, w=w: sc.activation(
                    out=probs[:, (1 * 6 + ci) * 512 + lo: (1 * 6 + ci) * 512 + lo + w],
                    in_=psc[cp2][:, 1 * 512 + lo: 1 * 512 + lo + w], func=AF.Exp, scale=0.125))
                P.wait("v", scB, P.n(scB))
                P.step("v", vB, lambda v, ci=ci, mset=mset: v.tensor_mul(
                    probs[:, (0 * 6 + ci) * 512: (0 * 6 + ci + 1) * 512],
                    probs[:, (0 * 6 + ci) * 512: (0 * 6 + ci + 1) * 512],
                    maskb[:, (mset * 6 + ci) * 512: (mset * 6 + ci + 1) * 512]))
                P.step("v", vB, lambda v, ci=ci, mset=mset: v.tensor_mul(
                    probs[:, (1 * 6 + ci) * 512: (1 * 6 + ci + 1) * 512],
                    probs[:, (1 * 6 + ci) * 512: (1 * 6 + ci + 1) * 512],
                    maskb[:, (mset * 6 + ci) * 512: (mset * 6 + ci + 1) * 512]))
                chunk_g += 1

            P.wait("t", vB, P.n(vB))
            P.wait("t", vC, P.n(vC))
            pavq = pav[qt % 2]
            for par in range(2):
                for ci in range(6):
                    vc = 4 * qt + ci
                    P.emit("t", lambda t, pavq=pavq, par=par, ci=ci, vc=vc, hp=hp: t.matmul(
                        pavq[64 * par:64 * par + 64, :],
                        vbuf[:, vc * 1024 + hp * 128 + par * 64: vc * 1024 + hp * 128 + par * 64 + 64],
                        probs[:, (par * 6 + ci) * 512: (par * 6 + ci + 1) * 512],
                        start=(ci == 0), stop=(ci == 5)))
            for par in range(2):
                plp = plp_e if par == 0 else plp_o
                for ci in range(6):
                    fn = lambda t, plp=plp, par=par, ci=ci: t.matmul(
                        plp[0:1, :], onecol[:, 0:1],
                        probs[:, (par * 6 + ci) * 512: (par * 6 + ci + 1) * 512],
                        start=(ci == 0), stop=(ci == 5))
                    if par == 1 and ci == 5:
                        P.step("t", tD, fn)
                    else:
                        P.emit("t", fn)
            P.wait("sc", tD, P.n(tD))
            P.step("sc", vC, lambda sc: sc.activation(
                out=linv[0:1, 0:512], in_=plp_e[0:1, :], func=AF.Ln))
            P.step("sc", vC, lambda sc: sc.activation(
                out=linv[0:1, 512:1024], in_=plp_o[0:1, :], func=AF.Ln))
            P.wait("sc", vC, P.n(vC))
            P.step("sc", vC, lambda sc: sc.activation(
                out=linv[0:1, 0:512], in_=linv[0:1, 0:512], func=AF.Exp, scale=-1.0))
            P.step("sc", vC, lambda sc: sc.activation(
                out=linv[0:1, 512:1024], in_=linv[0:1, 512:1024], func=AF.Exp, scale=-1.0))
            P.wait("t", vC, P.n(vC))
            P.wait("t", scC, P.n(scC))
            P.emit("t", lambda t: t.matmul(plb[:, :], selb[0:1, 0:128], linv[0:1, 0:512],
                                           start=True, stop=False))
            P.step("t", tE, lambda t: t.matmul(plb[:, :], selb[0:1, 128:256], linv[0:1, 512:1024],
                                               start=False, stop=True))
            P.wait("sc", tE, P.n(tE))
            P.step("sc", scC, lambda sc: sc.copy(out=lbs[:, :], in_=plb[:, :]))
            P.wait("v", scC, P.n(scC))
            P.step("v", vD, lambda v, pavq=pavq, hp=hp, qt=qt: v.tensor_mul(
                attnf[:, hp * ROWS + 512 * qt: hp * ROWS + 512 * qt + 512],
                pavq[:, :], lbs[:, :]))

    A.restore()
    zs = A.f32(8 * ROWS)
    woutb = A.bf16(8 * 128)
    x0t = [A.f32(512) for _ in range(2)]
    zb = A.bf16(512)
    zsq = A.bf16(512)
    stat_rows = A.f32(5 * 512)
    xAt = [A.f32(512) for _ in range(2)]

    all_attn = P.n(vD)
    P.wait("t", vD, all_attn)
    for e in ("s", "g", "v", "sc"):
        for sem in (vA, vB, vC, vD, vS, scA, scB, scC, scV):
            P.wait(e, sem)
    zi = 0
    for oc in range(8):
        P.step("g", gl_wout, lambda g, oc=oc: g.dma_start(
            out=woutb.rearrange("p (c o) -> p c o", c=8),
            in_=dram["p0_woutT"][oc * 1024:(oc + 1) * 1024, :].rearrange("(c p) o -> p c o", p=128)), 16)
        P.wait("t", gl_wout)
        for qt in range(4):
            par = zi % 2
            if zi >= 2:
                P.wait("s", vE, zi - 1)
            P.step("s", sl_x0[par], lambda s, oc=oc, qt=qt, par=par: s.dma_start(
                out=x0t[par][:, :],
                in_=dram["p0_xT"][oc * 128:(oc + 1) * 128, HALO + 512 * qt: HALO + 512 * qt + 512]), 16)
            if zi >= 1:
                P.wait("t", vE, zi)
            for ic in range(8):
                fn = lambda t, ic=ic, qt=qt: t.matmul(
                    pz[:, :], woutb[:, ic * 128:(ic + 1) * 128],
                    attnf[:, ic * ROWS + 512 * qt: ic * ROWS + 512 * qt + 512],
                    start=(ic == 0), stop=(ic == 7))
                if ic == 7:
                    P.step("t", tF, fn)
                else:
                    P.emit("t", fn)
            P.wait("v", tF, P.n(tF))
            P.wait("v", sl_x0[par], P.n(sl_x0[par]))
            P.step("v", vE, lambda v, oc=oc, qt=qt, par=par: v.tensor_add(
                zs[:, oc * ROWS + 512 * qt: oc * ROWS + 512 * qt + 512],
                pz[:, :], x0t[par][:, :]))
            zi += 1
        P.wait("g", tF, P.n(tF))

    z_done = P.n(vE)
    for e in ("t", "sc", "v"):
        P.wait(e, vE, z_done)
    ap_i = 0
    for qt in range(4):
        for oc in range(8):
            if oc >= 1:
                P.wait("v", tZ, P.n(tZ))
                P.wait("sc", tZ, P.n(tZ))
            P.step("v", vF, lambda v, oc=oc, qt=qt: v.tensor_copy(
                zb[:, :], zs[:, oc * ROWS + 512 * qt: oc * ROWS + 512 * qt + 512]))
            P.step("sc", scD, lambda sc, oc=oc, qt=qt: sc.activation(
                out=zsq[:, :], in_=zs[:, oc * ROWS + 512 * qt: oc * ROWS + 512 * qt + 512],
                func=AF.Square))
            P.wait("t", vF, P.n(vF))
            P.wait("t", scD, P.n(scD))
            P.emit("t", lambda t, oc=oc: t.matmul(
                pst_m[0:1, :], statcol[:, 0:1], zb[:, :], start=(oc == 0), stop=(oc == 7)))
            P.step("t", tZ, lambda t, oc=oc: t.matmul(
                pst_q[0:1, :], statcol[:, 0:1], zsq[:, :], start=(oc == 0), stop=(oc == 7)))
        P.wait("sc", tZ, 8 * (qt + 1))
        P.wait("sc", vG, P.n(vG))
        P.step("sc", scE, lambda sc: sc.copy(out=stat_rows[0:1, 0:512], in_=pst_m[0:1, :]))
        P.step("sc", scE, lambda sc: sc.activation(
            out=stat_rows[0:1, 512:1024], in_=pst_m[0:1, :], func=AF.Square))
        P.wait("v", scE, P.n(scE))
        P.step("v", vG, lambda v: v.tensor_sub(
            stat_rows[0:1, 1024:1536], pst_q[0:1, :], stat_rows[0:1, 512:1024]))
        P.wait("sc", vG, P.n(vG))
        P.step("sc", scS, lambda sc: sc.activation(
            out=stat_rows[0:1, 1536:2048], in_=stat_rows[0:1, 1024:1536],
            func=AF.Ln, bias=epsr[0:1, 0:1]))
        P.wait("sc", scS, P.n(scS))
        P.step("sc", scE, lambda sc: sc.activation(
            out=stat_rows[0:1, 1536:2048], in_=stat_rows[0:1, 1536:2048],
            func=AF.Exp, scale=-0.5))
        P.wait("v", scE, P.n(scE))
        P.step("v", vG, lambda v: v.tensor_mul(
            stat_rows[0:1, 2048:2560], stat_rows[0:1, 0:512], stat_rows[0:1, 1536:2048]))
        P.wait("t", vH, P.n(vH))
        P.wait("t", vG, P.n(vG))
        P.emit("t", lambda t: t.matmul(pbr[:, 0:512], onesrow[0:1, :],
                                       stat_rows[0:1, 1536:2048], start=True, stop=True))
        P.step("t", tH, lambda t: t.matmul(pbr[:, 512:1024], onesrow[0:1, :],
                                           stat_rows[0:1, 2048:2560], start=True, stop=True))
        P.wait("v", tH, P.n(tH))
        for oc in range(8):
            par = ap_i % 2
            if ap_i >= 2:
                P.wait("v", so_p[par], P.n(so_p[par]))
            P.step("v", vS, lambda v, oc=oc, qt=qt, par=par: v.tensor_mul(
                xAt[par][:, :], zs[:, oc * ROWS + 512 * qt: oc * ROWS + 512 * qt + 512],
                pbr[:, 0:512]))
            P.wait("v", vS, P.n(vS))
            P.step("v", vH, lambda v, par=par: v.tensor_sub(
                xAt[par][:, :], xAt[par][:, :], pbr[:, 512:1024]))
            P.wait("s", vH, P.n(vH))
            P.step("s", so_p[par], lambda s, oc=oc, qt=qt, par=par: s.dma_start(
                out=out_xA[oc * 128:(oc + 1) * 128, 512 * qt:512 * qt + 512],
                in_=xAt[par][:, :]), 16)
            ap_i += 1
    P.wait("s", so_p[0], P.n(so_p[0]))
    P.wait("s", so_p[1], P.n(so_p[1]))
    P.emit("s", lambda s: s.nop())
    return [so_p[0], so_p[1]]


def build_p0():
    nc = bass.Bass("TRN2", target_bir_lowering=False)
    dram = {
        "p0_xT": nc.dram_tensor("p0_xT", [D, TK], F32, kind="ExternalInput").ap(),
        "p0_wqkT": nc.dram_tensor("p0_wqkT", [16 * D, 128], F32, kind="ExternalInput").ap(),
        "p0_wvT": nc.dram_tensor("p0_wvT", [D, 1024], F32, kind="ExternalInput").ap(),
        "p0_woutT": nc.dram_tensor("p0_woutT", [8 * D, 128], F32, kind="ExternalInput").ap(),
        "p0_cos": nc.dram_tensor("p0_cos", [128, TK], F32, kind="ExternalInput").ap(),
        "p0_sin": nc.dram_tensor("p0_sin", [128, TK], F32, kind="ExternalInput").ap(),
        "p0_perm": nc.dram_tensor("p0_perm", [128, 128], BF16, kind="ExternalInput").ap(),
        "p0_mask": nc.dram_tensor("p0_mask", [128, 12 * 512], BF16, kind="ExternalInput").ap(),
        "c_sel": nc.dram_tensor("c_sel", [1, 256], F32, kind="ExternalInput").ap(),
        "c_ones_row": nc.dram_tensor("c_ones_row", [1, 128], F32, kind="ExternalInput").ap(),
        "c_statcol": nc.dram_tensor("c_statcol", [128, 1], BF16, kind="ExternalInput").ap(),
        "c_onecol": nc.dram_tensor("c_onecol", [128, 1], BF16, kind="ExternalInput").ap(),
        "c_eps": nc.dram_tensor("c_eps", [1, 1], F32, kind="ExternalInput").ap(),
    }
    out_xA = nc.dram_tensor("p0_xA", [D, ROWS], F32, kind="ExternalOutput").ap()
    stack = ExitStack()
    with stack:
        A = Arena(nc, stack, "arena", 104448)
        pool_t = stack.enter_context(nc.psum_tensor("pool", [128, 4096], F32))
        P = Plan(nc, stack)
        emit_p0(P, nc, A, pool_t, dram, out_xA)
        block = stack.enter_context(nc.Block())
        P.realize(block)
    return nc


def run_p0(inputs):
    maps = stage_p0(inputs)
    nc = build_p0()
    res = run_bass_kernel_spmd(nc, maps, core_ids=list(range(NCORE)))
    return [res.results[c]["p0_xA"] for c in range(NCORE)]



def emit_ln_fm(P, name, pool, bufs, n_sub, zsrc, sink, w=512, gb=None, sems=None):
    if sems is None:
        sems = {}
    def S(nm):
        if nm not in sems:
            sems[nm] = P.new_sem(name + nm)
        return sems[nm]
    tZ, tH, scE, scS2 = S("tZ"), S("tH"), S("scE"), S("scS2")
    vF, vG, vAp, vH = S("vF"), S("vG"), S("vAp"), S("vH")
    scGB = S("scGB") if gb is not None else None
    pst_m = pool[0:1, 0:w]
    pst_q = pool[0:1, 512:512 + w]
    pbr = pool[:, 1024:2048]
    zb, zsq, statr = bufs["zb"], bufs["zsq"], bufs["statr"]
    xnt = bufs["xnt"]
    statcol, onesrow, epsr = bufs["statcol"], bufs["onesrow"], bufs["epsr"]
    guards = sems.setdefault("_guards", [None, None])
    for sub in range(n_sub):
        for oc in range(8):
            if P.n(tZ) > 0:
                P.wait("v", tZ, P.n(tZ))
                P.wait("sc", tZ, P.n(tZ))
            P.step("v", vF, lambda v, oc=oc, sub=sub: v.tensor_copy(
                zb[:, 0:w], zsrc(oc, sub)))
            P.step("sc", scE, lambda sc, oc=oc, sub=sub: sc.activation(
                out=zsq[:, 0:w], in_=zsrc(oc, sub), func=AF.Square))
            P.wait("t", vF, P.n(vF))
            P.wait("t", scE, P.n(scE))
            P.emit("t", lambda t, oc=oc: t.matmul(
                pst_m[0:1, :], statcol[:, 0:1], zb[:, 0:w], start=(oc == 0), stop=(oc == 7)))
            P.step("t", tZ, lambda t, oc=oc: t.matmul(
                pst_q[0:1, :], statcol[:, 0:1], zsq[:, 0:w], start=(oc == 0), stop=(oc == 7)))
        P.wait("sc", tZ, P.n(tZ))
        P.wait("sc", vG, P.n(vG))
        P.step("sc", scE, lambda sc: sc.copy(out=statr[0:1, 0:w], in_=pst_m[0:1, :]))
        P.step("sc", scE, lambda sc: sc.activation(
            out=statr[0:1, 512:512 + w], in_=pst_m[0:1, :], func=AF.Square))
        P.wait("v", scE, P.n(scE))
        P.step("v", vG, lambda v: v.tensor_sub(
            statr[0:1, 1024:1024 + w], pst_q[0:1, :], statr[0:1, 512:512 + w]))
        P.wait("sc", vG, P.n(vG))
        P.step("sc", scS2, lambda sc: sc.activation(
            out=statr[0:1, 1536:1536 + w], in_=statr[0:1, 1024:1024 + w],
            func=AF.Ln, bias=epsr[0:1, 0:1]))
        P.wait("sc", scS2, P.n(scS2))
        P.step("sc", scE, lambda sc: sc.activation(
            out=statr[0:1, 1536:1536 + w], in_=statr[0:1, 1536:1536 + w],
            func=AF.Exp, scale=-0.5))
        P.wait("v", scE, P.n(scE))
        P.step("v", vG, lambda v: v.tensor_mul(
            statr[0:1, 2048:2048 + w], statr[0:1, 0:w], statr[0:1, 1536:1536 + w]))
        P.wait("t", vH, P.n(vH))
        if scGB is not None:
            P.wait("t", scGB, P.n(scGB))
        P.wait("t", vG, P.n(vG))
        P.emit("t", lambda t: t.matmul(pbr[:, 0:w], onesrow[0:1, :],
                                       statr[0:1, 1536:1536 + w], start=True, stop=True))
        P.step("t", tH, lambda t: t.matmul(pbr[:, 512:512 + w], onesrow[0:1, :],
                                           statr[0:1, 2048:2048 + w], start=True, stop=True))
        P.wait("v", tH, P.n(tH))
        for oc in range(8):
            par = sems.get("_appar", 0)
            sems["_appar"] = 1 - par
            if guards[par]:
                for (sname, v_) in guards[par]:
                    P.wait("v", sname, v_)
            P.step("v", vAp, lambda v, oc=oc, sub=sub, par=par: v.tensor_mul(
                xnt[par][:, 0:w], zsrc(oc, sub), pbr[:, 0:w]))
            P.wait("v", vAp, P.n(vAp))
            P.step("v", vH, lambda v, par=par: v.tensor_sub(
                xnt[par][:, 0:w], xnt[par][:, 0:w], pbr[:, 512:512 + w]))
            if gb is not None:
                gamma_ap, beta_ap = gb
                P.wait("sc", vH, P.n(vH))
                P.step("sc", scGB, lambda sc, oc=oc, par=par: sc.activation(
                    out=bufs["gbout"][par][:, 0:w], in_=xnt[par][:, 0:w], func=AF.Copy,
                    scale=gamma_ap(oc), bias=beta_ap(oc)))
                extra = [(scGB, P.n(scGB))]
            else:
                extra = []
            P.wait("v", vH, P.n(vH))
            guards[par] = extra + sink(sub, oc, par)
    return sems


def stage_p1(inputs):
    w1t = _pretile2(np.ascontiguousarray(inputs["L0_ffn_w1_w"].T))
    w2t = _pretile2(np.ascontiguousarray(inputs["L0_ffn_w2_w"].T))
    h1t = _pretile2(np.ascontiguousarray(inputs["L0_ffn_h1_w"].T))
    h2t = _pretile2(np.ascontiguousarray(inputs["L0_ffn_h2_w"].T))
    w3t = _pretile2(np.ascontiguousarray(inputs["L0_ffn_w3_w"].T))
    s1 = np.ascontiguousarray(inputs["L0_ffn_w1_s"].reshape(32, 128).T)
    s2 = np.ascontiguousarray(inputs["L0_ffn_w2_s"].reshape(32, 128).T)
    h1s = np.ascontiguousarray(inputs["L0_ffn_h1_s"].reshape(32, 128).T)
    h2s = np.ascontiguousarray(inputs["L0_ffn_h2_s"].reshape(32, 128).T)
    s3 = np.ascontiguousarray(inputs["L0_ffn_w3_s"].reshape(8, 128).T)
    wkt = _pretile2(np.ascontiguousarray(inputs["L1_attn_Wk"]))
    wv = np.ascontiguousarray(inputs["L1_attn_Wv"])
    shared = {
        "p1_w1t": w1t, "p1_w2t": w2t, "p1_h1t": h1t, "p1_h2t": h2t, "p1_w3t": w3t,
        "p1_s1": s1, "p1_s2": s2, "p1_h1s": h1s, "p1_h2s": h2s, "p1_s3": s3,
        "p1_wkt": wkt, "p1_wv": wv,
        "c_ones_row": np.ones((1, 128), dtype=np.float32),
        "c_statcol": np.full((128, 1), 1.0 / D, dtype=np.float32).astype(BF),
        "c_eps": np.full((1, 1), 1e-5, dtype=np.float32),
    }
    return [dict(shared) for _ in range(NCORE)]


def p1_dram_decls(nc):
    return {
        "p1_w1t": nc.dram_tensor("p1_w1t", [32 * 1024, 128], F32, kind="ExternalInput").ap(),
        "p1_w2t": nc.dram_tensor("p1_w2t", [32 * 1024, 128], F32, kind="ExternalInput").ap(),
        "p1_h1t": nc.dram_tensor("p1_h1t", [32 * 4096, 128], F32, kind="ExternalInput").ap(),
        "p1_h2t": nc.dram_tensor("p1_h2t", [32 * 4096, 128], F32, kind="ExternalInput").ap(),
        "p1_w3t": nc.dram_tensor("p1_w3t", [8 * 4096, 128], F32, kind="ExternalInput").ap(),
        "p1_s1": nc.dram_tensor("p1_s1", [128, 32], F32, kind="ExternalInput").ap(),
        "p1_s2": nc.dram_tensor("p1_s2", [128, 32], F32, kind="ExternalInput").ap(),
        "p1_h1s": nc.dram_tensor("p1_h1s", [128, 32], F32, kind="ExternalInput").ap(),
        "p1_h2s": nc.dram_tensor("p1_h2s", [128, 32], F32, kind="ExternalInput").ap(),
        "p1_s3": nc.dram_tensor("p1_s3", [128, 8], F32, kind="ExternalInput").ap(),
        "p1_wkt": nc.dram_tensor("p1_wkt", [8 * 1024, 128], F32, kind="ExternalInput").ap(),
        "p1_wv": nc.dram_tensor("p1_wv", [1024, 1024], F32, kind="ExternalInput").ap(),
    }


def emit_p1(P, nc, A, pool, dram, xa_src, out_xB, out_kf, out_vt, rt_done_cb=None):
    if not isinstance(out_xB, list):
        out_xB = [out_xB[:, 0:1024], out_xB[:, 1024:2048]]
        out_kf = [out_kf[:, 0:1024], out_kf[:, 1024:2048]]
        out_vt = [out_vt[0:1024, :], out_vt[1024:2048, :]]

    def slot(i, w=1):
        return pool[:, 512 * i:512 * (i + w)]

    gW = [P.new_dma_sem("p1gW0", "g"), P.new_dma_sem("p1gW1", "g")]
    gH = [P.new_dma_sem("p1gH0", "g"), P.new_dma_sem("p1gH1", "g")]
    gW3 = [P.new_dma_sem("p1gW30", "g"), P.new_dma_sem("p1gW31", "g")]
    gK = [P.new_dma_sem("p1gK0", "g"), P.new_dma_sem("p1gK1", "g")]
    gV = P.new_dma_sem("p1gV", "g")
    gX = P.new_dma_sem("p1gX", "g")
    sX = [P.new_dma_sem("p1sX0", "s"), P.new_dma_sem("p1sX1", "s")]
    soX = [P.new_dma_sem("p1soX0", "s"), P.new_dma_sem("p1soX1", "s")]
    soK = [P.new_dma_sem("p1soK0", "s"), P.new_dma_sem("p1soK1", "s")]
    soV = [P.new_dma_sem("p1soV0", "s"), P.new_dma_sem("p1soV1", "s")]
    tA1 = P.new_sem("p1tA1")
    tA2 = P.new_sem("p1tA2")
    tA3 = P.new_sem("p1tA3")
    tA4 = P.new_sem("p1tA4")
    tK = P.new_sem("p1tK")
    tV2 = P.new_sem("p1tV2")
    sc1 = P.new_sem("p1sc1")
    sc2 = P.new_sem("p1sc2")
    sc3 = P.new_sem("p1sc3")
    scK = P.new_sem("p1scK")
    scV2 = P.new_sem("p1scV2")
    v1 = P.new_sem("p1v1")
    v3 = P.new_sem("p1v3")
    vKc = P.new_sem("p1vKc")

    xabf = A.bf16(8 * 1024)
    h0b = A.bf16(32 * 1024)
    h1b = A.bf16(32 * 1024)
    wbuf = [[A.bf16(8 * 128), A.bf16(8 * 128)] for _ in range(2)]
    hbuf = [A.bf16(32 * 128) for _ in range(2)]
    w3buf = hbuf
    wkbuf = [A.bf16(8 * 128) for _ in range(2)]
    wvb8 = h1b[:, 16384:24576]
    tsil = [A.f32(1024) for _ in range(2)]
    xat = [A.f32(1024) for _ in range(2)]
    zbuf = h1b[:, 0:16384].bitcast(F32)
    xbbf = xabf
    kft = [A.bf16(1024) for _ in range(2)]
    vtt = [A.bf16(1024) for _ in range(2)]
    s1b = A.f32(32)
    s2b = A.f32(32)
    h1sb = A.f32(32)
    h2sb = A.f32(32)
    s3b = A.f32(8)
    zb = h0b[:, 0:512]
    zsq = h0b[:, 512:1024]
    statr = h0b[:, 1024:6144].bitcast(F32)
    xnt = [h0b[:, 6144:7168].bitcast(F32), h0b[:, 7168:8192].bitcast(F32)]
    statcol = A.bf16(2)
    onesrow = A.f32(128)
    epsr = A.f32(1)

    sB = P.new_dma_sem("p1sB", "s")
    for name, buf, w in (("p1_s1", s1b, 32), ("p1_s2", s2b, 32), ("p1_h1s", h1sb, 32),
                         ("p1_h2s", h2sb, 32), ("p1_s3", s3b, 8)):
        P.step("s", sB, lambda s, name=name, buf=buf, w=w: s.dma_start(
            out=buf[:, 0:w], in_=dram[name][:, :]), 16)
    P.step("s", sB, lambda s: s.dma_start(out=onesrow[0:1, :], in_=dram["c_ones_row"][:, :]), 16)
    P.step("s", sB, lambda s: s.dma_start(out=epsr[0:1, :], in_=dram["c_eps"][:, :]), 16)
    P.step("g", gV, lambda g: g.dma_start(out=statcol[:, 0:1], in_=dram["c_statcol"][:, :]), 16)
    for e in ("t", "v", "sc"):
        P.wait(e, sB)
        P.wait(e, gV)

    def slab_load(chan_sems, unit, bufap, src_ap, gate_sem, gate_val, nblk):
        par = unit % 2
        if gate_val > 0:
            P.wait("g", gate_sem, gate_val)
        P.step("g", chan_sems[par], lambda g, bufap=bufap, src_ap=src_ap, nblk=nblk: g.dma_start(
            out=bufap.rearrange("p (c o) -> p c o", c=nblk),
            in_=src_ap.rearrange("(c p) o -> p c o", p=128)), 16)
        return chan_sems[par], P.n(chan_sems[par])

    ln_guard_prev = None
    for rt in range(2):
        col0 = 1024 * rt
        if rt == 1:
            P.wait("g", tA1, 32 * rt)
            P.wait("g", tV2, P.n(tV2))
            P.wait("g", scK, P.n(scK))
        for ic in range(8):
            P.step("g", gX, lambda g, ic=ic, col0=col0: g.dma_start(
                out=xabf[:, ic * 1024:(ic + 1) * 1024],
                in_=xa_src[ic * 128:(ic + 1) * 128, col0:col0 + 1024]), 16)
        P.wait("t", gX)

        if rt == 1:
            P.wait("v", sc3, P.n(sc3))
            P.wait("t", scV2, P.n(scV2))
            P.wait("t", scK, P.n(scK))
            for sem in (ln_guard_prev or []) + [vKc, soX[0], soX[1]]:
                P.wait("v", sem, P.n(sem))
                P.wait("sc", sem, P.n(sem))
        for ob in range(32):
            unit = rt * 32 + ob
            par = ob % 2
            slab_load(gW, unit, wbuf[par][0],
                      dram["p1_w1t"][ob * 1024:(ob + 1) * 1024, :], tA1, unit - 1, 8)
            P.step("g", gW[par], lambda g, par=par, ob=ob: g.dma_start(
                out=wbuf[par][1].rearrange("p (c o) -> p c o", c=8),
                in_=dram["p1_w2t"][ob * 1024:(ob + 1) * 1024, :].rearrange("(c p) o -> p c o", p=128)), 16)
            P.wait("t", gW[par], P.n(gW[par]))
            if unit >= 2:
                P.wait("t", v1, unit - 1)
            pp1 = slot(4 * par, 2)
            pp2 = slot(4 * par + 2, 2)
            for w_i, pp in ((0, pp1), (1, pp2)):
                for ic in range(8):
                    for h in range(2):
                        fn = lambda t, w_i=w_i, pp=pp, ic=ic, h=h, par=par: t.matmul(
                            pp[:, h * 512:(h + 1) * 512],
                            wbuf[par][w_i][:, ic * 128:(ic + 1) * 128],
                            xabf[:, ic * 1024 + h * 512: ic * 1024 + (h + 1) * 512],
                            start=(ic == 0), stop=(ic == 7))
                        if w_i == 1 and ic == 7 and h == 1:
                            P.step("t", tA1, fn)
                        else:
                            P.emit("t", fn)
            P.wait("sc", tA1, P.n(tA1))
            P.step("sc", sc1, lambda sc, par=par, ob=ob, pp1=pp1: sc.activation(
                out=tsil[par][:, :], in_=pp1[:, :], func=AF.Silu,
                scale=s1b[:, ob:ob + 1]))
            P.wait("v", sc1, P.n(sc1))
            P.step("v", v1, lambda v, par=par, ob=ob, pp2=pp2: v.scalar_tensor_tensor(
                h0b[:, ob * 1024:(ob + 1) * 1024], pp2[:, :], s2b[:, ob:ob + 1],
                tsil[par][:, :], ALU.mult, ALU.mult))

        P.wait("t", v1, P.n(v1))
        for ob in range(32):
            unit = rt * 32 + ob
            par = ob % 2
            if ob < 2 and rt == 1:
                P.wait("g", tA3, P.n(tA3))
                _, wv_ = slab_load(gH, unit, hbuf[par],
                                   dram["p1_h1t"][ob * 4096:(ob + 1) * 4096, :], tA2, 0, 32)
            else:
                _, wv_ = slab_load(gH, unit, hbuf[par],
                                   dram["p1_h1t"][ob * 4096:(ob + 1) * 4096, :], tA2, unit - 1, 32)
            P.wait("t", gH[par], wv_)
            if unit >= 2:
                P.wait("t", sc2, unit - 1)
            ph = slot(4 * (par % 2), 2)
            for ic in range(32):
                for h in range(2):
                    fn = lambda t, ph=ph, ic=ic, h=h, par=par: t.matmul(
                        ph[:, h * 512:(h + 1) * 512],
                        hbuf[par][:, ic * 128:(ic + 1) * 128],
                        h0b[:, ic * 1024 + h * 512: ic * 1024 + (h + 1) * 512],
                        start=(ic == 0), stop=(ic == 31))
                    if ic == 31 and h == 1:
                        P.step("t", tA2, fn)
                    else:
                        P.emit("t", fn)
            P.wait("sc", tA2, P.n(tA2))
            P.step("sc", sc2, lambda sc, ob=ob, ph=ph: sc.activation(
                out=h1b[:, ob * 1024:(ob + 1) * 1024], in_=ph[:, :], func=AF.Gelu,
                scale=h1sb[:, ob:ob + 1]))

        P.wait("t", sc2, P.n(sc2))
        for ob in range(32):
            unit = rt * 32 + ob
            par = ob % 2
            if ob < 2:
                P.wait("g", tA2, P.n(tA2))
                _, wv_ = slab_load(gH, 64 + unit, hbuf[par],
                                   dram["p1_h2t"][ob * 4096:(ob + 1) * 4096, :], tA3, 0, 32)
            else:
                _, wv_ = slab_load(gH, 64 + unit, hbuf[par],
                                   dram["p1_h2t"][ob * 4096:(ob + 1) * 4096, :], tA3, unit - 1, 32)
            P.wait("t", gH[par], wv_)
            if unit >= 2:
                P.wait("t", sc3, unit - 1)
            ph = slot(4 * (par % 2), 2)
            for ic in range(32):
                for h in range(2):
                    fn = lambda t, ph=ph, ic=ic, h=h, par=par: t.matmul(
                        ph[:, h * 512:(h + 1) * 512],
                        hbuf[par][:, ic * 128:(ic + 1) * 128],
                        h1b[:, ic * 1024 + h * 512: ic * 1024 + (h + 1) * 512],
                        start=(ic == 0), stop=(ic == 31))
                    if ic == 31 and h == 1:
                        P.step("t", tA3, fn)
                    else:
                        P.emit("t", fn)
            P.wait("sc", tA3, P.n(tA3))
            P.step("sc", sc3, lambda sc, ob=ob, ph=ph: sc.activation(
                out=h0b[:, ob * 1024:(ob + 1) * 1024], in_=ph[:, :], func=AF.Gelu,
                scale=h2sb[:, ob:ob + 1]))

        P.wait("t", sc3, P.n(sc3))
        for ob in range(8):
            unit = rt * 8 + ob
            par = ob % 2
            if ob < 2:
                P.wait("g", tA3, P.n(tA3))
                _, wv3_ = slab_load(gW3, unit, w3buf[par],
                                    dram["p1_w3t"][ob * 4096:(ob + 1) * 4096, :], tA4, 0, 32)
            else:
                _, wv3_ = slab_load(gW3, unit, w3buf[par],
                                    dram["p1_w3t"][ob * 4096:(ob + 1) * 4096, :], tA4, unit - 1, 32)
            if unit >= 2:
                P.wait("s", v3, unit - 1)
            P.step("s", sX[par], lambda s, ob=ob, par=par, col0=col0: s.dma_start(
                out=xat[par][:, :], in_=xa_src[ob * 128:(ob + 1) * 128, col0:col0 + 1024]), 16)
            P.wait("t", gW3[par], wv3_)
            if unit >= 2:
                P.wait("t", v3, unit - 1)
            py = slot(4 * (par % 2), 2)
            for ic in range(32):
                for h in range(2):
                    fn = lambda t, py=py, ic=ic, h=h, par=par: t.matmul(
                        py[:, h * 512:(h + 1) * 512],
                        w3buf[par][:, ic * 128:(ic + 1) * 128],
                        h0b[:, ic * 1024 + h * 512: ic * 1024 + (h + 1) * 512],
                        start=(ic == 0), stop=(ic == 31))
                    if ic == 31 and h == 1:
                        P.step("t", tA4, fn)
                    else:
                        P.emit("t", fn)
            P.wait("v", tA4, P.n(tA4))
            P.wait("v", sX[par], P.n(sX[par]))
            P.step("v", v3, lambda v, ob=ob, par=par, py=py: v.scalar_tensor_tensor(
                zbuf[:, ob * 1024:(ob + 1) * 1024], py[:, :], s3b[:, ob:ob + 1],
                xat[par][:, :], ALU.mult, ALU.add))

        P.wait("t", v3, P.n(v3))
        P.wait("sc", v3, P.n(v3))
        P.wait("v", v3, P.n(v3))

        st_i = [0]

        def sink(sub, oc, par, rt=rt, st_i=st_i):
            P.step("v", vKc, lambda v, sub=sub, oc=oc, par=par: v.tensor_copy(
                xbbf[:, oc * 1024 + sub * 512: oc * 1024 + sub * 512 + 512],
                xnt[par][:, :]))
            P.wait("s", vKc, P.n(vKc))
            spar = st_i[0] % 2
            P.step("s", soX[spar], lambda s, sub=sub, oc=oc, par=par, rt=rt: s.dma_start(
                out=out_xB[rt][oc * 128:(oc + 1) * 128, sub * 512: sub * 512 + 512],
                in_=xnt[par][:, :]), 16)
            st_i[0] += 1
            return [(vKc, P.n(vKc)), (soX[spar], P.n(soX[spar]))]

        _lnsems = emit_ln_fm(P, f"p1ln{rt}", pool, {
            "zb": zb, "zsq": zsq, "statr": statr, "xnt": xnt,
            "statcol": statcol, "onesrow": onesrow, "epsr": epsr,
        }, 2, lambda oc, sub: zbuf[:, oc * 1024 + sub * 512: oc * 1024 + sub * 512 + 512],
            sink)
        ln_guard_prev = [v for v in _lnsems.values() if isinstance(v, str)]

        P.wait("g", tA3, P.n(tA3))
        for ic in range(8):
            P.step("g", gV, lambda g, ic=ic: g.dma_start(
                out=wvb8[:, ic * 1024:(ic + 1) * 1024],
                in_=dram["p1_wv"][ic * 128:(ic + 1) * 128, :]), 16)
        P.wait("t", gV)
        P.wait("t", vKc, P.n(vKc))
        for ob in range(8):
            unit = rt * 8 + ob
            par = ob % 2
            _, wk_ = slab_load(gK, unit, wkbuf[par],
                               dram["p1_wkt"][ob * 1024:(ob + 1) * 1024, :], tK, unit - 1, 8)
            P.wait("t", gK[par], wk_)
            if unit >= 2:
                P.wait("t", scK, unit - 1)
            pk = slot(4 * (par % 2), 2)
            for ic in range(8):
                for h in range(2):
                    fn = lambda t, pk=pk, ic=ic, h=h, par=par: t.matmul(
                        pk[:, h * 512:(h + 1) * 512],
                        wkbuf[par][:, ic * 128:(ic + 1) * 128],
                        xbbf[:, ic * 1024 + h * 512: ic * 1024 + (h + 1) * 512],
                        start=(ic == 0), stop=(ic == 7))
                    if ic == 7 and h == 1:
                        P.step("t", tK, fn)
                    else:
                        P.emit("t", fn)
            P.wait("sc", tK, P.n(tK))
            if unit >= 2:
                P.wait("sc", soK[par], P.n(soK[par]))
            P.step("sc", scK, lambda sc, par=par, pk=pk: sc.copy(
                out=kft[par][:, :], in_=pk[:, :]))
            P.wait("s", scK, P.n(scK))
            P.step("s", soK[par], lambda s, ob=ob, par=par, rt=rt: s.dma_start(
                out=out_kf[rt][ob * 128:(ob + 1) * 128, :],
                in_=kft[par][:, :]), 16)
        P.wait("t", scK, P.n(scK))
        for tc in range(8):
            unit = rt * 8 + tc
            par = tc % 2
            if unit >= 2:
                P.wait("t", scV2, unit - 1)
            pvv = slot(4 * par, 2)
            for ic in range(8):
                for h in range(2):
                    fn = lambda t, pvv=pvv, ic=ic, h=h, tc=tc: t.matmul(
                        pvv[:, h * 512:(h + 1) * 512],
                        xbbf[:, ic * 1024 + tc * 128: ic * 1024 + tc * 128 + 128],
                        wvb8[:, ic * 1024 + h * 512: ic * 1024 + (h + 1) * 512],
                        start=(ic == 0), stop=(ic == 7))
                    if ic == 7 and h == 1:
                        P.step("t", tV2, fn)
                    else:
                        P.emit("t", fn)
            P.wait("sc", tV2, P.n(tV2))
            if unit >= 2:
                P.wait("sc", soV[par], P.n(soV[par]))
            P.step("sc", scV2, lambda sc, par=par, pvv=pvv: sc.copy(
                out=vtt[par][:, :], in_=pvv[:, :]))
            P.wait("s", scV2, P.n(scV2))
            P.step("s", soV[par], lambda s, tc=tc, par=par, rt=rt: s.dma_start(
                out=out_vt[rt][tc * 128: tc * 128 + 128, :],
                in_=vtt[par][:, :]), 16)
        if rt_done_cb is not None:
            rt_done_cb(rt, [soX[0], soX[1], soK[0], soK[1], soV[0], soV[1]])

    for sem in (soX[0], soX[1], soK[0], soK[1], soV[0], soV[1]):
        P.wait("s", sem, P.n(sem))
    P.emit("s", lambda s: s.nop())
    return [soX[0], soX[1], soK[0], soK[1], soV[0], soV[1]]



def build_p1():
    nc = bass.Bass("TRN2", target_bir_lowering=False)
    dram = p1_dram_decls(nc)
    dram["p1_xA"] = nc.dram_tensor("p1_xA", [D, ROWS], F32, kind="ExternalInput").ap()
    dram["c_ones_row"] = nc.dram_tensor("c_ones_row", [1, 128], F32, kind="ExternalInput").ap()
    dram["c_statcol"] = nc.dram_tensor("c_statcol", [128, 1], BF16, kind="ExternalInput").ap()
    dram["c_eps"] = nc.dram_tensor("c_eps", [1, 1], F32, kind="ExternalInput").ap()
    out_xB = nc.dram_tensor("p1_xB", [D, ROWS], F32, kind="ExternalOutput").ap()
    out_kf = nc.dram_tensor("p1_kf", [D, ROWS], BF16, kind="ExternalOutput").ap()
    out_vt = nc.dram_tensor("p1_vt", [ROWS, D], BF16, kind="ExternalOutput").ap()
    stack = ExitStack()
    with stack:
        A = Arena(nc, stack, "arena", 104448)
        pool_t = stack.enter_context(nc.psum_tensor("pool", [128, 4096], F32))
        P = Plan(nc, stack)
        emit_p1(P, nc, A, pool_t, dram, dram["p1_xA"], out_xB, out_kf, out_vt)
        block = stack.enter_context(nc.Block())
        P.realize(block)
    return nc


def run_p1(inputs, xA_shards):
    maps = stage_p1(inputs)
    for c in range(NCORE):
        maps[c]["p1_xA"] = np.ascontiguousarray(xA_shards[c])
    nc = build_p1()
    res = run_bass_kernel_spmd(nc, maps, core_ids=list(range(NCORE)))
    return ([res.results[c]["p1_xB"] for c in range(NCORE)],
            [res.results[c]["p1_kf"] for c in range(NCORE)],
            [res.results[c]["p1_vt"] for c in range(NCORE)])



def p3_perm_rows(cp):
    rows = []
    for m in range(8):
        for j in (8 * m + cp, 8 * m + 7 - cp):
            rows.extend(range(128 * j, 128 * j + 128))
    return np.array(rows)


def _p3_masks(cp):
    out = np.zeros((128, 64 * 128), dtype=np.float32)
    kl = np.arange(128)[:, None]
    ql = np.arange(128)[None, :]
    for m in range(8):
        for k in range(8):
            c = 8 * m + k
            j = 8 * m + cp if k < 4 else 8 * m + 7 - cp
            if c < j:
                v = np.ones((128, 128), dtype=np.float32)
            elif c == j:
                v = (ql >= kl).astype(np.float32)
            else:
                v = np.zeros((128, 128), dtype=np.float32)
            out[:, (m * 8 + k) * 128:(m * 8 + k + 1) * 128] = v
    return out.astype(BF)


def stage_p3(inputs):
    wqt = _pretile2(np.ascontiguousarray(inputs["L1_attn_Wq"]))
    wvt2 = _pretile2(np.ascontiguousarray(inputs["L1_attn_Wv"]))
    wfct = _pretile2(np.ascontiguousarray(inputs["L1_attn_Wfc"]))
    bfc = np.ascontiguousarray(inputs["L1_attn_bfc"].reshape(8, 128).T)
    gam = np.ascontiguousarray(inputs["L1_attn_gamma"].reshape(8, 128).T)
    bet = np.ascontiguousarray(inputs["L1_attn_beta"].reshape(8, 128).T)
    shared = {
        "p3_wqt": wqt, "p3_wvt": wvt2, "p3_wfct": wfct,
        "p3_bfc": bfc, "p3_gam": gam, "p3_bet": bet,
        "c_ones_row": np.ones((1, 128), dtype=np.float32),
        "c_statcol": np.full((128, 1), 1.0 / D, dtype=np.float32).astype(BF),
        "c_onecol": np.ones((128, 1), dtype=np.float32).astype(BF),
        "c_eps": np.full((1, 1), 1e-5, dtype=np.float32),
    }
    maps = []
    for c in range(NCORE):
        cp = c % 4
        m = dict(shared)
        m["p3_mask"] = _p3_masks(cp)
        maps.append(m)
    return maps


def p3_dram_decls(nc):
    d = {
        "p3_wqt": nc.dram_tensor("p3_wqt", [8 * 1024, 128], F32, kind="ExternalInput").ap(),
        "p3_wvt": nc.dram_tensor("p3_wvt", [8 * 1024, 128], F32, kind="ExternalInput").ap(),
        "p3_wfct": nc.dram_tensor("p3_wfct", [8 * 1024, 128], F32, kind="ExternalInput").ap(),
        "p3_bfc": nc.dram_tensor("p3_bfc", [128, 8], F32, kind="ExternalInput").ap(),
        "p3_gam": nc.dram_tensor("p3_gam", [128, 8], F32, kind="ExternalInput").ap(),
        "p3_bet": nc.dram_tensor("p3_bet", [128, 8], F32, kind="ExternalInput").ap(),
        "p3_mask": nc.dram_tensor("p3_mask", [128, 64 * 128], BF16, kind="ExternalInput").ap(),
    }
    return d


def emit_p3(P, nc, A, pool, dram, xbi_src, kf_src, vt_src, out_xC,
            fused=False, xb_ag=None, idx_tab=None, ag_waits=None):
    if fused:
        xb_view = [h.rearrange("a (n o) -> (a n) o", o=128) for h in xb_ag]

    def slot(i, w=1):
        return pool[:, 512 * i:512 * (i + w)]

    gB = P.new_dma_sem("p3gB", "g")
    sB = P.new_dma_sem("p3sB", "s")
    gX = P.new_dma_sem("p3gX", "g")
    gKV = [P.new_dma_sem(f"p3gKV{i}", "s") for i in range(8)]
    sxb = [P.new_dma_sem("p3sxb0", "g" if fused else "s"),
           P.new_dma_sem("p3sxb1", "g" if fused else "s")]
    soC = [P.new_dma_sem("p3soC0", "s"), P.new_dma_sem("p3soC1", "s")]
    tQ = P.new_sem("p3tQ")
    tS = P.new_sem("p3tS")
    tS2 = P.new_sem("p3tS2")
    tAv = P.new_sem("p3tAv")
    tLb = P.new_sem("p3tLb")
    tFc = P.new_sem("p3tFc")
    scQ = P.new_sem("p3scQ")
    scE2 = P.new_sem("p3scE2")
    scLb = P.new_sem("p3scLb")
    scFc = P.new_sem("p3scFc")
    vM = P.new_sem("p3vM")
    vC2 = P.new_sem("p3vC2")
    vZ = P.new_sem("p3vZ")
    vZf = P.new_sem("p3vZf")
    vZ2 = P.new_sem("p3vZ2")

    xbibf = A.bf16(8 * 2048)
    qf = A.bf16(8 * 2048)
    vf = A.bf16(8 * 2048)
    qwb = A.bf16(64 * 128)
    vwb = A.bf16(64 * 128)
    fwb = A.bf16(64 * 128)
    maskb = A.bf16(64 * 128)
    kvb = [vwb[:, i * 1024:(i + 1) * 1024] for i in range(8)]
    vvb = [qwb[:, i * 1024:(i + 1) * 1024] for i in range(8)]
    probs = [A.bf16(256) for _ in range(2)]
    linv = A.f32(256)
    linbs = A.f32(256)
    zslot = A.f32(8 * 256)
    z2slot = zslot
    lnb = A.bf16(8 * 256)
    xbt = [A.f32(256) for _ in range(2)]
    bfcb = A.f32(8)
    gamb = A.f32(8)
    betb = A.f32(8)
    zb = A.bf16(512)
    zsq = A.bf16(512)
    statr = A.f32(2304)
    xnt = [A.f32(512) for _ in range(2)]
    statcol = A.bf16(2)
    onecol = A.bf16(2)
    onesrow = A.f32(128)
    epsr = A.f32(1)

    gB2 = P.new_dma_sem("p3gB2", "g")
    for name, buf in (("p3_wqt", qwb), ("p3_wvt", vwb)):
        for ob in range(8):
            P.step("g", gB, lambda g, name=name, buf=buf, ob=ob: g.dma_start(
                out=buf[:, ob * 1024:(ob + 1) * 1024].rearrange("p (c o) -> p c o", c=8),
                in_=dram[name][ob * 1024:(ob + 1) * 1024, :].rearrange("(c p) o -> p c o", p=128)), 16)
    if fused:
        idxg = A.bf16(256)
        idxgu = idxg.bitcast(U32)
        P.step("g", gX, lambda g: g.dma_start(out=idxgu[:, :], in_=idx_tab[:, :]), 16)
        P.wait("g", gX, P.n(gX))
        if ag_waits:
            P.wait("g", ag_waits["xb"][0][0], ag_waits["xb"][0][1])
        for ic in range(8):
            for s_ in (0, 1, 4, 5, 8, 9, 12, 13):
                P.step("g", gX, lambda g, ic=ic, s_=s_: g.indirect_dma_start(
                    out=xbibf[:, ic * 2048 + s_ * 128: ic * 2048 + s_ * 128 + 128],
                    out_offset=None, in_=xb_view[0][:, :],
                    in_offset=IndirectOffsetOnAxis(ap=idxgu[:, (s_ * 8 + ic):(s_ * 8 + ic) + 1], axis=0)), 16)
        if ag_waits:
            P.wait("g", ag_waits["xb"][1][0], ag_waits["xb"][1][1])
        for ic in range(8):
            for s_ in (2, 3, 6, 7, 10, 11, 14, 15):
                P.step("g", gX, lambda g, ic=ic, s_=s_: g.indirect_dma_start(
                    out=xbibf[:, ic * 2048 + s_ * 128: ic * 2048 + s_ * 128 + 128],
                    out_offset=None, in_=xb_view[1][:, :],
                    in_offset=IndirectOffsetOnAxis(ap=idxgu[:, (s_ * 8 + ic):(s_ * 8 + ic) + 1], axis=0)), 16)
    else:
        for ic in range(8):
            P.step("g", gX, lambda g, ic=ic: g.dma_start(
                out=xbibf[:, ic * 2048:(ic + 1) * 2048],
                in_=xbi_src[ic * 128:(ic + 1) * 128, :]), 16)
    for ob in range(8):
        P.step("g", gB2, lambda g, ob=ob: g.dma_start(
            out=fwb[:, ob * 1024:(ob + 1) * 1024].rearrange("p (c o) -> p c o", c=8),
            in_=dram["p3_wfct"][ob * 1024:(ob + 1) * 1024, :].rearrange("(c p) o -> p c o", p=128)), 16)
    P.step("g", gB2, lambda g: g.dma_start(out=maskb[:, :], in_=dram["p3_mask"][:, :]), 16)
    P.step("g", gB2, lambda g: g.dma_start(out=statcol[:, 0:1], in_=dram["c_statcol"][:, :]), 16)
    P.step("g", gB2, lambda g: g.dma_start(out=onecol[:, 0:1], in_=dram["c_onecol"][:, :]), 16)
    for name, buf in (("p3_bfc", bfcb), ("p3_gam", gamb), ("p3_bet", betb)):
        P.step("s", sB, lambda s, name=name, buf=buf: s.dma_start(
            out=buf[:, 0:8], in_=dram[name][:, :]), 16)
    P.step("s", sB, lambda s: s.dma_start(out=onesrow[0:1, :], in_=dram["c_ones_row"][:, :]), 16)
    P.step("s", sB, lambda s: s.dma_start(out=epsr[0:1, :], in_=dram["c_eps"][:, :]), 16)
    for e in ("t", "v", "sc"):
        P.wait(e, gB)
        P.wait(e, gX)
        P.wait(e, sB)

    for proj, wb, dst, dstf in (("q", qwb, qf, None), ("v", vwb, None, vf)):
        for ob in range(8):
            for tt in range(4):
                unit = P.n(tQ)
                par = unit % 2
                if unit >= 2:
                    P.wait("t", scQ, unit - 1)
                pp = slot(2 * par)
                for ic in range(8):
                    fn = lambda t, pp=pp, wb=wb, ic=ic, ob=ob, tt=tt: t.matmul(
                        pp[:, :], wb[:, ob * 1024 + ic * 128: ob * 1024 + (ic + 1) * 128],
                        xbibf[:, ic * 2048 + tt * 512: ic * 2048 + tt * 512 + 512],
                        start=(ic == 0), stop=(ic == 7))
                    if ic == 7:
                        P.step("t", tQ, fn)
                    else:
                        P.emit("t", fn)
                P.wait("sc", tQ, P.n(tQ))
                if dst is not None:
                    P.step("sc", scQ, lambda sc, pp=pp, ob=ob, tt=tt: sc.copy(
                        out=qf[:, ob * 2048 + tt * 512: ob * 2048 + tt * 512 + 512],
                        in_=pp[:, :]))
                else:
                    P.step("sc", scQ, lambda sc, pp=pp, ob=ob, tt=tt: sc.copy(
                        out=vf[:, ob * 2048 + tt * 512: ob * 2048 + tt * 512 + 512],
                        in_=pp[:, :]))

    for e in ("t", "v", "sc"):
        P.wait(e, gB2)
    lnsems = {}
    lnbufs = {"zb": zb, "zsq": zsq, "statr": statr, "xnt": xnt,
              "statcol": statcol, "onesrow": onesrow, "epsr": epsr}
    kv_unit = [0]
    ch_g = [0]
    for m in range(8):
        if m > 0:
            for s_ in ("vH", "scE"):
                if s_ in lnsems:
                    P.wait("t", lnsems[s_], P.n(lnsems[s_]))
            P.wait("t", scLb, P.n(scLb))
            P.wait("t", vZ2, P.n(vZ2))
        nch = 8 * m + 8
        ncommon = 8 * m + 4
        pend = []

        def emit_av(ent):
            (kpar_, ppar_, c_, common_, vm_target, sce_target) = ent
            if vm_target is not None:
                P.wait("t", vM, vm_target)
            else:
                P.wait("t", scE2, sce_target)
            for j in range(4):
                for half in range(2):
                    dv = 2 * j + half
                    P.emit("t", lambda t, j=j, half=half, dv=dv, kpar=kpar_, ppar=ppar_, c=c_, nch=nch: t.matmul(
                        slot(j)[:, half * 256: half * 256 + 256],
                        vvb[kpar][:, dv * 128:(dv + 1) * 128],
                        probs[ppar][:, 0:256],
                        start=(c == 0 and half == 0), stop=(c == nch - 1 and half == 1)))
            P.step("t", tS2, lambda t, ppar=ppar_, c=c_, nch=nch: t.matmul(
                pool[0:1, 3072:3328], onecol[:, 0:1], probs[ppar][:, 0:256],
                start=(c == 0), stop=(c == nch - 1)))

        for c in range(nch):
            common = c < ncommon
            n = 256
            qoff = 256 * m
            u = kv_unit[0]
            par = u % 8
            if u < 8:
                P.wait("s", scQ, P.n(scQ))
                if fused and ag_waits and u == 0:
                    for s_, v_ in ag_waits["kv"]:
                        P.wait("s", s_, v_)
            if u >= 8:
                P.wait("s", tS2, u - 7)
            if fused:
                r_, hh_ = c // 16, (c % 16) // 8
                lc_ = (c % 16) - 8 * hh_
                P.step("s", gKV[par], lambda s, par=par, r_=r_, hh_=hh_, lc_=lc_: s.dma_start(
                    out=kvb[par].rearrange("p (c1 o) -> p c1 o", c1=8),
                    in_=kf_src[hh_][r_ * 1024:(r_ + 1) * 1024, :]
                    .rearrange("(c1 p) kv -> p c1 kv", p=128)[:, :, lc_ * 128:(lc_ + 1) * 128]), 16)
            else:
                P.step("s", gKV[par], lambda s, par=par, c=c: s.dma_start(
                    out=kvb[par].rearrange("p (c1 o) -> p c1 o", c1=8),
                    in_=kf_src.rearrange("(c1 p) kv -> p c1 kv", p=128)[:, :, c * 128:(c + 1) * 128]), 16)
            if fused:
                r_, hh_ = c // 16, (c % 16) // 8
                rih_ = 128 * (c % 16) - 1024 * hh_
                P.step("s", gKV[par], lambda s, par=par, r_=r_, hh_=hh_, rih_=rih_: s.dma_start(
                    out=vvb[par][:, :],
                    in_=vt_src[hh_][r_ * 1024 + rih_: r_ * 1024 + rih_ + 128, :]), 16)
            else:
                P.step("s", gKV[par], lambda s, par=par, c=c: s.dma_start(
                    out=vvb[par][:, :], in_=vt_src[c * 128:(c + 1) * 128, :]), 16)
            kvw = P.n(gKV[par])
            kv_unit[0] += 1
            P.wait("t", gKV[par], kvw)
            if ch_g[0] >= 2:
                P.wait("t", scE2, ch_g[0] - 1)
            psc = pool[:, 2048 + 512 * (c % 2): 2048 + 512 * (c % 2) + 256]
            for ic in range(8):
                fn = lambda t, psc=psc, par=par, ic=ic, qoff=qoff, n=n: t.matmul(
                    psc[:, 0:n], kvb[par][:, ic * 128:(ic + 1) * 128],
                    qf[:, ic * 2048 + qoff: ic * 2048 + qoff + n],
                    start=(ic == 0), stop=(ic == 7))
                if ic == 7:
                    P.step("t", tS, fn)
                else:
                    P.emit("t", fn)
            ppar = ch_g[0] % 2
            P.wait("sc", tS, P.n(tS))
            P.step("sc", scE2, lambda sc, psc=psc, ppar=ppar, n=n: sc.activation(
                out=probs[ppar][:, 0:n], in_=psc[:, 0:n], func=AF.Exp, scale=1.0 / 32.0))
            need_v = False
            if common and 8 * m <= c < 8 * m + 4:
                mi = m * 8 + (c - 8 * m)
                P.wait("v", scE2, P.n(scE2))
                P.step("v", vM, lambda v, ppar=ppar, mi=mi: v.tensor_mul(
                    probs[ppar][:, 0:128], probs[ppar][:, 0:128],
                    maskb[:, mi * 128:(mi + 1) * 128]))
                need_v = True
            elif not common:
                mi = m * 8 + 4 + (c - ncommon)
                P.wait("v", scE2, P.n(scE2))
                P.step("v", vM, lambda v, ppar=ppar: v.memset(probs[ppar][:, 0:128], 0.0))
                P.step("v", vM, lambda v, ppar=ppar, mi=mi: v.tensor_mul(
                    probs[ppar][:, 128:256], probs[ppar][:, 128:256],
                    maskb[:, mi * 128:(mi + 1) * 128]))
                need_v = True
            ent = (par, ppar, c, common, P.n(vM) if need_v else None, P.n(scE2))
            pend.append(ent)
            if len(pend) >= 2:
                emit_av(pend.pop(0))
            ch_g[0] += 1
        while pend:
            emit_av(pend.pop(0))
        P.wait("sc", tS2, P.n(tS2))
        P.step("sc", vC2, lambda sc: sc.activation(
            out=linv[0:1, :], in_=pool[0:1, 3072:3328], func=AF.Ln))
        P.wait("sc", vC2, P.n(vC2))
        P.step("sc", vC2, lambda sc: sc.activation(
            out=linv[0:1, :], in_=linv[0:1, :], func=AF.Exp, scale=-1.0))
        P.wait("t", vC2, P.n(vC2))
        P.step("t", tLb, lambda t: t.matmul(
            pool[:, 3328:3584], onesrow[0:1, :], linv[0:1, :], start=True, stop=True))
        P.wait("sc", tLb, P.n(tLb))
        P.step("sc", scLb, lambda sc: sc.copy(out=linbs[:, :], in_=pool[:, 3328:3584]))
        P.wait("v", scLb, P.n(scLb))
        for oc in range(8):
            P.step("v", vZ, lambda v, oc=oc, m=m: v.tensor_mul(
                zslot[:, oc * 256:(oc + 1) * 256],
                slot(oc // 2)[:, (oc % 2) * 256:(oc % 2) * 256 + 256], linbs[:, :]))
            P.wait("v", vZ, P.n(vZ))
            P.step("v", vZf, lambda v, oc=oc, m=m: v.tensor_add(
                zslot[:, oc * 256:(oc + 1) * 256],
                zslot[:, oc * 256:(oc + 1) * 256],
                vf[:, oc * 2048 + m * 256: oc * 2048 + m * 256 + 256]))
        def sink1b(sub, oc, par, m=m):
            P.step("v", vM, lambda v, oc=oc, par=par: v.tensor_copy(
                lnb[:, oc * 256:(oc + 1) * 256], xnt[par][:, 0:256]))
            return [(vM, P.n(vM))]

        P.wait("v", vZf, P.n(vZf))
        P.wait("t", vZf, P.n(vZf))
        P.wait("sc", vZf, P.n(vZf))
        emit_ln_fm(P, "p3ln", pool, lnbufs, 1,
                   lambda oc, sub, m=m: zslot[:, oc * 256:(oc + 1) * 256],
                   sink1b, w=256, gb=None, sems=lnsems)
        P.wait("v", vM, P.n(vM))
        for oc in range(8):
            P.step("v", vM, lambda v, oc=oc: v.tensor_scalar(
                lnb[:, oc * 256:(oc + 1) * 256], lnb[:, oc * 256:(oc + 1) * 256],
                gamb[:, oc:oc + 1], betb[:, oc:oc + 1], ALU.mult, ALU.add))
        P.wait("t", vM, P.n(vM))
        for ob in range(8):
            unit = P.n(tFc)
            par = unit % 2
            if fused:
                if unit >= 2:
                    P.wait("g", vZ2, P.n(vZ2) - 1)
                for hh in range(2):
                    P.step("g", sxb[par], lambda g, ob=ob, par=par, m=m, hh=hh: g.indirect_dma_start(
                        out=xbt[par][:, hh * 128:(hh + 1) * 128], out_offset=None,
                        in_=xb_view[m % 2][:, :],
                        in_offset=IndirectOffsetOnAxis(
                            ap=idxgu[:, ((2 * m + hh) * 8 + ob):((2 * m + hh) * 8 + ob) + 1], axis=0)), 16)
            else:
                if unit >= 2:
                    P.wait("s", vZ2, P.n(vZ2) - 1)
                P.step("s", sxb[par], lambda s, ob=ob, par=par, m=m: s.dma_start(
                    out=xbt[par][:, :],
                    in_=xbi_src[ob * 128:(ob + 1) * 128, m * 256:(m + 1) * 256]), 16)
            pfc = pool[:, 3584:3840] if (unit % 2 == 0) else pool[:, 3328:3584]
            if P.n(vZ2) >= 2:
                P.wait("t", vZ2, P.n(vZ2) - 1)
            for ic in range(8):
                fn = lambda t, pfc=pfc, ic=ic, ob=ob: t.matmul(
                    pfc[:, :], fwb[:, ob * 1024 + ic * 128: ob * 1024 + (ic + 1) * 128],
                    lnb[:, ic * 256:(ic + 1) * 256], start=(ic == 0), stop=(ic == 7))
                if ic == 7:
                    P.step("t", tFc, fn)
                else:
                    P.emit("t", fn)
            P.wait("v", tFc, P.n(tFc))
            P.wait("v", sxb[par], P.n(sxb[par]))
            P.step("v", vZ2, lambda v, ob=ob, par=par, pfc=pfc: v.scalar_tensor_tensor(
                z2slot[:, ob * 256:(ob + 1) * 256], pfc[:, :], bfcb[:, ob:ob + 1],
                xbt[par][:, :], ALU.add, ALU.add))
        P.wait("v", vZ2, P.n(vZ2))
        P.wait("t", vZ2, P.n(vZ2))
        P.wait("sc", vZ2, P.n(vZ2))

        def sink2(sub, oc, par, m=m):
            spar = (m * 8 + oc) % 2
            P.wait("s", lnsems["vH"], P.n(lnsems["vH"]))
            P.step("s", soC[spar], lambda s, oc=oc, par=par, m=m: s.dma_start(
                out=out_xC[oc * 128:(oc + 1) * 128, m * 256:(m + 1) * 256],
                in_=xnt[par][:, 0:256]), 16)
            return [(soC[spar], P.n(soC[spar]))]

        emit_ln_fm(P, "p3ln", pool, lnbufs, 1,
                   lambda oc, sub, m=m: z2slot[:, oc * 256:(oc + 1) * 256],
                   sink2, w=256, gb=None, sems=lnsems)
    P.wait("s", soC[0], P.n(soC[0]))
    P.wait("s", soC[1], P.n(soC[1]))
    P.emit("s", lambda s: s.nop())
    return [soC[0], soC[1]]


def build_p3():
    nc = bass.Bass("TRN2", target_bir_lowering=False)
    dram = p3_dram_decls(nc)
    dram["c_ones_row"] = nc.dram_tensor("c_ones_row", [1, 128], F32, kind="ExternalInput").ap()
    dram["c_statcol"] = nc.dram_tensor("c_statcol", [128, 1], BF16, kind="ExternalInput").ap()
    dram["c_onecol"] = nc.dram_tensor("c_onecol", [128, 1], BF16, kind="ExternalInput").ap()
    dram["c_eps"] = nc.dram_tensor("c_eps", [1, 1], F32, kind="ExternalInput").ap()
    xbi = nc.dram_tensor("p3_xbi", [D, ROWS], F32, kind="ExternalInput").ap()
    kf = nc.dram_tensor("p3_kf", [D, T], BF16, kind="ExternalInput").ap()
    vt = nc.dram_tensor("p3_vt", [T, D], BF16, kind="ExternalInput").ap()
    out_xC = nc.dram_tensor("p3_xC", [D, ROWS], F32, kind="ExternalOutput").ap()
    stack = ExitStack()
    with stack:
        A = Arena(nc, stack, "arena", 104448)
        pool_t = stack.enter_context(nc.psum_tensor("pool", [128, 4096], F32))
        P = Plan(nc, stack)
        emit_p3(P, nc, A, pool_t, dram, xbi, kf, vt, out_xC)
        block = stack.enter_context(nc.Block())
        P.realize(block)
    return nc


def run_p3(inputs, xbi_shards, kf_full, vt_full):
    maps = stage_p3(inputs)
    for c in range(NCORE):
        b = c // 4
        maps[c]["p3_xbi"] = np.ascontiguousarray(xbi_shards[c])
        maps[c]["p3_kf"] = np.ascontiguousarray(kf_full[b])
        maps[c]["p3_vt"] = np.ascontiguousarray(vt_full[b])
    nc = build_p3()
    res = run_bass_kernel_spmd(nc, maps, core_ids=list(range(NCORE)))
    return [res.results[c]["p3_xC"] for c in range(NCORE)]



def stage_p4(inputs):
    wd1 = inputs["L1_ffn_Wd1"]
    wd2 = inputs["L1_ffn_Wd2"]
    wd1t = np.concatenate([_pretile2(np.ascontiguousarray(wd1[e])) for e in range(8)], axis=0)
    wd2t = np.concatenate([_pretile2(np.ascontiguousarray(wd2[e])) for e in range(8)], axis=0)
    bd1 = np.ascontiguousarray(inputs["L1_ffn_bd1"].reshape(8, 2, 128).transpose(2, 0, 1).reshape(128, 16))
    bd2 = np.ascontiguousarray(inputs["L1_ffn_bd2"].reshape(8, 8, 128).transpose(2, 0, 1).reshape(128, 64))
    wbt = _pretile2(np.ascontiguousarray(inputs["L1_ffn_W_base"]))
    bb = np.ascontiguousarray(inputs["L1_ffn_b_base"].reshape(8, 128).T)
    wad = np.ascontiguousarray(inputs["L1_ffn_Wa_down"].reshape(8 * 1024, 64))
    bad = np.ascontiguousarray(inputs["L1_ffn_ba_down"].reshape(8 * 64, 1))
    wau = np.ascontiguousarray(inputs["L1_ffn_Wa_up"].reshape(8 * 64, 1024))
    bau = np.ascontiguousarray(
        inputs["L1_ffn_ba_up"].reshape(8, 8, 128).transpose(0, 2, 1).reshape(8 * 128, 8))
    iota128 = np.arange(128, dtype=np.float32).reshape(128, 1)
    erow = np.arange(8, dtype=np.float32).reshape(1, 8)
    shared = {
        "p4_wd1t": wd1t, "p4_wd2t": wd2t, "p4_bd1": bd1, "p4_bd2": bd2,
        "p4_wbt": wbt, "p4_bb": bb, "p4_wad": wad, "p4_bad": bad,
        "p4_wau": wau, "p4_bau": bau, "p4_iota": iota128, "p4_erow": erow,
        "c_ones_row": np.ones((1, 128), dtype=np.float32),
        "c_statcol": np.full((128, 1), 1.0 / D, dtype=np.float32).astype(BF),
        "c_onecol": np.ones((128, 1), dtype=np.float32).astype(BF),
        "c_eps": np.full((1, 1), 1e-5, dtype=np.float32),
    }
    return [dict(shared) for _ in range(NCORE)]


def emit_p4a(P, nc, A, pool, dram, xci_src, out_loss):
    def slot(i, w=1):
        return pool[:, 512 * i:512 * (i + w)]
    gB = P.new_dma_sem("p4gB", "g")
    sB = P.new_dma_sem("p4sB", "s")
    gW1 = [P.new_dma_sem("p4gW10", "g"), P.new_dma_sem("p4gW11", "g")]
    gW2 = [P.new_dma_sem("p4gW20", "g"), P.new_dma_sem("p4gW21", "g")]
    tH1 = P.new_sem("p4tH1")
    tR = P.new_sem("p4tR")
    tL = P.new_sem("p4tL")
    scH = P.new_sem("p4scH")
    scS = P.new_sem("p4scS")
    vD2 = P.new_sem("p4vD2")
    vL = P.new_sem("p4vL")
    so4 = P.new_dma_sem("p4so", "s")

    xcibf = A.bf16(8 * 2048)
    xcif = A.f32(8 * 2048)
    w1buf = [A.bf16(16 * 128) for _ in range(2)]
    w2buf = [A.bf16(16 * 128) for _ in range(2)]
    heb = A.bf16(2 * 2048)
    bd1b = A.f32(16)
    bd2b = A.f32(64)
    diff = [A.f32(512) for _ in range(4)]
    dsq = [A.bf16(512) for _ in range(4)]
    lrow = A.f32(8)
    onecol = A.bf16(2)

    for ic in range(8):
        P.step("g", gB, lambda g, ic=ic: g.dma_start(
            out=xcibf[:, ic * 2048:(ic + 1) * 2048],
            in_=xci_src[ic * 128:(ic + 1) * 128, :]), 16)
    P.step("g", gB, lambda g: g.dma_start(out=onecol[:, 0:1], in_=dram["c_onecol"][:, :]), 16)
    for ic in range(8):
        P.step("s", sB, lambda s, ic=ic: s.dma_start(
            out=xcif[:, ic * 2048:(ic + 1) * 2048],
            in_=xci_src[ic * 128:(ic + 1) * 128, :]), 16)
    P.step("s", sB, lambda s: s.dma_start(out=bd1b[:, 0:16], in_=dram["p4_bd1"][:, :]), 16)
    P.step("s", sB, lambda s: s.dma_start(out=bd2b[:, 0:64], in_=dram["p4_bd2"][:, :]), 16)
    for e in ("t", "v", "sc"):
        P.wait(e, gB)
        P.wait(e, sB)

    for e in range(8):
        par = e % 2
        if e >= 2:
            P.wait("g", tH1, 8 * (e - 1))
        P.step("g", gW1[par], lambda g, e=e, par=par: g.dma_start(
            out=w1buf[par].rearrange("p (c o) -> p c o", c=16),
            in_=dram["p4_wd1t"][e * 2048:(e + 1) * 2048, :].rearrange("(c p) o -> p c o", p=128)), 16)
        w1v = P.n(gW1[par])
        if e >= 2:
            P.wait("g", tR, 32 * (e - 1))
        P.step("g", gW2[par], lambda g, e=e, par=par: g.dma_start(
            out=w2buf[par].rearrange("p (c o) -> p c o", c=16),
            in_=dram["p4_wd2t"][e * 2048:(e + 1) * 2048, :].rearrange("(c p) o -> p c o", p=128)), 16)
        w2v = P.n(gW2[par])
        P.wait("t", gW1[par], w1v)
        for hb in range(2):
            for tt in range(4):
                hu = P.n(tH1)
                if hu >= 2:
                    P.wait("t", scH, hu - 1)
                ph = slot(hu % 2)
                for ic in range(8):
                    fn = lambda t, ph=ph, par=par, hb=hb, ic=ic, tt=tt: t.matmul(
                        ph[:, :], w1buf[par][:, (hb * 8 + ic) * 128:(hb * 8 + ic + 1) * 128],
                        xcibf[:, ic * 2048 + tt * 512: ic * 2048 + tt * 512 + 512],
                        start=(ic == 0), stop=(ic == 7))
                    if ic == 7:
                        P.step("t", tH1, fn)
                    else:
                        P.emit("t", fn)
                P.wait("sc", tH1, P.n(tH1))
                P.step("sc", scH, lambda sc, e=e, hb=hb, tt=tt, ph=ph: sc.activation(
                    out=heb[:, hb * 2048 + tt * 512: hb * 2048 + tt * 512 + 512],
                    in_=ph[:, :], func=AF.Relu, bias=bd1b[:, (e * 2 + hb):(e * 2 + hb) + 1]))
        P.wait("t", gW2[par], w2v)
        P.wait("t", scH, P.n(scH))
        pend4 = []

        def emit_loss(ent, e=e):
            dpar_, u32_, scs_t, first = ent
            if first:
                P.wait("t", vL, P.n(vL))
            P.wait("t", scS, scs_t)
            P.step("t", tL, lambda t, dpar=dpar_, u32=u32_: t.matmul(
                pool[0:1, 3584:4096], onecol[:, 0:1], dsq[dpar][:, :],
                start=(u32 == 0), stop=(u32 == 31)))

        for ob in range(8):
            for tt in range(4):
                ru = P.n(tR)
                u32 = ru % 32
                dpar = ru % 4
                if ru >= 4:
                    P.wait("t", vD2, ru - 3)
                    P.wait("v", scS, ru - 3)
                    P.wait("sc", tL, ru - 3)
                pr = slot(2 + dpar)
                for hc in range(2):
                    fn = lambda t, pr=pr, par=par, ob=ob, hc=hc, tt=tt: t.matmul(
                        pr[:, :], w2buf[par][:, (ob * 2 + hc) * 128:(ob * 2 + hc + 1) * 128],
                        heb[:, hc * 2048 + tt * 512: hc * 2048 + tt * 512 + 512],
                        start=(hc == 0), stop=(hc == 1))
                    if hc == 1:
                        P.step("t", tR, fn)
                    else:
                        P.emit("t", fn)
                P.wait("v", tR, P.n(tR))
                P.step("v", vD2, lambda v, e=e, ob=ob, tt=tt, pr=pr, dpar=dpar: v.scalar_tensor_tensor(
                    diff[dpar][:, :], pr[:, :], bd2b[:, (e * 8 + ob):(e * 8 + ob) + 1],
                    xcif[:, ob * 2048 + tt * 512: ob * 2048 + tt * 512 + 512],
                    ALU.add, ALU.subtract))
                P.wait("sc", vD2, P.n(vD2))
                P.step("sc", scS, lambda sc, dpar=dpar: sc.activation(
                    out=dsq[dpar][:, :], in_=diff[dpar][:, :], func=AF.Square))
                pend4.append((dpar, u32, P.n(scS), u32 == 0))
                if len(pend4) > 2:
                    emit_loss(pend4.pop(0))
        while pend4:
            emit_loss(pend4.pop(0))
        P.wait("v", tL, P.n(tL))
        P.step("v", vL, lambda v, e=e: v.reduce_sum(
            lrow[0:1, e:e + 1], pool[0:1, 3584:4096], axis=AX))
    P.wait("s", vL, P.n(vL))
    P.step("s", so4, lambda s: s.dma_start(out=out_loss[:, :], in_=lrow[0:1, 0:8]), 16)
    P.wait("s", so4, P.n(so4))
    P.emit("s", lambda s: s.nop())
    return [so4]


def emit_p4b(P, nc, A, pool, dram, xci_src, parts_src, out_xD, ag_wait=None):
    def slot(i, w=1):
        return pool[:, 512 * i:512 * (i + w)]
    gB = P.new_dma_sem("q4gB", "g")
    sB = P.new_dma_sem("q4sB", "s")
    gG = P.new_dma_sem("q4gG", "g")
    tB2 = P.new_sem("q4tB2")
    tA5 = P.new_sem("q4tA5")
    tY = P.new_sem("q4tY")
    scB2 = P.new_sem("q4scB2")
    scA5 = P.new_sem("q4scA5")
    vI = P.new_sem("q4vI")
    vY = P.new_sem("q4vY")
    so5 = [P.new_dma_sem("q4so5a", "s"), P.new_dma_sem("q4so5b", "s")]

    xcibf = A.bf16(8 * 2048)
    xcif = A.f32(8 * 2048)
    wbb = A.bf16(64 * 128)
    wadb = A.bf16(8 * 64)
    waub = A.bf16(1024)
    badb = A.f32(1)
    baub = A.f32(8)
    bbb = A.f32(8)
    cbias = A.f32(8)
    abuf = A.bf16(2048)
    parts = A.f32(32)
    lsum = A.f32(24)
    erowb = A.f32(8)
    iotab = A.f32(1)
    fxb = A.f32(1)
    idxt = A.bf16(6)
    zbuf = A.f32(8 * 2048)
    zb = A.bf16(512)
    zsq = A.bf16(512)
    statr = A.f32(2560)
    xnt = [A.f32(512) for _ in range(2)]
    statcol = A.bf16(2)
    onesrow = A.f32(128)
    epsr = A.f32(1)

    idxu = idxt.bitcast(U32)

    for ic in range(8):
        P.step("g", gB, lambda g, ic=ic: g.dma_start(
            out=xcibf[:, ic * 2048:(ic + 1) * 2048],
            in_=xci_src[ic * 128:(ic + 1) * 128, :]), 16)
    for ob in range(8):
        P.step("g", gB, lambda g, ob=ob: g.dma_start(
            out=wbb[:, ob * 1024:(ob + 1) * 1024].rearrange("p (c o) -> p c o", c=8),
            in_=dram["p4_wbt"][ob * 1024:(ob + 1) * 1024, :].rearrange("(c p) o -> p c o", p=128)), 16)
    P.step("g", gB, lambda g: g.dma_start(out=statcol[:, 0:1], in_=dram["c_statcol"][:, :]), 16)
    for ic in range(8):
        P.step("s", sB, lambda s, ic=ic: s.dma_start(
            out=xcif[:, ic * 2048:(ic + 1) * 2048],
            in_=xci_src[ic * 128:(ic + 1) * 128, :]), 16)
    P.step("s", sB, lambda s: s.dma_start(out=bbb[:, 0:8], in_=dram["p4_bb"][:, :]), 16)
    P.step("s", sB, lambda s: s.dma_start(out=erowb[0:1, 0:8], in_=dram["p4_erow"][:, :]), 16)
    P.step("s", sB, lambda s: s.dma_start(out=iotab[:, 0:1], in_=dram["p4_iota"][:, :]), 16)
    P.step("s", sB, lambda s: s.dma_start(out=onesrow[0:1, :], in_=dram["c_ones_row"][:, :]), 16)
    P.step("s", sB, lambda s: s.dma_start(out=epsr[0:1, :], in_=dram["c_eps"][:, :]), 16)
    if ag_wait is not None:
        P.wait("s", ag_wait[0], ag_wait[1])
    if isinstance(parts_src, list):
        for j in range(4):
            P.step("s", sB, lambda s, j=j: s.dma_start(
                out=parts[0:1, 8 * j:8 * j + 8], in_=parts_src[j]), 16)
    else:
        P.step("s", sB, lambda s: s.dma_start(out=parts[0:1, 0:32], in_=parts_src[:, :]), 16)
    for e in ("t", "v", "sc", "g"):
        P.wait(e, gB)
        P.wait(e, sB)

    P.step("v", vI, lambda v: v.tensor_add(lsum[0:1, 0:8], parts[0:1, 0:8], parts[0:1, 8:16]))
    P.wait("v", vI, P.n(vI))
    P.step("v", vI, lambda v: v.tensor_add(lsum[0:1, 8:16], parts[0:1, 16:24], parts[0:1, 24:32]))
    P.wait("v", vI, P.n(vI))
    P.step("v", vI, lambda v: v.tensor_add(lsum[0:1, 0:8], lsum[0:1, 0:8], lsum[0:1, 8:16]))
    P.wait("v", vI, P.n(vI))
    P.step("v", vI, lambda v: v.tensor_reduce(
        lsum[0:1, 8:9], lsum[0:1, 0:8], AX, ALU.min))
    P.wait("v", vI, P.n(vI))
    P.step("v", vI, lambda v: v.tensor_scalar(
        lsum[0:1, 16:24], lsum[0:1, 0:8], lsum[0:1, 8:9], None, ALU.is_equal))
    P.wait("v", vI, P.n(vI))
    P.step("v", vI, lambda v: v.tensor_mul(lsum[0:1, 16:24], lsum[0:1, 16:24], erowb[0:1, 0:8]))
    P.wait("v", vI, P.n(vI))
    P.step("v", vI, lambda v: v.reduce_sum(fxb[0:1, 0:1], lsum[0:1, 16:24], axis=AX))
    P.wait("t", vI, P.n(vI))
    P.step("t", tB2, lambda t: t.matmul(pool[:, 3584:3585], onesrow[0:1, :], fxb[0:1, 0:1],
                                        start=True, stop=True))
    P.wait("sc", tB2, P.n(tB2))
    P.step("sc", scB2, lambda sc: sc.copy(out=fxb[:, 0:1] if False else lsum[:, 0:1], in_=pool[:, 3584:3585]))
    P.wait("v", scB2, P.n(scB2))
    P.step("v", vI, lambda v: v.scalar_tensor_tensor(
        idxu[:, 0:1], lsum[:, 0:1], 1024.0, iotab[:, 0:1], ALU.mult, ALU.add))
    P.wait("v", vI, P.n(vI))
    P.step("v", vI, lambda v: v.scalar_tensor_tensor(
        idxu[:, 1:2], lsum[:, 0:1], 64.0, iotab[:, 0:1], ALU.mult, ALU.add))
    P.wait("v", vI, P.n(vI))
    P.step("v", vI, lambda v: v.scalar_tensor_tensor(
        idxu[:, 2:3], lsum[:, 0:1], 128.0, iotab[:, 0:1], ALU.mult, ALU.add))
    P.wait("v", vI, P.n(vI))
    P.step("v", vI, lambda v: v.memset(waub[:, :], 0.0))
    P.wait("v", vI, P.n(vI))
    P.step("v", vI, lambda v: v.memset(abuf[:, :], 0.0))
    P.wait("g", vI, P.n(vI))
    for ic in range(8):
        P.step("g", gG, lambda g, ic=ic: g.indirect_dma_start(
            out=wadb[:, ic * 64:(ic + 1) * 64], out_offset=None,
            in_=dram["p4_wad"][:, :],
            in_offset=IndirectOffsetOnAxis(ap=idxu[:, 0:1], axis=0),
            element_offset=ic * 128 * 64), 16)
    P.step("g", gG, lambda g: g.indirect_dma_start(
        out=waub[0:64, :], out_offset=None, in_=dram["p4_wau"][:, :],
        in_offset=IndirectOffsetOnAxis(ap=idxu[0:64, 1:2], axis=0)), 16)
    P.step("g", gG, lambda g: g.indirect_dma_start(
        out=badb[0:64, :], out_offset=None, in_=dram["p4_bad"][:, :],
        in_offset=IndirectOffsetOnAxis(ap=idxu[0:64, 1:2], axis=0)), 16)
    P.step("g", gG, lambda g: g.indirect_dma_start(
        out=baub[:, 0:8], out_offset=None, in_=dram["p4_bau"][:, :],
        in_offset=IndirectOffsetOnAxis(ap=idxu[:, 2:3], axis=0)), 16)
    for e in ("t", "v", "sc"):
        P.wait(e, gG)

    for tt in range(4):
        pa = slot(4 + (tt % 2))
        if tt >= 2:
            P.wait("t", scA5, tt - 1)
        for ic in range(8):
            fn = lambda t, pa=pa, ic=ic, tt=tt: t.matmul(
                pa[0:64, :], wadb[:, ic * 64:(ic + 1) * 64],
                xcibf[:, ic * 2048 + tt * 512: ic * 2048 + tt * 512 + 512],
                start=(ic == 0), stop=(ic == 7))
            if ic == 7:
                P.step("t", tA5, fn)
            else:
                P.emit("t", fn)
        P.wait("sc", tA5, P.n(tA5))
        P.step("sc", scA5, lambda sc, tt=tt, pa=pa: sc.activation(
            out=abuf[0:64, tt * 512:(tt + 1) * 512], in_=pa[0:64, :], func=AF.Relu,
            bias=badb[0:64, 0:1]))
    P.wait("v", sB, P.n(sB))
    P.step("v", vI, lambda v: v.tensor_add(cbias[:, 0:8], bbb[:, 0:8], baub[:, 0:8]))
    P.wait("t", scA5, P.n(scA5))
    P.wait("t", vI, P.n(vI))
    for ob in range(8):
        for tt in range(4):
            yu = P.n(tY)
            py = slot(yu % 4)
            if yu >= 4:
                P.wait("t", vY, yu - 3)
            for ic in range(8):
                P.emit("t", lambda t, py=py, ic=ic, ob=ob, tt=tt: t.matmul(
                    py[:, :], wbb[:, ob * 1024 + ic * 128: ob * 1024 + (ic + 1) * 128],
                    xcibf[:, ic * 2048 + tt * 512: ic * 2048 + tt * 512 + 512],
                    start=(ic == 0), stop=False))
            fn = lambda t, py=py, ob=ob, tt=tt: t.matmul(
                py[:, :], waub[:, ob * 128:(ob + 1) * 128],
                abuf[:, tt * 512:(tt + 1) * 512], start=False, stop=True)
            P.step("t", tY, fn)
            P.wait("v", tY, P.n(tY))
            P.step("v", vY, lambda v, py=py, ob=ob, tt=tt: v.scalar_tensor_tensor(
                zbuf[:, ob * 2048 + tt * 512: ob * 2048 + tt * 512 + 512],
                py[:, :], cbias[:, ob:ob + 1],
                xcif[:, ob * 2048 + tt * 512: ob * 2048 + tt * 512 + 512],
                ALU.add, ALU.add))
    for e in ("t", "v", "sc"):
        P.wait(e, vY, P.n(vY))
    lnb2 = {"zb": zb, "zsq": zsq, "statr": statr, "xnt": xnt,
            "statcol": statcol, "onesrow": onesrow, "epsr": epsr}

    st5 = [0]

    def sink(sub, oc, par):
        spar = st5[0] % 2
        st5[0] += 1
        P.wait("s", "p4lnvH", P.n("p4lnvH"))
        P.step("s", so5[spar], lambda s, oc=oc, sub=sub, par=par: s.dma_start(
            out=out_xD[oc * 128:(oc + 1) * 128, sub * 512: sub * 512 + 512],
            in_=xnt[par][:, 0:512]), 16)
        return [(so5[spar], P.n(so5[spar]))]

    emit_ln_fm(P, "p4ln", pool, lnb2, 4,
               lambda oc, sub: zbuf[:, oc * 2048 + sub * 512: oc * 2048 + sub * 512 + 512],
               sink, w=512)
    P.wait("s", so5[0], P.n(so5[0]))
    P.wait("s", so5[1], P.n(so5[1]))
    P.emit("s", lambda s: s.nop())
    return [so5[0], so5[1]]


def build_p4a():
    nc = bass.Bass("TRN2", target_bir_lowering=False)
    dram = {
        "p4_wd1t": nc.dram_tensor("p4_wd1t", [8 * 2048, 128], F32, kind="ExternalInput").ap(),
        "p4_wd2t": nc.dram_tensor("p4_wd2t", [8 * 2048, 128], F32, kind="ExternalInput").ap(),
        "p4_bd1": nc.dram_tensor("p4_bd1", [128, 16], F32, kind="ExternalInput").ap(),
        "p4_bd2": nc.dram_tensor("p4_bd2", [128, 64], F32, kind="ExternalInput").ap(),
        "c_onecol": nc.dram_tensor("c_onecol", [128, 1], BF16, kind="ExternalInput").ap(),
    }
    xci = nc.dram_tensor("p4_xci", [D, ROWS], F32, kind="ExternalInput").ap()
    out_loss = nc.dram_tensor("p4_loss", [1, 8], F32, kind="ExternalOutput").ap()
    stack = ExitStack()
    with stack:
        A = Arena(nc, stack, "arena", 104448)
        pool_t = stack.enter_context(nc.psum_tensor("pool", [128, 4096], F32))
        P = Plan(nc, stack)
        emit_p4a(P, nc, A, pool_t, dram, xci, out_loss)
        block = stack.enter_context(nc.Block())
        P.realize(block)
    return nc


def build_p4b():
    nc = bass.Bass("TRN2", target_bir_lowering=False)
    dram = {
        "p4_wbt": nc.dram_tensor("p4_wbt", [8 * 1024, 128], F32, kind="ExternalInput").ap(),
        "p4_bb": nc.dram_tensor("p4_bb", [128, 8], F32, kind="ExternalInput").ap(),
        "p4_wad": nc.dram_tensor("p4_wad", [8 * 1024, 64], F32, kind="ExternalInput").ap(),
        "p4_bad": nc.dram_tensor("p4_bad", [8 * 64, 1], F32, kind="ExternalInput").ap(),
        "p4_wau": nc.dram_tensor("p4_wau", [8 * 64, 1024], F32, kind="ExternalInput").ap(),
        "p4_bau": nc.dram_tensor("p4_bau", [8 * 128, 8], F32, kind="ExternalInput").ap(),
        "p4_iota": nc.dram_tensor("p4_iota", [128, 1], F32, kind="ExternalInput").ap(),
        "p4_erow": nc.dram_tensor("p4_erow", [1, 8], F32, kind="ExternalInput").ap(),
        "c_ones_row": nc.dram_tensor("c_ones_row", [1, 128], F32, kind="ExternalInput").ap(),
        "c_statcol": nc.dram_tensor("c_statcol", [128, 1], BF16, kind="ExternalInput").ap(),
        "c_eps": nc.dram_tensor("c_eps", [1, 1], F32, kind="ExternalInput").ap(),
    }
    xci = nc.dram_tensor("p4_xci", [D, ROWS], F32, kind="ExternalInput").ap()
    parts = nc.dram_tensor("p4_parts", [1, 32], F32, kind="ExternalInput").ap()
    out_xD = nc.dram_tensor("p4_xD", [D, ROWS], F32, kind="ExternalOutput").ap()
    stack = ExitStack()
    with stack:
        A = Arena(nc, stack, "arena", 104448)
        pool_t = stack.enter_context(nc.psum_tensor("pool", [128, 4096], F32))
        P = Plan(nc, stack)
        emit_p4b(P, nc, A, pool_t, dram, xci, parts, out_xD)
        block = stack.enter_context(nc.Block())
        P.realize(block)
    return nc



_NC_CACHE = {}


def _get_nc(name, builder):
    if name not in _NC_CACHE:
        _NC_CACHE[name] = builder()
    return _NC_CACHE[name]


def kernel_unfused(**inputs):
    maps0 = stage_p0(inputs)
    nc0 = _get_nc("p0", build_p0)
    res0 = run_bass_kernel_spmd(nc0, maps0, core_ids=list(range(NCORE)))
    xA = [res0.results[c]["p0_xA"] for c in range(NCORE)]

    maps1 = stage_p1(inputs)
    for c in range(NCORE):
        maps1[c]["p1_xA"] = np.ascontiguousarray(xA[c])
    nc1 = _get_nc("p1", build_p1)
    res1 = run_bass_kernel_spmd(nc1, maps1, core_ids=list(range(NCORE)))
    xB = [res1.results[c]["p1_xB"] for c in range(NCORE)]
    kf = [res1.results[c]["p1_kf"] for c in range(NCORE)]
    vt = [res1.results[c]["p1_vt"] for c in range(NCORE)]

    kf_full = [np.ascontiguousarray(np.concatenate(kf[4 * b:4 * b + 4], axis=1)) for b in range(B)]
    vt_full = [np.ascontiguousarray(np.concatenate(vt[4 * b:4 * b + 4], axis=0)) for b in range(B)]
    xB_full = [np.concatenate(xB[4 * b:4 * b + 4], axis=1) for b in range(B)]

    maps3 = stage_p3(inputs)
    for c in range(NCORE):
        b, cp = c // 4, c % 4
        perm = p3_perm_rows(cp)
        maps3[c]["p3_xbi"] = np.ascontiguousarray(xB_full[b][:, perm])
        maps3[c]["p3_kf"] = kf_full[b]
        maps3[c]["p3_vt"] = vt_full[b]
    nc3 = _get_nc("p3", build_p3)
    res3 = run_bass_kernel_spmd(nc3, maps3, core_ids=list(range(NCORE)))
    xC = [res3.results[c]["p3_xC"] for c in range(NCORE)]

    maps4 = stage_p4(inputs)
    maps4a = []
    for c in range(NCORE):
        m = {k: maps4[c][k] for k in ("p4_wd1t", "p4_wd2t", "p4_bd1", "p4_bd2", "c_onecol")}
        m["p4_xci"] = np.ascontiguousarray(xC[c])
        maps4a.append(m)
    nc4a = _get_nc("p4a", build_p4a)
    res4a = run_bass_kernel_spmd(nc4a, maps4a, core_ids=list(range(NCORE)))
    parts = [res4a.results[c]["p4_loss"] for c in range(NCORE)]

    maps4b = []
    for c in range(NCORE):
        b = c // 4
        m = {k: maps4[c][k] for k in ("p4_wbt", "p4_bb", "p4_wad", "p4_bad", "p4_wau",
                                      "p4_bau", "p4_iota", "p4_erow", "c_ones_row",
                                      "c_statcol", "c_eps")}
        m["p4_xci"] = np.ascontiguousarray(xC[c])
        m["p4_parts"] = np.ascontiguousarray(
            np.concatenate([parts[4 * b + j] for j in range(4)], axis=1))
        maps4b.append(m)
    nc4b = _get_nc("p4b", build_p4b)
    res4b = run_bass_kernel_spmd(nc4b, maps4b, core_ids=list(range(NCORE)))

    out = np.empty((B, T, D), dtype=np.float32)
    for c in range(NCORE):
        b, cp = c // 4, c % 4
        perm = p3_perm_rows(cp)
        out[b, perm, :] = res4b.results[c]["p4_xD"].T
    return out



def _stage_idx_tab(cp):
    tab = np.zeros((128, 128), dtype=np.uint32)
    p = np.arange(128)
    blocks = []
    for m in range(8):
        blocks.extend([8 * m + cp, 8 * m + 7 - cp])
    for s_, j in enumerate(blocks):
        r, blk2 = j // 16, (j % 16) % 8
        for dc in range(8):
            tab[:, s_ * 8 + dc] = (r * 1024 + dc * 128 + p) * 8 + blk2
    return tab


def stage_fused(inputs):
    m0 = stage_p0(inputs)
    m1 = stage_p1(inputs)
    m3 = stage_p3(inputs)
    m4 = stage_p4(inputs)
    maps = []
    for c in range(NCORE):
        m = {}
        m.update(m0[c])
        m.update(m1[c])
        m.update(m3[c])
        m.update(m4[c])
        m["fx_idxtab"] = _stage_idx_tab(c % 4)
        maps.append(m)
    return maps


def build_fused():
    nc = bass.Bass("TRN2", target_bir_lowering=False)
    dram = {}
    dram.update({
        "p0_xT": nc.dram_tensor("p0_xT", [D, TK], F32, kind="ExternalInput").ap(),
        "p0_wqkT": nc.dram_tensor("p0_wqkT", [16 * D, 128], F32, kind="ExternalInput").ap(),
        "p0_wvT": nc.dram_tensor("p0_wvT", [D, 1024], F32, kind="ExternalInput").ap(),
        "p0_woutT": nc.dram_tensor("p0_woutT", [8 * D, 128], F32, kind="ExternalInput").ap(),
        "p0_cos": nc.dram_tensor("p0_cos", [128, TK], F32, kind="ExternalInput").ap(),
        "p0_sin": nc.dram_tensor("p0_sin", [128, TK], F32, kind="ExternalInput").ap(),
        "p0_perm": nc.dram_tensor("p0_perm", [128, 128], BF16, kind="ExternalInput").ap(),
        "p0_mask": nc.dram_tensor("p0_mask", [128, 12 * 512], BF16, kind="ExternalInput").ap(),
        "c_sel": nc.dram_tensor("c_sel", [1, 256], F32, kind="ExternalInput").ap(),
        "c_ones_row": nc.dram_tensor("c_ones_row", [1, 128], F32, kind="ExternalInput").ap(),
        "c_statcol": nc.dram_tensor("c_statcol", [128, 1], BF16, kind="ExternalInput").ap(),
        "c_onecol": nc.dram_tensor("c_onecol", [128, 1], BF16, kind="ExternalInput").ap(),
        "c_eps": nc.dram_tensor("c_eps", [1, 1], F32, kind="ExternalInput").ap(),
    })
    dram.update(p1_dram_decls(nc))
    dram.update(p3_dram_decls(nc))
    dram.update({
        "p4_wd1t": nc.dram_tensor("p4_wd1t", [8 * 2048, 128], F32, kind="ExternalInput").ap(),
        "p4_wd2t": nc.dram_tensor("p4_wd2t", [8 * 2048, 128], F32, kind="ExternalInput").ap(),
        "p4_bd1": nc.dram_tensor("p4_bd1", [128, 16], F32, kind="ExternalInput").ap(),
        "p4_bd2": nc.dram_tensor("p4_bd2", [128, 64], F32, kind="ExternalInput").ap(),
        "p4_wbt": nc.dram_tensor("p4_wbt", [8 * 1024, 128], F32, kind="ExternalInput").ap(),
        "p4_bb": nc.dram_tensor("p4_bb", [128, 8], F32, kind="ExternalInput").ap(),
        "p4_wad": nc.dram_tensor("p4_wad", [8 * 1024, 64], F32, kind="ExternalInput").ap(),
        "p4_bad": nc.dram_tensor("p4_bad", [8 * 64, 1], F32, kind="ExternalInput").ap(),
        "p4_wau": nc.dram_tensor("p4_wau", [8 * 64, 1024], F32, kind="ExternalInput").ap(),
        "p4_bau": nc.dram_tensor("p4_bau", [8 * 128, 8], F32, kind="ExternalInput").ap(),
        "p4_iota": nc.dram_tensor("p4_iota", [128, 1], F32, kind="ExternalInput").ap(),
        "p4_erow": nc.dram_tensor("p4_erow", [1, 8], F32, kind="ExternalInput").ap(),
        "fx_idxtab": nc.dram_tensor("fx_idxtab", [128, 128], U32, kind="ExternalInput").ap(),
    })
    xA_d = nc.dram_tensor("fx_xA", [D, ROWS], F32).ap()
    xB_d = [nc.dram_tensor(f"fx_xB{r}", [D, 1024], F32) for r in range(2)]
    kf_d = [nc.dram_tensor(f"fx_kf{r}", [D, 1024], BF16) for r in range(2)]
    vt_d = [nc.dram_tensor(f"fx_vt{r}", [1024, D], BF16) for r in range(2)]
    xbag = [nc.dram_tensor(f"fx_xbag{r}", [4 * D, 1024], F32) for r in range(2)]
    kfag = [nc.dram_tensor(f"fx_kfag{r}", [4 * D, 1024], BF16) for r in range(2)]
    vtag = [nc.dram_tensor(f"fx_vtag{r}", [4 * 1024, D], BF16) for r in range(2)]
    loss_d = nc.dram_tensor("fx_loss", [1, 8], F32)
    lossag = nc.dram_tensor("fx_lossag", [4, 8], F32)
    xC_d = nc.dram_tensor("fx_xC", [D, ROWS], F32).ap()
    out_xD = nc.dram_tensor("fx_xD", [D, ROWS], F32, kind="ExternalOutput").ap()

    groups = [[0, 1, 2, 3], [4, 5, 6, 7]]
    stack = ExitStack()
    with stack:
        A = Arena(nc, stack, "arena", 104448)
        pool_t = stack.enter_context(nc.psum_tensor("pool", [128, 4096], F32))
        P = Plan(nc, stack)

        def fence(terms):
            for e in ENGS:
                for s_ in terms:
                    P.wait(e, s_)

        rdy = P.new_sem("fxrdy")
        clr = P.new_sem("fxclr")

        def recycle(released):
            P.phase_reset()
            if not released:
                return
            for e in ENGS:
                P.step(e, rdy, lambda x: x.nop())
            tgt = P.n(rdy)
            P.wait("g", rdy, tgt)
            for nm_h, nm_n in released:
                P.emit("g", lambda g, h=nm_h, v=nm_n: g.wait_ge(h, v) if v > 0 else g.nop())
            for i, (h, n) in enumerate(released):
                if i == len(released) - 1:
                    P.step("g", clr, lambda g, h=h: g.sem_clear(h))
                else:
                    P.emit("g", lambda g, h=h: g.sem_clear(h))
            for e in ENGS:
                P.wait(e, clr, P.n(clr))

        base = A.off
        mk = P.mark()
        t0 = emit_p0(P, nc, A, pool_t, dram, xA_d)
        fence(t0)
        rel = P.release_since(mk)
        recycle(rel)
        A.off = base
        cc = [P.new_sem(f"fxcc{i}") for i in range(7)]
        mk = P.mark()

        def rt_done(rt, store_sems):
            for s_ in store_sems[0:2]:
                P.wait("g", s_, P.n(s_))
            P.step("g", cc[3 * rt + 0], lambda g, rt=rt: g.collective_compute(
                "AllGather", ALU.bypass, replica_groups=groups,
                ins=[xB_d[rt].ap().opt()], outs=[xbag[rt].ap().opt()]), 1)
            for s_ in store_sems[2:4]:
                P.wait("g", s_, P.n(s_))
            P.step("g", cc[3 * rt + 1], lambda g, rt=rt: g.collective_compute(
                "AllGather", ALU.bypass, replica_groups=groups,
                ins=[kf_d[rt].ap().opt()], outs=[kfag[rt].ap().opt()]), 1)
            for s_ in store_sems[4:6]:
                P.wait("g", s_, P.n(s_))
            P.step("g", cc[3 * rt + 2], lambda g, rt=rt: g.collective_compute(
                "AllGather", ALU.bypass, replica_groups=groups,
                ins=[vt_d[rt].ap().opt()], outs=[vtag[rt].ap().opt()]), 1)

        t1 = emit_p1(P, nc, A, pool_t, dram, xA_d,
                     [xB_d[0].ap(), xB_d[1].ap()],
                     [kf_d[0].ap(), kf_d[1].ap()],
                     [vt_d[0].ap(), vt_d[1].ap()], rt_done_cb=rt_done)
        fence(t1)
        rel = P.release_since(mk)
        recycle(rel)
        A.off = base
        mk = P.mark()
        t3 = emit_p3(P, nc, A, pool_t, dram, None,
                     [kfag[0].ap(), kfag[1].ap()], [vtag[0].ap(), vtag[1].ap()], xC_d,
                     fused=True, xb_ag=[xbag[0].ap(), xbag[1].ap()],
                     idx_tab=dram["fx_idxtab"],
                     ag_waits={"xb": [(cc[0], 1), (cc[3], 1)],
                               "kv": [(cc[1], 1), (cc[2], 1), (cc[4], 1), (cc[5], 1)]})
        fence(t3)
        rel = P.release_since(mk)
        recycle(rel)
        A.off = base
        mk = P.mark()
        t4 = emit_p4a(P, nc, A, pool_t, dram, xC_d, loss_d.ap())
        fence(t4)
        rel = P.release_since(mk)
        recycle(rel)
        P.step("g", cc[6], lambda g: g.collective_compute(
            "AllGather", ALU.bypass, replica_groups=groups,
            ins=[loss_d.ap().opt()], outs=[lossag.ap().opt()]), 1)
        A.off = base
        parts_aps = [lossag.ap()[j:j + 1, :] for j in range(4)]
        t5 = emit_p4b(P, nc, A, pool_t, dram, xC_d, parts_aps, out_xD,
                      ag_wait=(cc[6], 1))
        fence(t5)
        block = stack.enter_context(nc.Block())
        P.realize(block)
    return nc


def kernel(**inputs):
    maps = stage_fused(inputs)
    nc = _get_nc("fused", build_fused)
    res = run_bass_kernel_spmd(nc, maps, core_ids=list(range(NCORE)))
    out = np.empty((B, T, D), dtype=np.float32)
    for c in range(NCORE):
        b, cp = c // 4, c % 4
        perm = p3_perm_rows(cp)
        out[b, perm, :] = res.results[c]["fx_xD"].T
    return out


# version 13
# speedup vs baseline: 1.0699x; 1.0699x over previous
import numpy as np
import ml_dtypes
from contextlib import ExitStack

import concourse.bass as bass
import concourse.mybir as mybir
from concourse.bass_utils import run_bass_kernel_spmd
from concourse.bass import IndirectOffsetOnAxis

F32 = mybir.dt.float32
BF16 = mybir.dt.bfloat16
U32 = mybir.dt.uint32
AF = mybir.ActivationFunctionType
ALU = mybir.AluOpType
AX = mybir.AxisListType.X

BF = ml_dtypes.bfloat16

B, T, D = 2, 8192, 1024
NH, DH, WIN = 16, 64, 256
DF = 4096
NCORE = 8
ROWS = 2048
HALO = 256
TK = ROWS + HALO


ENGS = ("t", "sc", "v", "g", "s")


class Plan:

    def __init__(self, nc, stack):
        self.nc = nc
        self.stack = stack
        self.streams = {e: [] for e in ENGS}
        self.sem = {}
        self.free = []
        self.order = []

    def new_sem(self, name):
        assert name not in self.sem, name
        if self.free:
            h, n = self.free.pop()
            self.sem[name] = [h, 0]
        else:
            h = self.stack.enter_context(self.nc.semaphore(name))
            self.sem[name] = [h, 0]
        self.order.append(name)
        return name

    def new_dma_sem(self, name, kind="g"):
        assert name not in self.sem, name
        if not hasattr(self, "dma_pools"):
            self.dma_pools = {"g": [], "s": []}
            self.dma_cursor = {"g": 0, "s": 0}
        pool = self.dma_pools[kind]
        i = self.dma_cursor[kind]
        self.dma_cursor[kind] = i + 1
        if i >= len(pool):
            h = self.stack.enter_context(self.nc.semaphore(name))
            pool.append([h, 0])
        self.sem[name] = [pool[i][0], pool[i][1]]
        pool[i].append(name)
        return name

    def phase_reset(self):
        if hasattr(self, "dma_pools"):
            for kind in ("g", "s"):
                for ent in self.dma_pools[kind]:
                    while len(ent) > 2:
                        alias = ent.pop()
                        if alias in self.sem:
                            ent[1] = self.sem[alias][1]
                            del self.sem[alias]
                self.dma_cursor[kind] = 0

    def mark(self):
        return len(self.order)

    def release_since(self, mk):
        released = []
        for name in self.order[mk:]:
            ent = self.sem[name]
            released.append((ent[0], ent[1]))
            self.free.append([ent[0], 0])
            del self.sem[name]
        del self.order[mk:]
        return released

    def n(self, name):
        return self.sem[name][1]

    def emit(self, eng, fn):
        self.streams[eng].append(fn)

    def wait(self, eng, name, val=None):
        h, cnt = self.sem[name]
        v = cnt if val is None else val
        if v <= 0:
            return
        self.emit(eng, lambda e, h=h, v=v: e.wait_ge(h, v))

    def step(self, eng, name, fn, k=1):
        h, _ = self.sem[name]
        self.emit(eng, lambda e, h=h, k=k, fn=fn: fn(e).then_inc(h, k))
        self.sem[name][1] += k
        return self.sem[name][1]

    def realize(self, block):
        for ename, attr in (("t", "tensor"), ("sc", "scalar"), ("v", "vector"),
                            ("g", "gpsimd"), ("s", "sync")):
            stream = self.streams[ename]
            if not stream:
                continue
            deco = getattr(block, attr)

            def body(eng, stream=stream):
                for fn in stream:
                    fn(eng)
            deco(body)


class Arena:

    def __init__(self, nc, stack, name, n_bf16):
        self.t = stack.enter_context(nc.sbuf_tensor(name, [128, n_bf16], BF16))
        self.n = n_bf16
        self.off = 0
        self.marks = []

    def save(self):
        self.marks.append(self.off)

    def restore(self):
        self.off = self.marks.pop()

    def bf16(self, cols):
        o = self.off
        assert o + cols <= self.n, f"arena overflow {o}+{cols}>{self.n}"
        self.off = o + cols
        return self.t[:, o:o + cols]

    def f32(self, cols):
        if self.off % 2:
            self.off += 1
        ap = self.bf16(2 * cols)
        return ap.bitcast(F32)



def _rotary_tables(pos):
    inv_freq = (1.0 / 10000.0 ** (np.arange(0, DH, 2) / DH)).astype(np.float64)
    p = np.arange(128)
    freq = inv_freq[p % 32]
    ang = freq[:, None] * pos[None, :].astype(np.float64)
    cos = np.cos(ang).astype(np.float32)
    sin = np.sin(ang).astype(np.float32)
    sign = np.where(p % 64 < 32, -1.0, 1.0).astype(np.float32)
    return cos, (sin * sign[:, None]).astype(np.float32)


def _perm_swap32():
    Pm = np.zeros((128, 128), dtype=np.float32)
    for p in range(128):
        k = (p & ~63) | ((p + 32) % 64)
        Pm[k, p] = 1.0
    return Pm.astype(BF)


def _l0_masks(cp):
    out = np.zeros((128, 12 * 512), dtype=np.float32)
    kl = np.arange(128)[:, None]
    ql = np.arange(512)[None, :]
    for s in range(2):
        for ci in range(6):
            d = ql + 256 - 128 * ci - kl
            ok = (d >= 0) & (d < WIN)
            if cp == 0 and s == 0:
                kv_abs = -256 + 128 * ci + kl + 0 * ql
                ok = ok & (kv_abs >= 0)
            out[:, (s * 6 + ci) * 512:(s * 6 + ci + 1) * 512] = ok
    return out.astype(BF)


L0_RANGES = [(max(0, -256 + 128 * ci), min(512, -256 + 128 * ci + 384)) for ci in range(6)]


def _pretile2(w, blk=128):
    i, o = w.shape
    t = np.ascontiguousarray(w.reshape(i, o // blk, blk).transpose(1, 0, 2))
    return t.reshape((o // blk) * i, blk)


def stage_p0(inputs):
    x = inputs["L0_attn_x"]
    wqkvT = np.ascontiguousarray(inputs["L0_attn_w_qkv"].T)
    woutT = np.ascontiguousarray(inputs["L0_attn_w_out"].T)
    wqkT = _pretile2(wqkvT[:, :2048])
    wvT = np.ascontiguousarray(wqkvT[:, 2048:])
    woutTt = _pretile2(woutT)
    sel = np.zeros((1, 256), dtype=np.float32)
    sel[0, 0:64] = 1.0
    sel[0, 192:256] = 1.0
    ones_row = np.ones((1, 128), dtype=np.float32)
    statcol = np.full((128, 1), 1.0 / D, dtype=np.float32).astype(BF)
    onecol = np.ones((128, 1), dtype=np.float32).astype(BF)
    perm = _perm_swap32()

    maps = []
    for c in range(NCORE):
        b, cp = c // 4, c % 4
        q0 = 2048 * cp
        xT = np.zeros((D, TK), dtype=np.float32)
        lo = q0 - HALO
        src_lo = max(0, lo)
        xT[:, src_lo - lo:] = x[b, src_lo:q0 + ROWS].T
        pos = np.maximum(np.arange(lo, q0 + ROWS), 0)
        cosf, sinf = _rotary_tables(pos)
        maps.append({
            "p0_xT": np.ascontiguousarray(xT),
            "p0_wqkT": wqkT, "p0_wvT": wvT, "p0_woutT": woutTt,
            "p0_cos": cosf, "p0_sin": sinf,
            "p0_perm": perm, "p0_mask": _l0_masks(cp),
            "c_sel": sel, "c_ones_row": ones_row,
            "c_statcol": statcol, "c_onecol": onecol,
            "c_eps": np.full((1, 1), 1e-5, dtype=np.float32),
        })
    return maps



def emit_p0(P, nc, A, pool, dram, out_xA):
    st = P.stack

    gl_bulk = P.new_dma_sem("p0glb", "g")
    sl_bulk = P.new_dma_sem("p0slb", "s")
    gl_slab = P.new_dma_sem("p0gls", "g")
    gl_wout = [P.new_dma_sem("p0glw0", "g"), P.new_dma_sem("p0glw1", "g")]
    sl_x0 = [P.new_dma_sem("p0slx0", "s"), P.new_dma_sem("p0slx1", "s")]
    so_p = [P.new_dma_sem("p0soa", "s"), P.new_dma_sem("p0sob", "s")]
    tA = P.new_sem("p0tA")
    tB = P.new_sem("p0tB")
    tC = P.new_sem("p0tC")
    tD = P.new_sem("p0tD")
    tE = P.new_sem("p0tE")
    tF = P.new_sem("p0tF")
    tG = P.new_sem("p0tG")
    tH = P.new_sem("p0tH")
    tV = P.new_sem("p0tV")
    tZ = P.new_sem("p0tZ")
    scA = P.new_sem("p0scA")
    scB = P.new_sem("p0scB")
    scC = P.new_sem("p0scC")
    scD = P.new_sem("p0scD")
    scE = P.new_sem("p0scE")
    scV = P.new_sem("p0scV")
    vA = P.new_sem("p0vA")
    vB = P.new_sem("p0vB")
    vC = P.new_sem("p0vC")
    vD = P.new_sem("p0vD")
    vE = P.new_sem("p0vE")
    vF = P.new_sem("p0vF")
    vG = P.new_sem("p0vG")
    vH = P.new_sem("p0vH")
    vS = P.new_sem("p0vS")
    scS = P.new_sem("p0scS")


    def slot(i, w=1):
        return pool[:, 512 * i:512 * (i + w)]
    pj = [slot(0), slot(1)]
    pv = slot(0, 2)
    psw = [slot(2), slot(3)]
    psc = [slot(4, 2), slot(6, 2)]
    pav = [slot(2), slot(3)]
    plp_e = pool[0:1, 0:512]
    plp_o = pool[0:1, 512:1024]
    plb = pool[:, 2048:2560]
    pz = slot(0)
    pst_m = pool[0:1, 0:512]
    pst_q = pool[0:1, 512:1024]
    pbr = slot(2, 2)

    attnf = A.bf16(8 * ROWS)
    maskb = A.bf16(12 * 512)
    permb = A.bf16(128)
    selb = A.f32(256)
    onesrow = A.f32(128)
    statcol = A.bf16(2)
    onecol = A.bf16(2)
    epsr = A.f32(1)
    linv = A.f32(1024)
    lbs = A.f32(512)

    A.save()
    xbf = A.bf16(8 * TK)
    wvb = A.bf16(8 * 1024)
    vbuf = A.bf16(18 * 1024)
    qkslab = [A.bf16(8 * 128) for _ in range(2)]
    qp = A.bf16(ROWS)
    kp = A.bf16(TK)
    qc = [A.bf16(512) for _ in range(2)]
    rt1 = [A.f32(512) for _ in range(2)]
    rt2 = [A.f32(512) for _ in range(2)]
    probs = A.bf16(12 * 512)
    cosb = A.f32(TK)
    sinb = A.f32(TK)

    for i in range(8):
        P.step("g", gl_bulk, lambda g, i=i: g.dma_start(
            out=xbf[:, i * TK:(i + 1) * TK], in_=dram["p0_xT"][i * 128:(i + 1) * 128, :]), 16)
    for i in range(8):
        P.step("g", gl_bulk, lambda g, i=i: g.dma_start(
            out=wvb[:, i * 1024:(i + 1) * 1024], in_=dram["p0_wvT"][i * 128:(i + 1) * 128, :]), 16)
    P.step("g", gl_bulk, lambda g: g.dma_start(out=permb[:, :], in_=dram["p0_perm"][:, :]), 16)
    P.step("g", gl_bulk, lambda g: g.dma_start(out=maskb[:, :], in_=dram["p0_mask"][:, :]), 16)
    P.step("g", gl_bulk, lambda g: g.dma_start(out=statcol[:, 0:1], in_=dram["c_statcol"][:, :]), 16)
    P.step("g", gl_bulk, lambda g: g.dma_start(out=onecol[:, 0:1], in_=dram["c_onecol"][:, :]), 16)
    P.step("s", sl_bulk, lambda s: s.dma_start(out=cosb[:, :], in_=dram["p0_cos"][:, :]), 16)
    P.step("s", sl_bulk, lambda s: s.dma_start(out=sinb[:, :], in_=dram["p0_sin"][:, :]), 16)
    P.step("s", sl_bulk, lambda s: s.dma_start(out=selb[0:1, :], in_=dram["c_sel"][:, :]), 16)
    P.step("s", sl_bulk, lambda s: s.dma_start(out=onesrow[0:1, :], in_=dram["c_ones_row"][:, :]), 16)
    P.step("s", sl_bulk, lambda s: s.dma_start(out=epsr[0:1, :], in_=dram["c_eps"][:, :]), 16)
    for e in ("t", "v", "sc"):
        P.wait(e, gl_bulk)
        P.wait(e, sl_bulk)

    P.step("v", vS, lambda v: v.memset(probs[:, :], 0.0))

    for tc in range(18):
        if tc >= 1:
            P.wait("t", scV, tc)
        for ic in range(8):
            for h in range(2):
                fn = lambda t, tc=tc, ic=ic, h=h: t.matmul(
                    pv[:, h * 512:(h + 1) * 512],
                    xbf[:, ic * TK + tc * 128: ic * TK + tc * 128 + 128],
                    wvb[:, ic * 1024 + h * 512: ic * 1024 + (h + 1) * 512],
                    start=(ic == 0), stop=(ic == 7))
                if ic == 7 and h == 1:
                    P.step("t", tV, fn)
                else:
                    P.emit("t", fn)
        P.wait("sc", tV, P.n(tV))
        P.step("sc", scV, lambda sc, tc=tc: sc.copy(
            out=vbuf[:, tc * 1024:(tc + 1) * 1024], in_=pv[:, :]))

    P.wait("t", scV, P.n(scV))
    tile_g = 0
    chunk_g = 0
    for hp in range(8):
        if hp >= 1:
            P.wait("g", tA, 9 * hp)
        P.step("g", gl_slab, lambda g, hp=hp: g.dma_start(
            out=qkslab[0].rearrange("p (c o) -> p c o", c=8),
            in_=dram["p0_wqkT"][hp * 1024:(hp + 1) * 1024, :].rearrange("(c p) o -> p c o", p=128)), 16)
        P.step("g", gl_slab, lambda g, hp=hp: g.dma_start(
            out=qkslab[1].rearrange("p (c o) -> p c o", c=8),
            in_=dram["p0_wqkT"][(hp + 8) * 1024:(hp + 9) * 1024, :].rearrange("(c p) o -> p c o", p=128)), 16)
        P.wait("t", gl_slab)

        P.wait("t", vD, P.n(vD))
        P.wait("t", vC, P.n(vC))
        tiles = []
        for i in range(4):
            tiles.append(("q", qp, 512 * i, 512, HALO + 512 * i))
        for i in range(5):
            w = 512 if i < 4 else 256
            tiles.append(("k", kp, 512 * i, w, 512 * i))
        for (kind, dest, dcol, w, xcol) in tiles:
            par = tile_g % 2
            slabi = 0 if kind == "q" else 1
            if tile_g >= 2:
                P.wait("t", vA, tile_g - 1)
            for ic in range(8):
                fn = lambda t, par=par, slabi=slabi, ic=ic, xcol=xcol, w=w: t.matmul(
                    pj[par][:, 0:w],
                    qkslab[slabi][:, ic * 128:(ic + 1) * 128],
                    xbf[:, ic * TK + xcol: ic * TK + xcol + w],
                    start=(ic == 0), stop=(ic == 7))
                if ic == 7:
                    P.step("t", tA, fn)
                else:
                    P.emit("t", fn)
            P.wait("sc", tA, P.n(tA))
            P.step("sc", scA, lambda sc, par=par, w=w: sc.copy(
                out=qc[par][:, 0:w], in_=pj[par][:, 0:w]))
            P.wait("t", scA, P.n(scA))
            P.step("t", tB, lambda t, par=par, w=w: t.matmul(
                psw[par][:, 0:w], permb[:, :], qc[par][:, 0:w], start=True, stop=True))
            P.wait("v", tA, P.n(tA))
            P.wait("v", tB, P.n(tB))
            if tile_g >= 2:
                P.wait("v", vA, tile_g - 1)
            P.step("v", vS, lambda v, par=par, w=w, xcol=xcol: v.tensor_mul(
                rt1[par][:, 0:w], pj[par][:, 0:w], cosb[:, xcol:xcol + w]))
            P.step("v", vS, lambda v, par=par, w=w, xcol=xcol: v.tensor_mul(
                rt2[par][:, 0:w], psw[par][:, 0:w], sinb[:, xcol:xcol + w]))
            P.wait("v", vS, P.n(vS))
            P.step("v", vA, lambda v, par=par, w=w, dest=dest, dcol=dcol: v.tensor_add(
                dest[:, dcol:dcol + w], rt1[par][:, 0:w], rt2[par][:, 0:w]))
            tile_g += 1

        rot_done = P.n(vA)
        P.wait("t", vA, rot_done)

        for qt in range(4):
            P.wait("t", vD, P.n(vD) - 1)
            mset = 0 if qt == 0 else 1
            for ci in range(6):
                lo, hi = L0_RANGES[ci]
                w = hi - lo
                kcol = 512 * qt + 128 * ci
                cp2 = chunk_g % 2
                if chunk_g >= 2:
                    P.wait("t", scB, 2 * (chunk_g - 1))
                if ci == 0:
                    P.wait("t", scC, P.n(scC))
                for par in range(2):
                    fn = lambda t, cp2=cp2, par=par, lo=lo, w=w, kcol=kcol, qt=qt: t.matmul(
                        psc[cp2][:, par * 512 + lo: par * 512 + lo + w],
                        kp[64 * par:64 * par + 64, kcol:kcol + 128],
                        qp[64 * par:64 * par + 64, 512 * qt + lo: 512 * qt + lo + w],
                        start=True, stop=True)
                    if par == 1:
                        P.step("t", tC, fn)
                    else:
                        P.emit("t", fn)
                P.wait("sc", tC, P.n(tC))
                P.step("sc", scB, lambda sc, cp2=cp2, ci=ci, lo=lo, w=w: sc.activation(
                    out=probs[:, (0 * 6 + ci) * 512 + lo: (0 * 6 + ci) * 512 + lo + w],
                    in_=psc[cp2][:, 0 * 512 + lo: 0 * 512 + lo + w], func=AF.Exp, scale=0.125))
                P.step("sc", scB, lambda sc, cp2=cp2, ci=ci, lo=lo, w=w: sc.activation(
                    out=probs[:, (1 * 6 + ci) * 512 + lo: (1 * 6 + ci) * 512 + lo + w],
                    in_=psc[cp2][:, 1 * 512 + lo: 1 * 512 + lo + w], func=AF.Exp, scale=0.125))
                P.wait("v", scB, P.n(scB))
                P.step("v", vB, lambda v, ci=ci, mset=mset: v.tensor_mul(
                    probs[:, (0 * 6 + ci) * 512: (0 * 6 + ci + 1) * 512],
                    probs[:, (0 * 6 + ci) * 512: (0 * 6 + ci + 1) * 512],
                    maskb[:, (mset * 6 + ci) * 512: (mset * 6 + ci + 1) * 512]))
                P.step("v", vB, lambda v, ci=ci, mset=mset: v.tensor_mul(
                    probs[:, (1 * 6 + ci) * 512: (1 * 6 + ci + 1) * 512],
                    probs[:, (1 * 6 + ci) * 512: (1 * 6 + ci + 1) * 512],
                    maskb[:, (mset * 6 + ci) * 512: (mset * 6 + ci + 1) * 512]))
                chunk_g += 1

            P.wait("t", vB, P.n(vB))
            P.wait("t", vC, P.n(vC))
            pavq = pav[qt % 2]
            for par in range(2):
                for ci in range(6):
                    vc = 4 * qt + ci
                    P.emit("t", lambda t, pavq=pavq, par=par, ci=ci, vc=vc, hp=hp: t.matmul(
                        pavq[64 * par:64 * par + 64, :],
                        vbuf[:, vc * 1024 + hp * 128 + par * 64: vc * 1024 + hp * 128 + par * 64 + 64],
                        probs[:, (par * 6 + ci) * 512: (par * 6 + ci + 1) * 512],
                        start=(ci == 0), stop=(ci == 5)))
            for par in range(2):
                plp = plp_e if par == 0 else plp_o
                for ci in range(6):
                    fn = lambda t, plp=plp, par=par, ci=ci: t.matmul(
                        plp[0:1, :], onecol[:, 0:1],
                        probs[:, (par * 6 + ci) * 512: (par * 6 + ci + 1) * 512],
                        start=(ci == 0), stop=(ci == 5))
                    if par == 1 and ci == 5:
                        P.step("t", tD, fn)
                    else:
                        P.emit("t", fn)
            P.wait("sc", tD, P.n(tD))
            P.step("sc", vC, lambda sc: sc.activation(
                out=linv[0:1, 0:512], in_=plp_e[0:1, :], func=AF.Ln))
            P.step("sc", vC, lambda sc: sc.activation(
                out=linv[0:1, 512:1024], in_=plp_o[0:1, :], func=AF.Ln))
            P.wait("sc", vC, P.n(vC))
            P.step("sc", vC, lambda sc: sc.activation(
                out=linv[0:1, 0:512], in_=linv[0:1, 0:512], func=AF.Exp, scale=-1.0))
            P.step("sc", vC, lambda sc: sc.activation(
                out=linv[0:1, 512:1024], in_=linv[0:1, 512:1024], func=AF.Exp, scale=-1.0))
            P.wait("t", vC, P.n(vC))
            P.wait("t", scC, P.n(scC))
            P.emit("t", lambda t: t.matmul(plb[:, :], selb[0:1, 0:128], linv[0:1, 0:512],
                                           start=True, stop=False))
            P.step("t", tE, lambda t: t.matmul(plb[:, :], selb[0:1, 128:256], linv[0:1, 512:1024],
                                               start=False, stop=True))
            P.wait("sc", tE, P.n(tE))
            P.step("sc", scC, lambda sc: sc.copy(out=lbs[:, :], in_=plb[:, :]))
            P.wait("v", scC, P.n(scC))
            P.step("v", vD, lambda v, pavq=pavq, hp=hp, qt=qt: v.tensor_mul(
                attnf[:, hp * ROWS + 512 * qt: hp * ROWS + 512 * qt + 512],
                pavq[:, :], lbs[:, :]))

    A.restore()
    zs = A.f32(8 * ROWS)
    woutb = [A.bf16(8 * 128) for _ in range(2)]
    x0t = [A.f32(512) for _ in range(2)]
    zb = A.bf16(512)
    zsq = A.bf16(512)
    stat_rows = A.f32(5 * 512)
    xAt = [A.f32(512) for _ in range(2)]

    all_attn = P.n(vD)
    P.wait("t", vD, all_attn)
    for e in ("s", "g", "v", "sc"):
        for sem in (vA, vB, vC, vD, vS, scA, scB, scC, scV):
            P.wait(e, sem)
    zi = 0
    for oc in range(8):
        if oc >= 2:
            P.wait("g", tF, 4 * (oc - 1))
        P.step("g", gl_wout[oc % 2], lambda g, oc=oc: g.dma_start(
            out=woutb[oc % 2].rearrange("p (c o) -> p c o", c=8),
            in_=dram["p0_woutT"][oc * 1024:(oc + 1) * 1024, :].rearrange("(c p) o -> p c o", p=128)), 16)
        P.wait("t", gl_wout[oc % 2])
        for qt in range(4):
            par = zi % 2
            if zi >= 2:
                P.wait("s", vE, zi - 1)
            P.step("s", sl_x0[par], lambda s, oc=oc, qt=qt, par=par: s.dma_start(
                out=x0t[par][:, :],
                in_=dram["p0_xT"][oc * 128:(oc + 1) * 128, HALO + 512 * qt: HALO + 512 * qt + 512]), 16)
            if zi >= 1:
                P.wait("t", vE, zi)
            for ic in range(8):
                fn = lambda t, ic=ic, qt=qt, oc=oc: t.matmul(
                    pz[:, :], woutb[oc % 2][:, ic * 128:(ic + 1) * 128],
                    attnf[:, ic * ROWS + 512 * qt: ic * ROWS + 512 * qt + 512],
                    start=(ic == 0), stop=(ic == 7))
                if ic == 7:
                    P.step("t", tF, fn)
                else:
                    P.emit("t", fn)
            P.wait("v", tF, P.n(tF))
            P.wait("v", sl_x0[par], P.n(sl_x0[par]))
            P.step("v", vE, lambda v, oc=oc, qt=qt, par=par: v.tensor_add(
                zs[:, oc * ROWS + 512 * qt: oc * ROWS + 512 * qt + 512],
                pz[:, :], x0t[par][:, :]))
            zi += 1


    z_done = P.n(vE)
    for e in ("t", "sc", "v"):
        P.wait(e, vE, z_done)
    ap_i = 0
    for qt in range(4):
        for oc in range(8):
            if oc >= 1:
                P.wait("v", tZ, P.n(tZ))
                P.wait("sc", tZ, P.n(tZ))
            P.step("v", vF, lambda v, oc=oc, qt=qt: v.tensor_copy(
                zb[:, :], zs[:, oc * ROWS + 512 * qt: oc * ROWS + 512 * qt + 512]))
            P.step("sc", scD, lambda sc, oc=oc, qt=qt: sc.activation(
                out=zsq[:, :], in_=zs[:, oc * ROWS + 512 * qt: oc * ROWS + 512 * qt + 512],
                func=AF.Square))
            P.wait("t", vF, P.n(vF))
            P.wait("t", scD, P.n(scD))
            P.emit("t", lambda t, oc=oc: t.matmul(
                pst_m[0:1, :], statcol[:, 0:1], zb[:, :], start=(oc == 0), stop=(oc == 7)))
            P.step("t", tZ, lambda t, oc=oc: t.matmul(
                pst_q[0:1, :], statcol[:, 0:1], zsq[:, :], start=(oc == 0), stop=(oc == 7)))
        P.wait("sc", tZ, 8 * (qt + 1))
        P.wait("sc", vG, P.n(vG))
        P.step("sc", scE, lambda sc: sc.copy(out=stat_rows[0:1, 0:512], in_=pst_m[0:1, :]))
        P.step("sc", scE, lambda sc: sc.activation(
            out=stat_rows[0:1, 512:1024], in_=pst_m[0:1, :], func=AF.Square))
        P.wait("v", scE, P.n(scE))
        P.step("v", vG, lambda v: v.tensor_sub(
            stat_rows[0:1, 1024:1536], pst_q[0:1, :], stat_rows[0:1, 512:1024]))
        P.wait("sc", vG, P.n(vG))
        P.step("sc", scS, lambda sc: sc.activation(
            out=stat_rows[0:1, 1536:2048], in_=stat_rows[0:1, 1024:1536],
            func=AF.Ln, bias=epsr[0:1, 0:1]))
        P.wait("sc", scS, P.n(scS))
        P.step("sc", scE, lambda sc: sc.activation(
            out=stat_rows[0:1, 1536:2048], in_=stat_rows[0:1, 1536:2048],
            func=AF.Exp, scale=-0.5))
        P.wait("v", scE, P.n(scE))
        P.step("v", vG, lambda v: v.tensor_mul(
            stat_rows[0:1, 2048:2560], stat_rows[0:1, 0:512], stat_rows[0:1, 1536:2048]))
        P.wait("t", vH, P.n(vH))
        P.wait("t", vG, P.n(vG))
        P.emit("t", lambda t: t.matmul(pbr[:, 0:512], onesrow[0:1, :],
                                       stat_rows[0:1, 1536:2048], start=True, stop=True))
        P.step("t", tH, lambda t: t.matmul(pbr[:, 512:1024], onesrow[0:1, :],
                                           stat_rows[0:1, 2048:2560], start=True, stop=True))
        P.wait("v", tH, P.n(tH))
        for oc in range(8):
            par = ap_i % 2
            if ap_i >= 2:
                P.wait("v", so_p[par], P.n(so_p[par]))
            P.step("v", vS, lambda v, oc=oc, qt=qt, par=par: v.tensor_mul(
                xAt[par][:, :], zs[:, oc * ROWS + 512 * qt: oc * ROWS + 512 * qt + 512],
                pbr[:, 0:512]))
            P.wait("v", vS, P.n(vS))
            P.step("v", vH, lambda v, par=par: v.tensor_sub(
                xAt[par][:, :], xAt[par][:, :], pbr[:, 512:1024]))
            P.wait("s", vH, P.n(vH))
            P.step("s", so_p[par], lambda s, oc=oc, qt=qt, par=par: s.dma_start(
                out=out_xA[oc * 128:(oc + 1) * 128, 512 * qt:512 * qt + 512],
                in_=xAt[par][:, :]), 16)
            ap_i += 1
    P.wait("s", so_p[0], P.n(so_p[0]))
    P.wait("s", so_p[1], P.n(so_p[1]))
    P.emit("s", lambda s: s.nop())
    return [so_p[0], so_p[1]]


def build_p0():
    nc = bass.Bass("TRN2", target_bir_lowering=False)
    dram = {
        "p0_xT": nc.dram_tensor("p0_xT", [D, TK], F32, kind="ExternalInput").ap(),
        "p0_wqkT": nc.dram_tensor("p0_wqkT", [16 * D, 128], F32, kind="ExternalInput").ap(),
        "p0_wvT": nc.dram_tensor("p0_wvT", [D, 1024], F32, kind="ExternalInput").ap(),
        "p0_woutT": nc.dram_tensor("p0_woutT", [8 * D, 128], F32, kind="ExternalInput").ap(),
        "p0_cos": nc.dram_tensor("p0_cos", [128, TK], F32, kind="ExternalInput").ap(),
        "p0_sin": nc.dram_tensor("p0_sin", [128, TK], F32, kind="ExternalInput").ap(),
        "p0_perm": nc.dram_tensor("p0_perm", [128, 128], BF16, kind="ExternalInput").ap(),
        "p0_mask": nc.dram_tensor("p0_mask", [128, 12 * 512], BF16, kind="ExternalInput").ap(),
        "c_sel": nc.dram_tensor("c_sel", [1, 256], F32, kind="ExternalInput").ap(),
        "c_ones_row": nc.dram_tensor("c_ones_row", [1, 128], F32, kind="ExternalInput").ap(),
        "c_statcol": nc.dram_tensor("c_statcol", [128, 1], BF16, kind="ExternalInput").ap(),
        "c_onecol": nc.dram_tensor("c_onecol", [128, 1], BF16, kind="ExternalInput").ap(),
        "c_eps": nc.dram_tensor("c_eps", [1, 1], F32, kind="ExternalInput").ap(),
    }
    out_xA = nc.dram_tensor("p0_xA", [D, ROWS], F32, kind="ExternalOutput").ap()
    stack = ExitStack()
    with stack:
        A = Arena(nc, stack, "arena", 104448)
        pool_t = stack.enter_context(nc.psum_tensor("pool", [128, 4096], F32))
        P = Plan(nc, stack)
        emit_p0(P, nc, A, pool_t, dram, out_xA)
        block = stack.enter_context(nc.Block())
        P.realize(block)
    return nc


def run_p0(inputs):
    maps = stage_p0(inputs)
    nc = build_p0()
    res = run_bass_kernel_spmd(nc, maps, core_ids=list(range(NCORE)))
    return [res.results[c]["p0_xA"] for c in range(NCORE)]



def emit_ln_fm(P, name, pool, bufs, n_sub, zsrc, sink, w=512, gb=None, sems=None):
    if sems is None:
        sems = {}
    def S(nm):
        if nm not in sems:
            sems[nm] = P.new_sem(name + nm)
        return sems[nm]
    tZ, tH, scE, scS2 = S("tZ"), S("tH"), S("scE"), S("scS2")
    vF, vG, vAp, vH = S("vF"), S("vG"), S("vAp"), S("vH")
    scGB = S("scGB") if gb is not None else None
    pst_m = pool[0:1, 0:w]
    pst_q = pool[0:1, 512:512 + w]
    pbr = pool[:, 1024:2048]
    zb, zsq, statr = bufs["zb"], bufs["zsq"], bufs["statr"]
    xnt = bufs["xnt"]
    statcol, onesrow, epsr = bufs["statcol"], bufs["onesrow"], bufs["epsr"]
    guards = sems.setdefault("_guards", [None, None])
    for sub in range(n_sub):
        for oc in range(8):
            if P.n(tZ) > 0:
                P.wait("v", tZ, P.n(tZ))
                P.wait("sc", tZ, P.n(tZ))
            P.step("v", vF, lambda v, oc=oc, sub=sub: v.tensor_copy(
                zb[:, 0:w], zsrc(oc, sub)))
            P.step("sc", scE, lambda sc, oc=oc, sub=sub: sc.activation(
                out=zsq[:, 0:w], in_=zsrc(oc, sub), func=AF.Square))
            P.wait("t", vF, P.n(vF))
            P.wait("t", scE, P.n(scE))
            P.emit("t", lambda t, oc=oc: t.matmul(
                pst_m[0:1, :], statcol[:, 0:1], zb[:, 0:w], start=(oc == 0), stop=(oc == 7)))
            P.step("t", tZ, lambda t, oc=oc: t.matmul(
                pst_q[0:1, :], statcol[:, 0:1], zsq[:, 0:w], start=(oc == 0), stop=(oc == 7)))
        P.wait("sc", tZ, P.n(tZ))
        P.wait("sc", vG, P.n(vG))
        P.step("sc", scE, lambda sc: sc.copy(out=statr[0:1, 0:w], in_=pst_m[0:1, :]))
        P.step("sc", scE, lambda sc: sc.activation(
            out=statr[0:1, 512:512 + w], in_=pst_m[0:1, :], func=AF.Square))
        P.wait("v", scE, P.n(scE))
        P.step("v", vG, lambda v: v.tensor_sub(
            statr[0:1, 1024:1024 + w], pst_q[0:1, :], statr[0:1, 512:512 + w]))
        P.wait("sc", vG, P.n(vG))
        P.step("sc", scS2, lambda sc: sc.activation(
            out=statr[0:1, 1536:1536 + w], in_=statr[0:1, 1024:1024 + w],
            func=AF.Ln, bias=epsr[0:1, 0:1]))
        P.wait("sc", scS2, P.n(scS2))
        P.step("sc", scE, lambda sc: sc.activation(
            out=statr[0:1, 1536:1536 + w], in_=statr[0:1, 1536:1536 + w],
            func=AF.Exp, scale=-0.5))
        P.wait("v", scE, P.n(scE))
        P.step("v", vG, lambda v: v.tensor_mul(
            statr[0:1, 2048:2048 + w], statr[0:1, 0:w], statr[0:1, 1536:1536 + w]))
        P.wait("t", vH, P.n(vH))
        if scGB is not None:
            P.wait("t", scGB, P.n(scGB))
        P.wait("t", vG, P.n(vG))
        P.emit("t", lambda t: t.matmul(pbr[:, 0:w], onesrow[0:1, :],
                                       statr[0:1, 1536:1536 + w], start=True, stop=True))
        P.step("t", tH, lambda t: t.matmul(pbr[:, 512:512 + w], onesrow[0:1, :],
                                           statr[0:1, 2048:2048 + w], start=True, stop=True))
        P.wait("v", tH, P.n(tH))
        for oc in range(8):
            par = sems.get("_appar", 0)
            sems["_appar"] = 1 - par
            if guards[par]:
                for (sname, v_) in guards[par]:
                    P.wait("v", sname, v_)
            P.step("v", vAp, lambda v, oc=oc, sub=sub, par=par: v.tensor_mul(
                xnt[par][:, 0:w], zsrc(oc, sub), pbr[:, 0:w]))
            P.wait("v", vAp, P.n(vAp))
            P.step("v", vH, lambda v, par=par: v.tensor_sub(
                xnt[par][:, 0:w], xnt[par][:, 0:w], pbr[:, 512:512 + w]))
            if gb is not None:
                gamma_ap, beta_ap = gb
                P.wait("sc", vH, P.n(vH))
                P.step("sc", scGB, lambda sc, oc=oc, par=par: sc.activation(
                    out=bufs["gbout"][par][:, 0:w], in_=xnt[par][:, 0:w], func=AF.Copy,
                    scale=gamma_ap(oc), bias=beta_ap(oc)))
                extra = [(scGB, P.n(scGB))]
            else:
                extra = []
            P.wait("v", vH, P.n(vH))
            guards[par] = extra + sink(sub, oc, par)
    return sems


def stage_p1(inputs):
    w1t = _pretile2(np.ascontiguousarray(inputs["L0_ffn_w1_w"].T))
    w2t = _pretile2(np.ascontiguousarray(inputs["L0_ffn_w2_w"].T))
    h1t = _pretile2(np.ascontiguousarray(inputs["L0_ffn_h1_w"].T))
    h2t = _pretile2(np.ascontiguousarray(inputs["L0_ffn_h2_w"].T))
    w3t = _pretile2(np.ascontiguousarray(inputs["L0_ffn_w3_w"].T))
    s1 = np.ascontiguousarray(inputs["L0_ffn_w1_s"].reshape(32, 128).T)
    s2 = np.ascontiguousarray(inputs["L0_ffn_w2_s"].reshape(32, 128).T)
    h1s = np.ascontiguousarray(inputs["L0_ffn_h1_s"].reshape(32, 128).T)
    h2s = np.ascontiguousarray(inputs["L0_ffn_h2_s"].reshape(32, 128).T)
    s3 = np.ascontiguousarray(inputs["L0_ffn_w3_s"].reshape(8, 128).T)
    wkt = _pretile2(np.ascontiguousarray(inputs["L1_attn_Wk"]))
    wv = np.ascontiguousarray(inputs["L1_attn_Wv"])
    shared = {
        "p1_w1t": w1t, "p1_w2t": w2t, "p1_h1t": h1t, "p1_h2t": h2t, "p1_w3t": w3t,
        "p1_s1": s1, "p1_s2": s2, "p1_h1s": h1s, "p1_h2s": h2s, "p1_s3": s3,
        "p1_wkt": wkt, "p1_wv": wv,
        "c_ones_row": np.ones((1, 128), dtype=np.float32),
        "c_statcol": np.full((128, 1), 1.0 / D, dtype=np.float32).astype(BF),
        "c_eps": np.full((1, 1), 1e-5, dtype=np.float32),
    }
    return [dict(shared) for _ in range(NCORE)]


def p1_dram_decls(nc):
    return {
        "p1_w1t": nc.dram_tensor("p1_w1t", [32 * 1024, 128], F32, kind="ExternalInput").ap(),
        "p1_w2t": nc.dram_tensor("p1_w2t", [32 * 1024, 128], F32, kind="ExternalInput").ap(),
        "p1_h1t": nc.dram_tensor("p1_h1t", [32 * 4096, 128], F32, kind="ExternalInput").ap(),
        "p1_h2t": nc.dram_tensor("p1_h2t", [32 * 4096, 128], F32, kind="ExternalInput").ap(),
        "p1_w3t": nc.dram_tensor("p1_w3t", [8 * 4096, 128], F32, kind="ExternalInput").ap(),
        "p1_s1": nc.dram_tensor("p1_s1", [128, 32], F32, kind="ExternalInput").ap(),
        "p1_s2": nc.dram_tensor("p1_s2", [128, 32], F32, kind="ExternalInput").ap(),
        "p1_h1s": nc.dram_tensor("p1_h1s", [128, 32], F32, kind="ExternalInput").ap(),
        "p1_h2s": nc.dram_tensor("p1_h2s", [128, 32], F32, kind="ExternalInput").ap(),
        "p1_s3": nc.dram_tensor("p1_s3", [128, 8], F32, kind="ExternalInput").ap(),
        "p1_wkt": nc.dram_tensor("p1_wkt", [8 * 1024, 128], F32, kind="ExternalInput").ap(),
        "p1_wv": nc.dram_tensor("p1_wv", [1024, 1024], F32, kind="ExternalInput").ap(),
    }


def emit_p1(P, nc, A, pool, dram, xa_src, out_xB, out_kf, out_vt, rt_done_cb=None):
    if not isinstance(out_xB, list):
        out_xB = [out_xB[:, 0:1024], out_xB[:, 1024:2048]]
        out_kf = [out_kf[:, 0:1024], out_kf[:, 1024:2048]]
        out_vt = [out_vt[0:1024, :], out_vt[1024:2048, :]]

    def slot(i, w=1):
        return pool[:, 512 * i:512 * (i + w)]

    gW = [P.new_dma_sem("p1gW0", "g"), P.new_dma_sem("p1gW1", "g")]
    gH = [P.new_dma_sem("p1gH0", "g"), P.new_dma_sem("p1gH1", "g")]
    gW3 = [P.new_dma_sem("p1gW30", "g"), P.new_dma_sem("p1gW31", "g")]
    gK = [P.new_dma_sem("p1gK0", "g"), P.new_dma_sem("p1gK1", "g")]
    gV = P.new_dma_sem("p1gV", "g")
    gX = P.new_dma_sem("p1gX", "g")
    sX = [P.new_dma_sem("p1sX0", "s"), P.new_dma_sem("p1sX1", "s")]
    soX = [P.new_dma_sem("p1soX0", "s"), P.new_dma_sem("p1soX1", "s")]
    soK = [P.new_dma_sem("p1soK0", "s"), P.new_dma_sem("p1soK1", "s")]
    soV = [P.new_dma_sem("p1soV0", "s"), P.new_dma_sem("p1soV1", "s")]
    tA1 = P.new_sem("p1tA1")
    tA2 = P.new_sem("p1tA2")
    tA3 = P.new_sem("p1tA3")
    tA4 = P.new_sem("p1tA4")
    tK = P.new_sem("p1tK")
    tV2 = P.new_sem("p1tV2")
    sc1 = P.new_sem("p1sc1")
    sc2 = P.new_sem("p1sc2")
    sc3 = P.new_sem("p1sc3")
    scK = P.new_sem("p1scK")
    scV2 = P.new_sem("p1scV2")
    v1 = P.new_sem("p1v1")
    v3 = P.new_sem("p1v3")
    vKc = P.new_sem("p1vKc")

    xabf = A.bf16(8 * 1024)
    h0b = A.bf16(32 * 1024)
    h1b = A.bf16(32 * 1024)
    wbuf = [[A.bf16(8 * 128), A.bf16(8 * 128)] for _ in range(2)]
    hbuf = [A.bf16(32 * 128) for _ in range(2)]
    w3buf = hbuf
    wkbuf = [A.bf16(8 * 128) for _ in range(2)]
    wvb8 = h1b[:, 16384:24576]
    tsil = [A.f32(1024) for _ in range(2)]
    xat = [A.f32(1024) for _ in range(2)]
    zbuf = h1b[:, 0:16384].bitcast(F32)
    xbbf = xabf
    kft = [A.bf16(1024) for _ in range(2)]
    vtt = [A.bf16(1024) for _ in range(2)]
    s1b = A.f32(32)
    s2b = A.f32(32)
    h1sb = A.f32(32)
    h2sb = A.f32(32)
    s3b = A.f32(8)
    zb = h0b[:, 0:512]
    zsq = h0b[:, 512:1024]
    statr = h0b[:, 1024:6144].bitcast(F32)
    xnt = [h0b[:, 6144:7168].bitcast(F32), h0b[:, 7168:8192].bitcast(F32)]
    statcol = A.bf16(2)
    onesrow = A.f32(128)
    epsr = A.f32(1)

    sB = P.new_dma_sem("p1sB", "s")
    for name, buf, w in (("p1_s1", s1b, 32), ("p1_s2", s2b, 32), ("p1_h1s", h1sb, 32),
                         ("p1_h2s", h2sb, 32), ("p1_s3", s3b, 8)):
        P.step("s", sB, lambda s, name=name, buf=buf, w=w: s.dma_start(
            out=buf[:, 0:w], in_=dram[name][:, :]), 16)
    P.step("s", sB, lambda s: s.dma_start(out=onesrow[0:1, :], in_=dram["c_ones_row"][:, :]), 16)
    P.step("s", sB, lambda s: s.dma_start(out=epsr[0:1, :], in_=dram["c_eps"][:, :]), 16)
    P.step("g", gV, lambda g: g.dma_start(out=statcol[:, 0:1], in_=dram["c_statcol"][:, :]), 16)
    for e in ("t", "v", "sc"):
        P.wait(e, sB)
        P.wait(e, gV)

    def slab_load(chan_sems, unit, bufap, src_ap, gate_sem, gate_val, nblk):
        par = unit % 2
        if gate_val > 0:
            P.wait("g", gate_sem, gate_val)
        P.step("g", chan_sems[par], lambda g, bufap=bufap, src_ap=src_ap, nblk=nblk: g.dma_start(
            out=bufap.rearrange("p (c o) -> p c o", c=nblk),
            in_=src_ap.rearrange("(c p) o -> p c o", p=128)), 16)
        return chan_sems[par], P.n(chan_sems[par])

    ln_guard_prev = None
    for rt in range(2):
        col0 = 1024 * rt
        if rt == 1:
            P.wait("g", tA1, 32 * rt)
            P.wait("g", tV2, P.n(tV2))
            P.wait("g", scK, P.n(scK))
        for ic in range(8):
            P.step("g", gX, lambda g, ic=ic, col0=col0: g.dma_start(
                out=xabf[:, ic * 1024:(ic + 1) * 1024],
                in_=xa_src[ic * 128:(ic + 1) * 128, col0:col0 + 1024]), 16)
        P.wait("t", gX)

        if rt == 1:
            P.wait("v", sc3, P.n(sc3))
            P.wait("t", scV2, P.n(scV2))
            P.wait("t", scK, P.n(scK))
            for sem in (ln_guard_prev or []) + [vKc, soX[0], soX[1]]:
                P.wait("v", sem, P.n(sem))
                P.wait("sc", sem, P.n(sem))
        for ob in range(32):
            unit = rt * 32 + ob
            par = ob % 2
            slab_load(gW, unit, wbuf[par][0],
                      dram["p1_w1t"][ob * 1024:(ob + 1) * 1024, :], tA1, unit - 1, 8)
            P.step("g", gW[par], lambda g, par=par, ob=ob: g.dma_start(
                out=wbuf[par][1].rearrange("p (c o) -> p c o", c=8),
                in_=dram["p1_w2t"][ob * 1024:(ob + 1) * 1024, :].rearrange("(c p) o -> p c o", p=128)), 16)
            P.wait("t", gW[par], P.n(gW[par]))
            if unit >= 2:
                P.wait("t", v1, unit - 1)
            pp1 = slot(4 * par, 2)
            pp2 = slot(4 * par + 2, 2)
            for w_i, pp in ((0, pp1), (1, pp2)):
                for ic in range(8):
                    for h in range(2):
                        fn = lambda t, w_i=w_i, pp=pp, ic=ic, h=h, par=par: t.matmul(
                            pp[:, h * 512:(h + 1) * 512],
                            wbuf[par][w_i][:, ic * 128:(ic + 1) * 128],
                            xabf[:, ic * 1024 + h * 512: ic * 1024 + (h + 1) * 512],
                            start=(ic == 0), stop=(ic == 7))
                        if w_i == 1 and ic == 7 and h == 1:
                            P.step("t", tA1, fn)
                        else:
                            P.emit("t", fn)
            P.wait("sc", tA1, P.n(tA1))
            P.step("sc", sc1, lambda sc, par=par, ob=ob, pp1=pp1: sc.activation(
                out=tsil[par][:, :], in_=pp1[:, :], func=AF.Silu,
                scale=s1b[:, ob:ob + 1]))
            P.wait("v", sc1, P.n(sc1))
            P.step("v", v1, lambda v, par=par, ob=ob, pp2=pp2: v.scalar_tensor_tensor(
                h0b[:, ob * 1024:(ob + 1) * 1024], pp2[:, :], s2b[:, ob:ob + 1],
                tsil[par][:, :], ALU.mult, ALU.mult))

        P.wait("t", v1, P.n(v1))
        for ob in range(32):
            unit = rt * 32 + ob
            par = ob % 2
            if ob < 2 and rt == 1:
                P.wait("g", tA3, P.n(tA3))
                _, wv_ = slab_load(gH, unit, hbuf[par],
                                   dram["p1_h1t"][ob * 4096:(ob + 1) * 4096, :], tA2, 0, 32)
            else:
                _, wv_ = slab_load(gH, unit, hbuf[par],
                                   dram["p1_h1t"][ob * 4096:(ob + 1) * 4096, :], tA2, unit - 1, 32)
            P.wait("t", gH[par], wv_)
            if unit >= 2:
                P.wait("t", sc2, unit - 1)
            ph = slot(4 * (par % 2), 2)
            for ic in range(32):
                for h in range(2):
                    fn = lambda t, ph=ph, ic=ic, h=h, par=par: t.matmul(
                        ph[:, h * 512:(h + 1) * 512],
                        hbuf[par][:, ic * 128:(ic + 1) * 128],
                        h0b[:, ic * 1024 + h * 512: ic * 1024 + (h + 1) * 512],
                        start=(ic == 0), stop=(ic == 31))
                    if ic == 31 and h == 1:
                        P.step("t", tA2, fn)
                    else:
                        P.emit("t", fn)
            P.wait("sc", tA2, P.n(tA2))
            P.step("sc", sc2, lambda sc, ob=ob, ph=ph: sc.activation(
                out=h1b[:, ob * 1024:(ob + 1) * 1024], in_=ph[:, :], func=AF.Gelu,
                scale=h1sb[:, ob:ob + 1]))

        P.wait("t", sc2, P.n(sc2))
        for ob in range(32):
            unit = rt * 32 + ob
            par = ob % 2
            if ob < 2:
                P.wait("g", tA2, P.n(tA2))
                _, wv_ = slab_load(gH, 64 + unit, hbuf[par],
                                   dram["p1_h2t"][ob * 4096:(ob + 1) * 4096, :], tA3, 0, 32)
            else:
                _, wv_ = slab_load(gH, 64 + unit, hbuf[par],
                                   dram["p1_h2t"][ob * 4096:(ob + 1) * 4096, :], tA3, unit - 1, 32)
            P.wait("t", gH[par], wv_)
            if unit >= 2:
                P.wait("t", sc3, unit - 1)
            ph = slot(4 * (par % 2), 2)
            for ic in range(32):
                for h in range(2):
                    fn = lambda t, ph=ph, ic=ic, h=h, par=par: t.matmul(
                        ph[:, h * 512:(h + 1) * 512],
                        hbuf[par][:, ic * 128:(ic + 1) * 128],
                        h1b[:, ic * 1024 + h * 512: ic * 1024 + (h + 1) * 512],
                        start=(ic == 0), stop=(ic == 31))
                    if ic == 31 and h == 1:
                        P.step("t", tA3, fn)
                    else:
                        P.emit("t", fn)
            P.wait("sc", tA3, P.n(tA3))
            P.step("sc", sc3, lambda sc, ob=ob, ph=ph: sc.activation(
                out=h0b[:, ob * 1024:(ob + 1) * 1024], in_=ph[:, :], func=AF.Gelu,
                scale=h2sb[:, ob:ob + 1]))

        P.wait("t", sc3, P.n(sc3))
        for ob in range(8):
            unit = rt * 8 + ob
            par = ob % 2
            if ob < 2:
                P.wait("g", tA3, P.n(tA3))
                _, wv3_ = slab_load(gW3, unit, w3buf[par],
                                    dram["p1_w3t"][ob * 4096:(ob + 1) * 4096, :], tA4, 0, 32)
            else:
                _, wv3_ = slab_load(gW3, unit, w3buf[par],
                                    dram["p1_w3t"][ob * 4096:(ob + 1) * 4096, :], tA4, unit - 1, 32)
            if unit >= 2:
                P.wait("s", v3, unit - 1)
            P.step("s", sX[par], lambda s, ob=ob, par=par, col0=col0: s.dma_start(
                out=xat[par][:, :], in_=xa_src[ob * 128:(ob + 1) * 128, col0:col0 + 1024]), 16)
            P.wait("t", gW3[par], wv3_)
            if unit >= 2:
                P.wait("t", v3, unit - 1)
            py = slot(4 * (par % 2), 2)
            for ic in range(32):
                for h in range(2):
                    fn = lambda t, py=py, ic=ic, h=h, par=par: t.matmul(
                        py[:, h * 512:(h + 1) * 512],
                        w3buf[par][:, ic * 128:(ic + 1) * 128],
                        h0b[:, ic * 1024 + h * 512: ic * 1024 + (h + 1) * 512],
                        start=(ic == 0), stop=(ic == 31))
                    if ic == 31 and h == 1:
                        P.step("t", tA4, fn)
                    else:
                        P.emit("t", fn)
            P.wait("v", tA4, P.n(tA4))
            P.wait("v", sX[par], P.n(sX[par]))
            P.step("v", v3, lambda v, ob=ob, par=par, py=py: v.scalar_tensor_tensor(
                zbuf[:, ob * 1024:(ob + 1) * 1024], py[:, :], s3b[:, ob:ob + 1],
                xat[par][:, :], ALU.mult, ALU.add))

        P.wait("t", v3, P.n(v3))
        P.wait("sc", v3, P.n(v3))
        P.wait("v", v3, P.n(v3))

        st_i = [0]

        def sink(sub, oc, par, rt=rt, st_i=st_i):
            P.step("v", vKc, lambda v, sub=sub, oc=oc, par=par: v.tensor_copy(
                xbbf[:, oc * 1024 + sub * 512: oc * 1024 + sub * 512 + 512],
                xnt[par][:, :]))
            P.wait("s", vKc, P.n(vKc))
            spar = st_i[0] % 2
            P.step("s", soX[spar], lambda s, sub=sub, oc=oc, par=par, rt=rt: s.dma_start(
                out=out_xB[rt][oc * 128:(oc + 1) * 128, sub * 512: sub * 512 + 512],
                in_=xnt[par][:, :]), 16)
            st_i[0] += 1
            return [(vKc, P.n(vKc)), (soX[spar], P.n(soX[spar]))]

        _lnsems = emit_ln_fm(P, f"p1ln{rt}", pool, {
            "zb": zb, "zsq": zsq, "statr": statr, "xnt": xnt,
            "statcol": statcol, "onesrow": onesrow, "epsr": epsr,
        }, 2, lambda oc, sub: zbuf[:, oc * 1024 + sub * 512: oc * 1024 + sub * 512 + 512],
            sink)
        ln_guard_prev = [v for v in _lnsems.values() if isinstance(v, str)]

        P.wait("g", tA3, P.n(tA3))
        for ic in range(8):
            P.step("g", gV, lambda g, ic=ic: g.dma_start(
                out=wvb8[:, ic * 1024:(ic + 1) * 1024],
                in_=dram["p1_wv"][ic * 128:(ic + 1) * 128, :]), 16)
        P.wait("t", gV)
        P.wait("t", vKc, P.n(vKc))
        for ob in range(8):
            unit = rt * 8 + ob
            par = ob % 2
            _, wk_ = slab_load(gK, unit, wkbuf[par],
                               dram["p1_wkt"][ob * 1024:(ob + 1) * 1024, :], tK, unit - 1, 8)
            P.wait("t", gK[par], wk_)
            if unit >= 2:
                P.wait("t", scK, unit - 1)
            pk = slot(4 * (par % 2), 2)
            for ic in range(8):
                for h in range(2):
                    fn = lambda t, pk=pk, ic=ic, h=h, par=par: t.matmul(
                        pk[:, h * 512:(h + 1) * 512],
                        wkbuf[par][:, ic * 128:(ic + 1) * 128],
                        xbbf[:, ic * 1024 + h * 512: ic * 1024 + (h + 1) * 512],
                        start=(ic == 0), stop=(ic == 7))
                    if ic == 7 and h == 1:
                        P.step("t", tK, fn)
                    else:
                        P.emit("t", fn)
            P.wait("sc", tK, P.n(tK))
            if unit >= 2:
                P.wait("sc", soK[par], P.n(soK[par]))
            P.step("sc", scK, lambda sc, par=par, pk=pk: sc.copy(
                out=kft[par][:, :], in_=pk[:, :]))
            P.wait("s", scK, P.n(scK))
            P.step("s", soK[par], lambda s, ob=ob, par=par, rt=rt: s.dma_start(
                out=out_kf[rt][ob * 128:(ob + 1) * 128, :],
                in_=kft[par][:, :]), 16)
        P.wait("t", scK, P.n(scK))
        for tc in range(8):
            unit = rt * 8 + tc
            par = tc % 2
            if unit >= 2:
                P.wait("t", scV2, unit - 1)
            pvv = slot(4 * par, 2)
            for ic in range(8):
                for h in range(2):
                    fn = lambda t, pvv=pvv, ic=ic, h=h, tc=tc: t.matmul(
                        pvv[:, h * 512:(h + 1) * 512],
                        xbbf[:, ic * 1024 + tc * 128: ic * 1024 + tc * 128 + 128],
                        wvb8[:, ic * 1024 + h * 512: ic * 1024 + (h + 1) * 512],
                        start=(ic == 0), stop=(ic == 7))
                    if ic == 7 and h == 1:
                        P.step("t", tV2, fn)
                    else:
                        P.emit("t", fn)
            P.wait("sc", tV2, P.n(tV2))
            if unit >= 2:
                P.wait("sc", soV[par], P.n(soV[par]))
            P.step("sc", scV2, lambda sc, par=par, pvv=pvv: sc.copy(
                out=vtt[par][:, :], in_=pvv[:, :]))
            P.wait("s", scV2, P.n(scV2))
            P.step("s", soV[par], lambda s, tc=tc, par=par, rt=rt: s.dma_start(
                out=out_vt[rt][tc * 128: tc * 128 + 128, :],
                in_=vtt[par][:, :]), 16)
        if rt_done_cb is not None:
            rt_done_cb(rt, [soX[0], soX[1], soK[0], soK[1], soV[0], soV[1]])

    for sem in (soX[0], soX[1], soK[0], soK[1], soV[0], soV[1]):
        P.wait("s", sem, P.n(sem))
    P.emit("s", lambda s: s.nop())
    return [soX[0], soX[1], soK[0], soK[1], soV[0], soV[1]]



def build_p1():
    nc = bass.Bass("TRN2", target_bir_lowering=False)
    dram = p1_dram_decls(nc)
    dram["p1_xA"] = nc.dram_tensor("p1_xA", [D, ROWS], F32, kind="ExternalInput").ap()
    dram["c_ones_row"] = nc.dram_tensor("c_ones_row", [1, 128], F32, kind="ExternalInput").ap()
    dram["c_statcol"] = nc.dram_tensor("c_statcol", [128, 1], BF16, kind="ExternalInput").ap()
    dram["c_eps"] = nc.dram_tensor("c_eps", [1, 1], F32, kind="ExternalInput").ap()
    out_xB = nc.dram_tensor("p1_xB", [D, ROWS], F32, kind="ExternalOutput").ap()
    out_kf = nc.dram_tensor("p1_kf", [D, ROWS], BF16, kind="ExternalOutput").ap()
    out_vt = nc.dram_tensor("p1_vt", [ROWS, D], BF16, kind="ExternalOutput").ap()
    stack = ExitStack()
    with stack:
        A = Arena(nc, stack, "arena", 104448)
        pool_t = stack.enter_context(nc.psum_tensor("pool", [128, 4096], F32))
        P = Plan(nc, stack)
        emit_p1(P, nc, A, pool_t, dram, dram["p1_xA"], out_xB, out_kf, out_vt)
        block = stack.enter_context(nc.Block())
        P.realize(block)
    return nc


def run_p1(inputs, xA_shards):
    maps = stage_p1(inputs)
    for c in range(NCORE):
        maps[c]["p1_xA"] = np.ascontiguousarray(xA_shards[c])
    nc = build_p1()
    res = run_bass_kernel_spmd(nc, maps, core_ids=list(range(NCORE)))
    return ([res.results[c]["p1_xB"] for c in range(NCORE)],
            [res.results[c]["p1_kf"] for c in range(NCORE)],
            [res.results[c]["p1_vt"] for c in range(NCORE)])



def p3_perm_rows(cp):
    rows = []
    for m in range(8):
        for j in (8 * m + cp, 8 * m + 7 - cp):
            rows.extend(range(128 * j, 128 * j + 128))
    return np.array(rows)


def _p3_masks(cp):
    out = np.zeros((128, 64 * 128), dtype=np.float32)
    kl = np.arange(128)[:, None]
    ql = np.arange(128)[None, :]
    for m in range(8):
        for k in range(8):
            c = 8 * m + k
            j = 8 * m + cp if k < 4 else 8 * m + 7 - cp
            if c < j:
                v = np.ones((128, 128), dtype=np.float32)
            elif c == j:
                v = (ql >= kl).astype(np.float32)
            else:
                v = np.zeros((128, 128), dtype=np.float32)
            out[:, (m * 8 + k) * 128:(m * 8 + k + 1) * 128] = v
    return out.astype(BF)


def stage_p3(inputs):
    wqt = _pretile2(np.ascontiguousarray(inputs["L1_attn_Wq"]))
    wvt2 = _pretile2(np.ascontiguousarray(inputs["L1_attn_Wv"]))
    wfct = _pretile2(np.ascontiguousarray(inputs["L1_attn_Wfc"]))
    bfc = np.ascontiguousarray(inputs["L1_attn_bfc"].reshape(8, 128).T)
    gam = np.ascontiguousarray(inputs["L1_attn_gamma"].reshape(8, 128).T)
    bet = np.ascontiguousarray(inputs["L1_attn_beta"].reshape(8, 128).T)
    shared = {
        "p3_wqt": wqt, "p3_wvt": wvt2, "p3_wfct": wfct,
        "p3_bfc": bfc, "p3_gam": gam, "p3_bet": bet,
        "c_ones_row": np.ones((1, 128), dtype=np.float32),
        "c_statcol": np.full((128, 1), 1.0 / D, dtype=np.float32).astype(BF),
        "c_onecol": np.ones((128, 1), dtype=np.float32).astype(BF),
        "c_eps": np.full((1, 1), 1e-5, dtype=np.float32),
    }
    maps = []
    for c in range(NCORE):
        cp = c % 4
        m = dict(shared)
        m["p3_mask"] = _p3_masks(cp)
        maps.append(m)
    return maps


def p3_dram_decls(nc):
    d = {
        "p3_wqt": nc.dram_tensor("p3_wqt", [8 * 1024, 128], F32, kind="ExternalInput").ap(),
        "p3_wvt": nc.dram_tensor("p3_wvt", [8 * 1024, 128], F32, kind="ExternalInput").ap(),
        "p3_wfct": nc.dram_tensor("p3_wfct", [8 * 1024, 128], F32, kind="ExternalInput").ap(),
        "p3_bfc": nc.dram_tensor("p3_bfc", [128, 8], F32, kind="ExternalInput").ap(),
        "p3_gam": nc.dram_tensor("p3_gam", [128, 8], F32, kind="ExternalInput").ap(),
        "p3_bet": nc.dram_tensor("p3_bet", [128, 8], F32, kind="ExternalInput").ap(),
        "p3_mask": nc.dram_tensor("p3_mask", [128, 64 * 128], BF16, kind="ExternalInput").ap(),
    }
    return d


def emit_p3(P, nc, A, pool, dram, xbi_src, kf_src, vt_src, out_xC,
            fused=False, xb_ag=None, idx_tab=None, ag_waits=None):
    if fused:
        xb_view = [h.rearrange("a (n o) -> (a n) o", o=128) for h in xb_ag]

    def slot(i, w=1):
        return pool[:, 512 * i:512 * (i + w)]

    gB = P.new_dma_sem("p3gB", "g")
    sB = P.new_dma_sem("p3sB", "s")
    gX = P.new_dma_sem("p3gX", "g")
    gKV = [P.new_dma_sem(f"p3gKV{i}", "s") for i in range(8)]
    sxb = [P.new_dma_sem("p3sxb0", "g" if fused else "s"),
           P.new_dma_sem("p3sxb1", "g" if fused else "s")]
    soC = [P.new_dma_sem("p3soC0", "s"), P.new_dma_sem("p3soC1", "s")]
    tQ = P.new_sem("p3tQ")
    tS = P.new_sem("p3tS")
    tS2 = P.new_sem("p3tS2")
    tAv = P.new_sem("p3tAv")
    tLb = P.new_sem("p3tLb")
    tFc = P.new_sem("p3tFc")
    scQ = P.new_sem("p3scQ")
    scE2 = P.new_sem("p3scE2")
    scLb = P.new_sem("p3scLb")
    scFc = P.new_sem("p3scFc")
    vM = P.new_sem("p3vM")
    vC2 = P.new_sem("p3vC2")
    vZ = P.new_sem("p3vZ")
    vZf = P.new_sem("p3vZf")
    vZ2 = P.new_sem("p3vZ2")

    xbibf = A.bf16(8 * 2048)
    qf = A.bf16(8 * 2048)
    vf = A.bf16(8 * 2048)
    qwb = A.bf16(64 * 128)
    vwb = A.bf16(64 * 128)
    fwb = A.bf16(64 * 128)
    maskb = A.bf16(64 * 128)
    kvb = [vwb[:, i * 1024:(i + 1) * 1024] for i in range(8)]
    vvb = [qwb[:, i * 1024:(i + 1) * 1024] for i in range(8)]
    probs = [A.bf16(256) for _ in range(2)]
    linv = A.f32(256)
    linbs = A.f32(256)
    zslot = A.f32(8 * 256)
    z2slot = zslot
    lnb = A.bf16(8 * 256)
    xbt = [A.f32(256) for _ in range(2)]
    bfcb = A.f32(8)
    gamb = A.f32(8)
    betb = A.f32(8)
    zb = A.bf16(512)
    zsq = A.bf16(512)
    statr = A.f32(2304)
    xnt = [A.f32(512) for _ in range(2)]
    statcol = A.bf16(2)
    onecol = A.bf16(2)
    onesrow = A.f32(128)
    epsr = A.f32(1)

    for name, buf in (("p3_wqt", qwb), ("p3_wvt", vwb), ("p3_wfct", fwb)):
        for ob in range(8):
            P.step("g", gB, lambda g, name=name, buf=buf, ob=ob: g.dma_start(
                out=buf[:, ob * 1024:(ob + 1) * 1024].rearrange("p (c o) -> p c o", c=8),
                in_=dram[name][ob * 1024:(ob + 1) * 1024, :].rearrange("(c p) o -> p c o", p=128)), 16)
    P.step("g", gB, lambda g: g.dma_start(out=maskb[:, :], in_=dram["p3_mask"][:, :]), 16)
    P.step("g", gB, lambda g: g.dma_start(out=statcol[:, 0:1], in_=dram["c_statcol"][:, :]), 16)
    P.step("g", gB, lambda g: g.dma_start(out=onecol[:, 0:1], in_=dram["c_onecol"][:, :]), 16)
    if fused:
        idxg = A.bf16(256)
        idxgu = idxg.bitcast(U32)
        P.step("g", gX, lambda g: g.dma_start(out=idxgu[:, :], in_=idx_tab[:, :]), 16)
        P.wait("g", gX, P.n(gX))
        if ag_waits:
            for s_, v_ in ag_waits["xb"]:
                P.wait("g", s_, v_)
        for ic in range(8):
            for s_ in range(16):
                hh_ = (s_ // 2) % 2
                P.step("g", gX, lambda g, ic=ic, s_=s_, hh_=hh_: g.indirect_dma_start(
                    out=xbibf[:, ic * 2048 + s_ * 128: ic * 2048 + s_ * 128 + 128],
                    out_offset=None, in_=xb_view[hh_][:, :],
                    in_offset=IndirectOffsetOnAxis(ap=idxgu[:, (s_ * 8 + ic):(s_ * 8 + ic) + 1], axis=0)), 16)
    else:
        for ic in range(8):
            P.step("g", gX, lambda g, ic=ic: g.dma_start(
                out=xbibf[:, ic * 2048:(ic + 1) * 2048],
                in_=xbi_src[ic * 128:(ic + 1) * 128, :]), 16)
    for name, buf in (("p3_bfc", bfcb), ("p3_gam", gamb), ("p3_bet", betb)):
        P.step("s", sB, lambda s, name=name, buf=buf: s.dma_start(
            out=buf[:, 0:8], in_=dram[name][:, :]), 16)
    P.step("s", sB, lambda s: s.dma_start(out=onesrow[0:1, :], in_=dram["c_ones_row"][:, :]), 16)
    P.step("s", sB, lambda s: s.dma_start(out=epsr[0:1, :], in_=dram["c_eps"][:, :]), 16)
    for e in ("t", "v", "sc"):
        P.wait(e, gB)
        P.wait(e, gX)
        P.wait(e, sB)

    for proj, wb, dst, dstf in (("q", qwb, qf, None), ("v", vwb, None, vf)):
        for ob in range(8):
            for tt in range(4):
                unit = P.n(tQ)
                par = unit % 2
                if unit >= 2:
                    P.wait("t", scQ, unit - 1)
                pp = slot(2 * par)
                for ic in range(8):
                    fn = lambda t, pp=pp, wb=wb, ic=ic, ob=ob, tt=tt: t.matmul(
                        pp[:, :], wb[:, ob * 1024 + ic * 128: ob * 1024 + (ic + 1) * 128],
                        xbibf[:, ic * 2048 + tt * 512: ic * 2048 + tt * 512 + 512],
                        start=(ic == 0), stop=(ic == 7))
                    if ic == 7:
                        P.step("t", tQ, fn)
                    else:
                        P.emit("t", fn)
                P.wait("sc", tQ, P.n(tQ))
                if dst is not None:
                    P.step("sc", scQ, lambda sc, pp=pp, ob=ob, tt=tt: sc.copy(
                        out=qf[:, ob * 2048 + tt * 512: ob * 2048 + tt * 512 + 512],
                        in_=pp[:, :]))
                else:
                    P.step("sc", scQ, lambda sc, pp=pp, ob=ob, tt=tt: sc.copy(
                        out=vf[:, ob * 2048 + tt * 512: ob * 2048 + tt * 512 + 512],
                        in_=pp[:, :]))

    lnsems = {}
    lnbufs = {"zb": zb, "zsq": zsq, "statr": statr, "xnt": xnt,
              "statcol": statcol, "onesrow": onesrow, "epsr": epsr}
    kv_unit = [0]
    ch_g = [0]
    for m in range(8):
        if m > 0:
            for s_ in ("vH", "scE"):
                if s_ in lnsems:
                    P.wait("t", lnsems[s_], P.n(lnsems[s_]))
            P.wait("t", scLb, P.n(scLb))
            P.wait("t", vZ2, P.n(vZ2))
        nch = 8 * m + 8
        ncommon = 8 * m + 4
        pend = []

        def emit_av(ent):
            (kpar_, ppar_, c_, common_, vm_target, sce_target) = ent
            if vm_target is not None:
                P.wait("t", vM, vm_target)
            else:
                P.wait("t", scE2, sce_target)
            for j in range(4):
                for half in range(2):
                    dv = 2 * j + half
                    P.emit("t", lambda t, j=j, half=half, dv=dv, kpar=kpar_, ppar=ppar_, c=c_, nch=nch: t.matmul(
                        slot(j)[:, half * 256: half * 256 + 256],
                        vvb[kpar][:, dv * 128:(dv + 1) * 128],
                        probs[ppar][:, 0:256],
                        start=(c == 0 and half == 0), stop=(c == nch - 1 and half == 1)))
            P.step("t", tS2, lambda t, ppar=ppar_, c=c_, nch=nch: t.matmul(
                pool[0:1, 3072:3328], onecol[:, 0:1], probs[ppar][:, 0:256],
                start=(c == 0), stop=(c == nch - 1)))

        for c in range(nch):
            common = c < ncommon
            n = 256
            qoff = 256 * m
            u = kv_unit[0]
            par = u % 8
            if u < 8:
                P.wait("s", scQ, P.n(scQ))
                if fused and ag_waits and u == 0:
                    for s_, v_ in ag_waits["kv"]:
                        P.wait("s", s_, v_)
            if u >= 8:
                P.wait("s", tS2, u - 7)
            if fused:
                r_, hh_ = c // 16, (c % 16) // 8
                lc_ = (c % 16) - 8 * hh_
                P.step("s", gKV[par], lambda s, par=par, r_=r_, hh_=hh_, lc_=lc_: s.dma_start(
                    out=kvb[par].rearrange("p (c1 o) -> p c1 o", c1=8),
                    in_=kf_src[hh_][r_ * 1024:(r_ + 1) * 1024, :]
                    .rearrange("(c1 p) kv -> p c1 kv", p=128)[:, :, lc_ * 128:(lc_ + 1) * 128]), 16)
            else:
                P.step("s", gKV[par], lambda s, par=par, c=c: s.dma_start(
                    out=kvb[par].rearrange("p (c1 o) -> p c1 o", c1=8),
                    in_=kf_src.rearrange("(c1 p) kv -> p c1 kv", p=128)[:, :, c * 128:(c + 1) * 128]), 16)
            if fused:
                r_, hh_ = c // 16, (c % 16) // 8
                rih_ = 128 * (c % 16) - 1024 * hh_
                P.step("s", gKV[par], lambda s, par=par, r_=r_, hh_=hh_, rih_=rih_: s.dma_start(
                    out=vvb[par][:, :],
                    in_=vt_src[hh_][r_ * 1024 + rih_: r_ * 1024 + rih_ + 128, :]), 16)
            else:
                P.step("s", gKV[par], lambda s, par=par, c=c: s.dma_start(
                    out=vvb[par][:, :], in_=vt_src[c * 128:(c + 1) * 128, :]), 16)
            kvw = P.n(gKV[par])
            kv_unit[0] += 1
            P.wait("t", gKV[par], kvw)
            if ch_g[0] >= 2:
                P.wait("t", scE2, ch_g[0] - 1)
            psc = pool[:, 2048 + 512 * (c % 2): 2048 + 512 * (c % 2) + 256]
            for ic in range(8):
                fn = lambda t, psc=psc, par=par, ic=ic, qoff=qoff, n=n: t.matmul(
                    psc[:, 0:n], kvb[par][:, ic * 128:(ic + 1) * 128],
                    qf[:, ic * 2048 + qoff: ic * 2048 + qoff + n],
                    start=(ic == 0), stop=(ic == 7))
                if ic == 7:
                    P.step("t", tS, fn)
                else:
                    P.emit("t", fn)
            ppar = ch_g[0] % 2
            P.wait("sc", tS, P.n(tS))
            P.step("sc", scE2, lambda sc, psc=psc, ppar=ppar, n=n: sc.activation(
                out=probs[ppar][:, 0:n], in_=psc[:, 0:n], func=AF.Exp, scale=1.0 / 32.0))
            need_v = False
            if common and 8 * m <= c < 8 * m + 4:
                mi = m * 8 + (c - 8 * m)
                P.wait("v", scE2, P.n(scE2))
                P.step("v", vM, lambda v, ppar=ppar, mi=mi: v.tensor_mul(
                    probs[ppar][:, 0:128], probs[ppar][:, 0:128],
                    maskb[:, mi * 128:(mi + 1) * 128]))
                need_v = True
            elif not common:
                mi = m * 8 + 4 + (c - ncommon)
                P.wait("v", scE2, P.n(scE2))
                P.step("v", vM, lambda v, ppar=ppar: v.memset(probs[ppar][:, 0:128], 0.0))
                P.step("v", vM, lambda v, ppar=ppar, mi=mi: v.tensor_mul(
                    probs[ppar][:, 128:256], probs[ppar][:, 128:256],
                    maskb[:, mi * 128:(mi + 1) * 128]))
                need_v = True
            ent = (par, ppar, c, common, P.n(vM) if need_v else None, P.n(scE2))
            pend.append(ent)
            if len(pend) >= 2:
                emit_av(pend.pop(0))
            ch_g[0] += 1
        while pend:
            emit_av(pend.pop(0))
        P.wait("sc", tS2, P.n(tS2))
        P.step("sc", vC2, lambda sc: sc.activation(
            out=linv[0:1, :], in_=pool[0:1, 3072:3328], func=AF.Ln))
        P.wait("sc", vC2, P.n(vC2))
        P.step("sc", vC2, lambda sc: sc.activation(
            out=linv[0:1, :], in_=linv[0:1, :], func=AF.Exp, scale=-1.0))
        P.wait("t", vC2, P.n(vC2))
        P.step("t", tLb, lambda t: t.matmul(
            pool[:, 3328:3584], onesrow[0:1, :], linv[0:1, :], start=True, stop=True))
        P.wait("sc", tLb, P.n(tLb))
        P.step("sc", scLb, lambda sc: sc.copy(out=linbs[:, :], in_=pool[:, 3328:3584]))
        P.wait("v", scLb, P.n(scLb))
        for oc in range(8):
            P.step("v", vZ, lambda v, oc=oc, m=m: v.tensor_mul(
                zslot[:, oc * 256:(oc + 1) * 256],
                slot(oc // 2)[:, (oc % 2) * 256:(oc % 2) * 256 + 256], linbs[:, :]))
            P.wait("v", vZ, P.n(vZ))
            P.step("v", vZf, lambda v, oc=oc, m=m: v.tensor_add(
                zslot[:, oc * 256:(oc + 1) * 256],
                zslot[:, oc * 256:(oc + 1) * 256],
                vf[:, oc * 2048 + m * 256: oc * 2048 + m * 256 + 256]))
        def sink1b(sub, oc, par, m=m):
            P.step("v", vM, lambda v, oc=oc, par=par: v.tensor_copy(
                lnb[:, oc * 256:(oc + 1) * 256], xnt[par][:, 0:256]))
            return [(vM, P.n(vM))]

        P.wait("v", vZf, P.n(vZf))
        P.wait("t", vZf, P.n(vZf))
        P.wait("sc", vZf, P.n(vZf))
        emit_ln_fm(P, "p3ln", pool, lnbufs, 1,
                   lambda oc, sub, m=m: zslot[:, oc * 256:(oc + 1) * 256],
                   sink1b, w=256, gb=None, sems=lnsems)
        P.wait("v", vM, P.n(vM))
        for oc in range(8):
            P.step("v", vM, lambda v, oc=oc: v.tensor_scalar(
                lnb[:, oc * 256:(oc + 1) * 256], lnb[:, oc * 256:(oc + 1) * 256],
                gamb[:, oc:oc + 1], betb[:, oc:oc + 1], ALU.mult, ALU.add))
        P.wait("t", vM, P.n(vM))
        for ob in range(8):
            unit = P.n(tFc)
            par = unit % 2
            if fused:
                if unit >= 2:
                    P.wait("g", vZ2, P.n(vZ2) - 1)
                for hh in range(2):
                    P.step("g", sxb[par], lambda g, ob=ob, par=par, m=m, hh=hh: g.indirect_dma_start(
                        out=xbt[par][:, hh * 128:(hh + 1) * 128], out_offset=None,
                        in_=xb_view[m % 2][:, :],
                        in_offset=IndirectOffsetOnAxis(
                            ap=idxgu[:, ((2 * m + hh) * 8 + ob):((2 * m + hh) * 8 + ob) + 1], axis=0)), 16)
            else:
                if unit >= 2:
                    P.wait("s", vZ2, P.n(vZ2) - 1)
                P.step("s", sxb[par], lambda s, ob=ob, par=par, m=m: s.dma_start(
                    out=xbt[par][:, :],
                    in_=xbi_src[ob * 128:(ob + 1) * 128, m * 256:(m + 1) * 256]), 16)
            pfc = pool[:, 3584:3840] if (unit % 2 == 0) else pool[:, 3328:3584]
            if P.n(vZ2) >= 2:
                P.wait("t", vZ2, P.n(vZ2) - 1)
            for ic in range(8):
                fn = lambda t, pfc=pfc, ic=ic, ob=ob: t.matmul(
                    pfc[:, :], fwb[:, ob * 1024 + ic * 128: ob * 1024 + (ic + 1) * 128],
                    lnb[:, ic * 256:(ic + 1) * 256], start=(ic == 0), stop=(ic == 7))
                if ic == 7:
                    P.step("t", tFc, fn)
                else:
                    P.emit("t", fn)
            P.wait("v", tFc, P.n(tFc))
            P.wait("v", sxb[par], P.n(sxb[par]))
            P.step("v", vZ2, lambda v, ob=ob, par=par, pfc=pfc: v.scalar_tensor_tensor(
                z2slot[:, ob * 256:(ob + 1) * 256], pfc[:, :], bfcb[:, ob:ob + 1],
                xbt[par][:, :], ALU.add, ALU.add))
        P.wait("v", vZ2, P.n(vZ2))
        P.wait("t", vZ2, P.n(vZ2))
        P.wait("sc", vZ2, P.n(vZ2))

        def sink2(sub, oc, par, m=m):
            spar = (m * 8 + oc) % 2
            P.wait("s", lnsems["vH"], P.n(lnsems["vH"]))
            P.step("s", soC[spar], lambda s, oc=oc, par=par, m=m: s.dma_start(
                out=out_xC[oc * 128:(oc + 1) * 128, m * 256:(m + 1) * 256],
                in_=xnt[par][:, 0:256]), 16)
            return [(soC[spar], P.n(soC[spar]))]

        emit_ln_fm(P, "p3ln", pool, lnbufs, 1,
                   lambda oc, sub, m=m: z2slot[:, oc * 256:(oc + 1) * 256],
                   sink2, w=256, gb=None, sems=lnsems)
    P.wait("s", soC[0], P.n(soC[0]))
    P.wait("s", soC[1], P.n(soC[1]))
    P.emit("s", lambda s: s.nop())
    return [soC[0], soC[1]]


def build_p3():
    nc = bass.Bass("TRN2", target_bir_lowering=False)
    dram = p3_dram_decls(nc)
    dram["c_ones_row"] = nc.dram_tensor("c_ones_row", [1, 128], F32, kind="ExternalInput").ap()
    dram["c_statcol"] = nc.dram_tensor("c_statcol", [128, 1], BF16, kind="ExternalInput").ap()
    dram["c_onecol"] = nc.dram_tensor("c_onecol", [128, 1], BF16, kind="ExternalInput").ap()
    dram["c_eps"] = nc.dram_tensor("c_eps", [1, 1], F32, kind="ExternalInput").ap()
    xbi = nc.dram_tensor("p3_xbi", [D, ROWS], F32, kind="ExternalInput").ap()
    kf = nc.dram_tensor("p3_kf", [D, T], BF16, kind="ExternalInput").ap()
    vt = nc.dram_tensor("p3_vt", [T, D], BF16, kind="ExternalInput").ap()
    out_xC = nc.dram_tensor("p3_xC", [D, ROWS], F32, kind="ExternalOutput").ap()
    stack = ExitStack()
    with stack:
        A = Arena(nc, stack, "arena", 104448)
        pool_t = stack.enter_context(nc.psum_tensor("pool", [128, 4096], F32))
        P = Plan(nc, stack)
        emit_p3(P, nc, A, pool_t, dram, xbi, kf, vt, out_xC)
        block = stack.enter_context(nc.Block())
        P.realize(block)
    return nc


def run_p3(inputs, xbi_shards, kf_full, vt_full):
    maps = stage_p3(inputs)
    for c in range(NCORE):
        b = c // 4
        maps[c]["p3_xbi"] = np.ascontiguousarray(xbi_shards[c])
        maps[c]["p3_kf"] = np.ascontiguousarray(kf_full[b])
        maps[c]["p3_vt"] = np.ascontiguousarray(vt_full[b])
    nc = build_p3()
    res = run_bass_kernel_spmd(nc, maps, core_ids=list(range(NCORE)))
    return [res.results[c]["p3_xC"] for c in range(NCORE)]



def stage_p4(inputs):
    wd1 = inputs["L1_ffn_Wd1"]
    wd2 = inputs["L1_ffn_Wd2"]
    wd1t = np.concatenate([_pretile2(np.ascontiguousarray(wd1[e])) for e in range(8)], axis=0)
    wd2t = np.concatenate([_pretile2(np.ascontiguousarray(wd2[e])) for e in range(8)], axis=0)
    bd1 = np.ascontiguousarray(inputs["L1_ffn_bd1"].reshape(8, 2, 128).transpose(2, 0, 1).reshape(128, 16))
    bd2 = np.ascontiguousarray(inputs["L1_ffn_bd2"].reshape(8, 8, 128).transpose(2, 0, 1).reshape(128, 64))
    wbt = _pretile2(np.ascontiguousarray(inputs["L1_ffn_W_base"]))
    bb = np.ascontiguousarray(inputs["L1_ffn_b_base"].reshape(8, 128).T)
    wad = np.ascontiguousarray(inputs["L1_ffn_Wa_down"].reshape(8 * 1024, 64))
    bad = np.ascontiguousarray(inputs["L1_ffn_ba_down"].reshape(8 * 64, 1))
    wau = np.ascontiguousarray(inputs["L1_ffn_Wa_up"].reshape(8 * 64, 1024))
    bau = np.ascontiguousarray(
        inputs["L1_ffn_ba_up"].reshape(8, 8, 128).transpose(0, 2, 1).reshape(8 * 128, 8))
    iota128 = np.arange(128, dtype=np.float32).reshape(128, 1)
    erow = np.arange(8, dtype=np.float32).reshape(1, 8)
    shared = {
        "p4_wd1t": wd1t, "p4_wd2t": wd2t, "p4_bd1": bd1, "p4_bd2": bd2,
        "p4_wbt": wbt, "p4_bb": bb, "p4_wad": wad, "p4_bad": bad,
        "p4_wau": wau, "p4_bau": bau, "p4_iota": iota128, "p4_erow": erow,
        "c_ones_row": np.ones((1, 128), dtype=np.float32),
        "c_statcol": np.full((128, 1), 1.0 / D, dtype=np.float32).astype(BF),
        "c_onecol": np.ones((128, 1), dtype=np.float32).astype(BF),
        "c_eps": np.full((1, 1), 1e-5, dtype=np.float32),
    }
    return [dict(shared) for _ in range(NCORE)]


def emit_p4a(P, nc, A, pool, dram, xci_src, out_loss):
    def slot(i, w=1):
        return pool[:, 512 * i:512 * (i + w)]
    gB = P.new_dma_sem("p4gB", "g")
    sB = P.new_dma_sem("p4sB", "s")
    gW1 = [P.new_dma_sem("p4gW10", "g"), P.new_dma_sem("p4gW11", "g")]
    gW2 = [P.new_dma_sem("p4gW20", "g"), P.new_dma_sem("p4gW21", "g")]
    tH1 = P.new_sem("p4tH1")
    tR = P.new_sem("p4tR")
    tL = P.new_sem("p4tL")
    scH = P.new_sem("p4scH")
    scS = P.new_sem("p4scS")
    vD2 = P.new_sem("p4vD2")
    vL = P.new_sem("p4vL")
    so4 = P.new_dma_sem("p4so", "s")

    xcibf = A.bf16(8 * 2048)
    xcif = A.f32(8 * 2048)
    w1buf = [A.bf16(16 * 128) for _ in range(2)]
    w2buf = [A.bf16(16 * 128) for _ in range(2)]
    heb = A.bf16(2 * 2048)
    bd1b = A.f32(16)
    bd2b = A.f32(64)
    diff = [A.f32(512) for _ in range(4)]
    dsq = [A.bf16(512) for _ in range(4)]
    lrow = A.f32(8)
    onecol = A.bf16(2)

    for ic in range(8):
        P.step("g", gB, lambda g, ic=ic: g.dma_start(
            out=xcibf[:, ic * 2048:(ic + 1) * 2048],
            in_=xci_src[ic * 128:(ic + 1) * 128, :]), 16)
    P.step("g", gB, lambda g: g.dma_start(out=onecol[:, 0:1], in_=dram["c_onecol"][:, :]), 16)
    for ic in range(8):
        P.step("s", sB, lambda s, ic=ic: s.dma_start(
            out=xcif[:, ic * 2048:(ic + 1) * 2048],
            in_=xci_src[ic * 128:(ic + 1) * 128, :]), 16)
    P.step("s", sB, lambda s: s.dma_start(out=bd1b[:, 0:16], in_=dram["p4_bd1"][:, :]), 16)
    P.step("s", sB, lambda s: s.dma_start(out=bd2b[:, 0:64], in_=dram["p4_bd2"][:, :]), 16)
    for e in ("t", "v", "sc"):
        P.wait(e, gB)
        P.wait(e, sB)

    for e in range(8):
        par = e % 2
        if e >= 2:
            P.wait("g", tH1, 8 * (e - 1))
        P.step("g", gW1[par], lambda g, e=e, par=par: g.dma_start(
            out=w1buf[par].rearrange("p (c o) -> p c o", c=16),
            in_=dram["p4_wd1t"][e * 2048:(e + 1) * 2048, :].rearrange("(c p) o -> p c o", p=128)), 16)
        w1v = P.n(gW1[par])
        if e >= 2:
            P.wait("g", tR, 32 * (e - 1))
        P.step("g", gW2[par], lambda g, e=e, par=par: g.dma_start(
            out=w2buf[par].rearrange("p (c o) -> p c o", c=16),
            in_=dram["p4_wd2t"][e * 2048:(e + 1) * 2048, :].rearrange("(c p) o -> p c o", p=128)), 16)
        w2v = P.n(gW2[par])
        P.wait("t", gW1[par], w1v)
        for hb in range(2):
            for tt in range(4):
                hu = P.n(tH1)
                if hu >= 2:
                    P.wait("t", scH, hu - 1)
                ph = slot(hu % 2)
                for ic in range(8):
                    fn = lambda t, ph=ph, par=par, hb=hb, ic=ic, tt=tt: t.matmul(
                        ph[:, :], w1buf[par][:, (hb * 8 + ic) * 128:(hb * 8 + ic + 1) * 128],
                        xcibf[:, ic * 2048 + tt * 512: ic * 2048 + tt * 512 + 512],
                        start=(ic == 0), stop=(ic == 7))
                    if ic == 7:
                        P.step("t", tH1, fn)
                    else:
                        P.emit("t", fn)
                P.wait("sc", tH1, P.n(tH1))
                P.step("sc", scH, lambda sc, e=e, hb=hb, tt=tt, ph=ph: sc.activation(
                    out=heb[:, hb * 2048 + tt * 512: hb * 2048 + tt * 512 + 512],
                    in_=ph[:, :], func=AF.Relu, bias=bd1b[:, (e * 2 + hb):(e * 2 + hb) + 1]))
        P.wait("t", gW2[par], w2v)
        P.wait("t", scH, P.n(scH))
        pend4 = []

        def emit_loss(ent, e=e):
            dpar_, u32_, scs_t, first = ent
            if first:
                P.wait("t", vL, P.n(vL))
            P.wait("t", scS, scs_t)
            P.step("t", tL, lambda t, dpar=dpar_, u32=u32_: t.matmul(
                pool[0:1, 3584:4096], onecol[:, 0:1], dsq[dpar][:, :],
                start=(u32 == 0), stop=(u32 == 31)))

        for ob in range(8):
            for tt in range(4):
                ru = P.n(tR)
                u32 = ru % 32
                dpar = ru % 4
                if ru >= 4:
                    P.wait("t", vD2, ru - 3)
                    P.wait("v", scS, ru - 3)
                    P.wait("sc", tL, ru - 3)
                pr = slot(2 + dpar)
                for hc in range(2):
                    fn = lambda t, pr=pr, par=par, ob=ob, hc=hc, tt=tt: t.matmul(
                        pr[:, :], w2buf[par][:, (ob * 2 + hc) * 128:(ob * 2 + hc + 1) * 128],
                        heb[:, hc * 2048 + tt * 512: hc * 2048 + tt * 512 + 512],
                        start=(hc == 0), stop=(hc == 1))
                    if hc == 1:
                        P.step("t", tR, fn)
                    else:
                        P.emit("t", fn)
                P.wait("v", tR, P.n(tR))
                P.step("v", vD2, lambda v, e=e, ob=ob, tt=tt, pr=pr, dpar=dpar: v.scalar_tensor_tensor(
                    diff[dpar][:, :], pr[:, :], bd2b[:, (e * 8 + ob):(e * 8 + ob) + 1],
                    xcif[:, ob * 2048 + tt * 512: ob * 2048 + tt * 512 + 512],
                    ALU.add, ALU.subtract))
                P.wait("sc", vD2, P.n(vD2))
                P.step("sc", scS, lambda sc, dpar=dpar: sc.activation(
                    out=dsq[dpar][:, :], in_=diff[dpar][:, :], func=AF.Square))
                pend4.append((dpar, u32, P.n(scS), u32 == 0))
                if len(pend4) > 2:
                    emit_loss(pend4.pop(0))
        while pend4:
            emit_loss(pend4.pop(0))
        P.wait("v", tL, P.n(tL))
        P.step("v", vL, lambda v, e=e: v.reduce_sum(
            lrow[0:1, e:e + 1], pool[0:1, 3584:4096], axis=AX))
    P.wait("s", vL, P.n(vL))
    P.step("s", so4, lambda s: s.dma_start(out=out_loss[:, :], in_=lrow[0:1, 0:8]), 16)
    P.wait("s", so4, P.n(so4))
    P.emit("s", lambda s: s.nop())
    return [so4]


def emit_p4b(P, nc, A, pool, dram, xci_src, parts_src, out_xD, ag_wait=None):
    def slot(i, w=1):
        return pool[:, 512 * i:512 * (i + w)]
    gB = P.new_dma_sem("q4gB", "g")
    sB = P.new_dma_sem("q4sB", "s")
    gG = P.new_dma_sem("q4gG", "g")
    tB2 = P.new_sem("q4tB2")
    tA5 = P.new_sem("q4tA5")
    tY = P.new_sem("q4tY")
    scB2 = P.new_sem("q4scB2")
    scA5 = P.new_sem("q4scA5")
    vI = P.new_sem("q4vI")
    vY = P.new_sem("q4vY")
    so5 = [P.new_dma_sem("q4so5a", "s"), P.new_dma_sem("q4so5b", "s")]

    xcibf = A.bf16(8 * 2048)
    xcif = A.f32(8 * 2048)
    wbb = A.bf16(64 * 128)
    wadb = A.bf16(8 * 64)
    waub = A.bf16(1024)
    badb = A.f32(1)
    baub = A.f32(8)
    bbb = A.f32(8)
    cbias = A.f32(8)
    abuf = A.bf16(2048)
    parts = A.f32(32)
    lsum = A.f32(24)
    erowb = A.f32(8)
    iotab = A.f32(1)
    fxb = A.f32(1)
    idxt = A.bf16(6)
    zbuf = A.f32(8 * 2048)
    zb = A.bf16(512)
    zsq = A.bf16(512)
    statr = A.f32(2560)
    xnt = [A.f32(512) for _ in range(2)]
    statcol = A.bf16(2)
    onesrow = A.f32(128)
    epsr = A.f32(1)

    idxu = idxt.bitcast(U32)

    for ic in range(8):
        P.step("g", gB, lambda g, ic=ic: g.dma_start(
            out=xcibf[:, ic * 2048:(ic + 1) * 2048],
            in_=xci_src[ic * 128:(ic + 1) * 128, :]), 16)
    for ob in range(8):
        P.step("g", gB, lambda g, ob=ob: g.dma_start(
            out=wbb[:, ob * 1024:(ob + 1) * 1024].rearrange("p (c o) -> p c o", c=8),
            in_=dram["p4_wbt"][ob * 1024:(ob + 1) * 1024, :].rearrange("(c p) o -> p c o", p=128)), 16)
    P.step("g", gB, lambda g: g.dma_start(out=statcol[:, 0:1], in_=dram["c_statcol"][:, :]), 16)
    for ic in range(8):
        P.step("s", sB, lambda s, ic=ic: s.dma_start(
            out=xcif[:, ic * 2048:(ic + 1) * 2048],
            in_=xci_src[ic * 128:(ic + 1) * 128, :]), 16)
    P.step("s", sB, lambda s: s.dma_start(out=bbb[:, 0:8], in_=dram["p4_bb"][:, :]), 16)
    P.step("s", sB, lambda s: s.dma_start(out=erowb[0:1, 0:8], in_=dram["p4_erow"][:, :]), 16)
    P.step("s", sB, lambda s: s.dma_start(out=iotab[:, 0:1], in_=dram["p4_iota"][:, :]), 16)
    P.step("s", sB, lambda s: s.dma_start(out=onesrow[0:1, :], in_=dram["c_ones_row"][:, :]), 16)
    P.step("s", sB, lambda s: s.dma_start(out=epsr[0:1, :], in_=dram["c_eps"][:, :]), 16)
    if ag_wait is not None:
        P.wait("s", ag_wait[0], ag_wait[1])
    if isinstance(parts_src, list):
        for j in range(4):
            P.step("s", sB, lambda s, j=j: s.dma_start(
                out=parts[0:1, 8 * j:8 * j + 8], in_=parts_src[j]), 16)
    else:
        P.step("s", sB, lambda s: s.dma_start(out=parts[0:1, 0:32], in_=parts_src[:, :]), 16)
    for e in ("t", "v", "sc", "g"):
        P.wait(e, gB)
        P.wait(e, sB)

    P.step("v", vI, lambda v: v.tensor_add(lsum[0:1, 0:8], parts[0:1, 0:8], parts[0:1, 8:16]))
    P.wait("v", vI, P.n(vI))
    P.step("v", vI, lambda v: v.tensor_add(lsum[0:1, 8:16], parts[0:1, 16:24], parts[0:1, 24:32]))
    P.wait("v", vI, P.n(vI))
    P.step("v", vI, lambda v: v.tensor_add(lsum[0:1, 0:8], lsum[0:1, 0:8], lsum[0:1, 8:16]))
    P.wait("v", vI, P.n(vI))
    P.step("v", vI, lambda v: v.tensor_reduce(
        lsum[0:1, 8:9], lsum[0:1, 0:8], AX, ALU.min))
    P.wait("v", vI, P.n(vI))
    P.step("v", vI, lambda v: v.tensor_scalar(
        lsum[0:1, 16:24], lsum[0:1, 0:8], lsum[0:1, 8:9], None, ALU.is_equal))
    P.wait("v", vI, P.n(vI))
    P.step("v", vI, lambda v: v.tensor_mul(lsum[0:1, 16:24], lsum[0:1, 16:24], erowb[0:1, 0:8]))
    P.wait("v", vI, P.n(vI))
    P.step("v", vI, lambda v: v.reduce_sum(fxb[0:1, 0:1], lsum[0:1, 16:24], axis=AX))
    P.wait("t", vI, P.n(vI))
    P.step("t", tB2, lambda t: t.matmul(pool[:, 3584:3585], onesrow[0:1, :], fxb[0:1, 0:1],
                                        start=True, stop=True))
    P.wait("sc", tB2, P.n(tB2))
    P.step("sc", scB2, lambda sc: sc.copy(out=fxb[:, 0:1] if False else lsum[:, 0:1], in_=pool[:, 3584:3585]))
    P.wait("v", scB2, P.n(scB2))
    P.step("v", vI, lambda v: v.scalar_tensor_tensor(
        idxu[:, 0:1], lsum[:, 0:1], 1024.0, iotab[:, 0:1], ALU.mult, ALU.add))
    P.wait("v", vI, P.n(vI))
    P.step("v", vI, lambda v: v.scalar_tensor_tensor(
        idxu[:, 1:2], lsum[:, 0:1], 64.0, iotab[:, 0:1], ALU.mult, ALU.add))
    P.wait("v", vI, P.n(vI))
    P.step("v", vI, lambda v: v.scalar_tensor_tensor(
        idxu[:, 2:3], lsum[:, 0:1], 128.0, iotab[:, 0:1], ALU.mult, ALU.add))
    P.wait("v", vI, P.n(vI))
    P.step("v", vI, lambda v: v.memset(waub[:, :], 0.0))
    P.wait("v", vI, P.n(vI))
    P.step("v", vI, lambda v: v.memset(abuf[:, :], 0.0))
    P.wait("g", vI, P.n(vI))
    for ic in range(8):
        P.step("g", gG, lambda g, ic=ic: g.indirect_dma_start(
            out=wadb[:, ic * 64:(ic + 1) * 64], out_offset=None,
            in_=dram["p4_wad"][:, :],
            in_offset=IndirectOffsetOnAxis(ap=idxu[:, 0:1], axis=0),
            element_offset=ic * 128 * 64), 16)
    P.step("g", gG, lambda g: g.indirect_dma_start(
        out=waub[0:64, :], out_offset=None, in_=dram["p4_wau"][:, :],
        in_offset=IndirectOffsetOnAxis(ap=idxu[0:64, 1:2], axis=0)), 16)
    P.step("g", gG, lambda g: g.indirect_dma_start(
        out=badb[0:64, :], out_offset=None, in_=dram["p4_bad"][:, :],
        in_offset=IndirectOffsetOnAxis(ap=idxu[0:64, 1:2], axis=0)), 16)
    P.step("g", gG, lambda g: g.indirect_dma_start(
        out=baub[:, 0:8], out_offset=None, in_=dram["p4_bau"][:, :],
        in_offset=IndirectOffsetOnAxis(ap=idxu[:, 2:3], axis=0)), 16)
    for e in ("t", "v", "sc"):
        P.wait(e, gG)

    for tt in range(4):
        pa = slot(4 + (tt % 2))
        if tt >= 2:
            P.wait("t", scA5, tt - 1)
        for ic in range(8):
            fn = lambda t, pa=pa, ic=ic, tt=tt: t.matmul(
                pa[0:64, :], wadb[:, ic * 64:(ic + 1) * 64],
                xcibf[:, ic * 2048 + tt * 512: ic * 2048 + tt * 512 + 512],
                start=(ic == 0), stop=(ic == 7))
            if ic == 7:
                P.step("t", tA5, fn)
            else:
                P.emit("t", fn)
        P.wait("sc", tA5, P.n(tA5))
        P.step("sc", scA5, lambda sc, tt=tt, pa=pa: sc.activation(
            out=abuf[0:64, tt * 512:(tt + 1) * 512], in_=pa[0:64, :], func=AF.Relu,
            bias=badb[0:64, 0:1]))
    P.wait("v", sB, P.n(sB))
    P.step("v", vI, lambda v: v.tensor_add(cbias[:, 0:8], bbb[:, 0:8], baub[:, 0:8]))
    P.wait("t", scA5, P.n(scA5))
    P.wait("t", vI, P.n(vI))
    for ob in range(8):
        for tt in range(4):
            yu = P.n(tY)
            py = slot(yu % 4)
            if yu >= 4:
                P.wait("t", vY, yu - 3)
            for ic in range(8):
                P.emit("t", lambda t, py=py, ic=ic, ob=ob, tt=tt: t.matmul(
                    py[:, :], wbb[:, ob * 1024 + ic * 128: ob * 1024 + (ic + 1) * 128],
                    xcibf[:, ic * 2048 + tt * 512: ic * 2048 + tt * 512 + 512],
                    start=(ic == 0), stop=False))
            fn = lambda t, py=py, ob=ob, tt=tt: t.matmul(
                py[:, :], waub[:, ob * 128:(ob + 1) * 128],
                abuf[:, tt * 512:(tt + 1) * 512], start=False, stop=True)
            P.step("t", tY, fn)
            P.wait("v", tY, P.n(tY))
            P.step("v", vY, lambda v, py=py, ob=ob, tt=tt: v.scalar_tensor_tensor(
                zbuf[:, ob * 2048 + tt * 512: ob * 2048 + tt * 512 + 512],
                py[:, :], cbias[:, ob:ob + 1],
                xcif[:, ob * 2048 + tt * 512: ob * 2048 + tt * 512 + 512],
                ALU.add, ALU.add))
    for e in ("t", "v", "sc"):
        P.wait(e, vY, P.n(vY))
    lnb2 = {"zb": zb, "zsq": zsq, "statr": statr, "xnt": xnt,
            "statcol": statcol, "onesrow": onesrow, "epsr": epsr}

    st5 = [0]

    def sink(sub, oc, par):
        spar = st5[0] % 2
        st5[0] += 1
        P.wait("s", "p4lnvH", P.n("p4lnvH"))
        P.step("s", so5[spar], lambda s, oc=oc, sub=sub, par=par: s.dma_start(
            out=out_xD[oc * 128:(oc + 1) * 128, sub * 512: sub * 512 + 512],
            in_=xnt[par][:, 0:512]), 16)
        return [(so5[spar], P.n(so5[spar]))]

    emit_ln_fm(P, "p4ln", pool, lnb2, 4,
               lambda oc, sub: zbuf[:, oc * 2048 + sub * 512: oc * 2048 + sub * 512 + 512],
               sink, w=512)
    P.wait("s", so5[0], P.n(so5[0]))
    P.wait("s", so5[1], P.n(so5[1]))
    P.emit("s", lambda s: s.nop())
    return [so5[0], so5[1]]


def build_p4a():
    nc = bass.Bass("TRN2", target_bir_lowering=False)
    dram = {
        "p4_wd1t": nc.dram_tensor("p4_wd1t", [8 * 2048, 128], F32, kind="ExternalInput").ap(),
        "p4_wd2t": nc.dram_tensor("p4_wd2t", [8 * 2048, 128], F32, kind="ExternalInput").ap(),
        "p4_bd1": nc.dram_tensor("p4_bd1", [128, 16], F32, kind="ExternalInput").ap(),
        "p4_bd2": nc.dram_tensor("p4_bd2", [128, 64], F32, kind="ExternalInput").ap(),
        "c_onecol": nc.dram_tensor("c_onecol", [128, 1], BF16, kind="ExternalInput").ap(),
    }
    xci = nc.dram_tensor("p4_xci", [D, ROWS], F32, kind="ExternalInput").ap()
    out_loss = nc.dram_tensor("p4_loss", [1, 8], F32, kind="ExternalOutput").ap()
    stack = ExitStack()
    with stack:
        A = Arena(nc, stack, "arena", 104448)
        pool_t = stack.enter_context(nc.psum_tensor("pool", [128, 4096], F32))
        P = Plan(nc, stack)
        emit_p4a(P, nc, A, pool_t, dram, xci, out_loss)
        block = stack.enter_context(nc.Block())
        P.realize(block)
    return nc


def build_p4b():
    nc = bass.Bass("TRN2", target_bir_lowering=False)
    dram = {
        "p4_wbt": nc.dram_tensor("p4_wbt", [8 * 1024, 128], F32, kind="ExternalInput").ap(),
        "p4_bb": nc.dram_tensor("p4_bb", [128, 8], F32, kind="ExternalInput").ap(),
        "p4_wad": nc.dram_tensor("p4_wad", [8 * 1024, 64], F32, kind="ExternalInput").ap(),
        "p4_bad": nc.dram_tensor("p4_bad", [8 * 64, 1], F32, kind="ExternalInput").ap(),
        "p4_wau": nc.dram_tensor("p4_wau", [8 * 64, 1024], F32, kind="ExternalInput").ap(),
        "p4_bau": nc.dram_tensor("p4_bau", [8 * 128, 8], F32, kind="ExternalInput").ap(),
        "p4_iota": nc.dram_tensor("p4_iota", [128, 1], F32, kind="ExternalInput").ap(),
        "p4_erow": nc.dram_tensor("p4_erow", [1, 8], F32, kind="ExternalInput").ap(),
        "c_ones_row": nc.dram_tensor("c_ones_row", [1, 128], F32, kind="ExternalInput").ap(),
        "c_statcol": nc.dram_tensor("c_statcol", [128, 1], BF16, kind="ExternalInput").ap(),
        "c_eps": nc.dram_tensor("c_eps", [1, 1], F32, kind="ExternalInput").ap(),
    }
    xci = nc.dram_tensor("p4_xci", [D, ROWS], F32, kind="ExternalInput").ap()
    parts = nc.dram_tensor("p4_parts", [1, 32], F32, kind="ExternalInput").ap()
    out_xD = nc.dram_tensor("p4_xD", [D, ROWS], F32, kind="ExternalOutput").ap()
    stack = ExitStack()
    with stack:
        A = Arena(nc, stack, "arena", 104448)
        pool_t = stack.enter_context(nc.psum_tensor("pool", [128, 4096], F32))
        P = Plan(nc, stack)
        emit_p4b(P, nc, A, pool_t, dram, xci, parts, out_xD)
        block = stack.enter_context(nc.Block())
        P.realize(block)
    return nc



_NC_CACHE = {}


def _get_nc(name, builder):
    if name not in _NC_CACHE:
        _NC_CACHE[name] = builder()
    return _NC_CACHE[name]


def kernel_unfused(**inputs):
    maps0 = stage_p0(inputs)
    nc0 = _get_nc("p0", build_p0)
    res0 = run_bass_kernel_spmd(nc0, maps0, core_ids=list(range(NCORE)))
    xA = [res0.results[c]["p0_xA"] for c in range(NCORE)]

    maps1 = stage_p1(inputs)
    for c in range(NCORE):
        maps1[c]["p1_xA"] = np.ascontiguousarray(xA[c])
    nc1 = _get_nc("p1", build_p1)
    res1 = run_bass_kernel_spmd(nc1, maps1, core_ids=list(range(NCORE)))
    xB = [res1.results[c]["p1_xB"] for c in range(NCORE)]
    kf = [res1.results[c]["p1_kf"] for c in range(NCORE)]
    vt = [res1.results[c]["p1_vt"] for c in range(NCORE)]

    kf_full = [np.ascontiguousarray(np.concatenate(kf[4 * b:4 * b + 4], axis=1)) for b in range(B)]
    vt_full = [np.ascontiguousarray(np.concatenate(vt[4 * b:4 * b + 4], axis=0)) for b in range(B)]
    xB_full = [np.concatenate(xB[4 * b:4 * b + 4], axis=1) for b in range(B)]

    maps3 = stage_p3(inputs)
    for c in range(NCORE):
        b, cp = c // 4, c % 4
        perm = p3_perm_rows(cp)
        maps3[c]["p3_xbi"] = np.ascontiguousarray(xB_full[b][:, perm])
        maps3[c]["p3_kf"] = kf_full[b]
        maps3[c]["p3_vt"] = vt_full[b]
    nc3 = _get_nc("p3", build_p3)
    res3 = run_bass_kernel_spmd(nc3, maps3, core_ids=list(range(NCORE)))
    xC = [res3.results[c]["p3_xC"] for c in range(NCORE)]

    maps4 = stage_p4(inputs)
    maps4a = []
    for c in range(NCORE):
        m = {k: maps4[c][k] for k in ("p4_wd1t", "p4_wd2t", "p4_bd1", "p4_bd2", "c_onecol")}
        m["p4_xci"] = np.ascontiguousarray(xC[c])
        maps4a.append(m)
    nc4a = _get_nc("p4a", build_p4a)
    res4a = run_bass_kernel_spmd(nc4a, maps4a, core_ids=list(range(NCORE)))
    parts = [res4a.results[c]["p4_loss"] for c in range(NCORE)]

    maps4b = []
    for c in range(NCORE):
        b = c // 4
        m = {k: maps4[c][k] for k in ("p4_wbt", "p4_bb", "p4_wad", "p4_bad", "p4_wau",
                                      "p4_bau", "p4_iota", "p4_erow", "c_ones_row",
                                      "c_statcol", "c_eps")}
        m["p4_xci"] = np.ascontiguousarray(xC[c])
        m["p4_parts"] = np.ascontiguousarray(
            np.concatenate([parts[4 * b + j] for j in range(4)], axis=1))
        maps4b.append(m)
    nc4b = _get_nc("p4b", build_p4b)
    res4b = run_bass_kernel_spmd(nc4b, maps4b, core_ids=list(range(NCORE)))

    out = np.empty((B, T, D), dtype=np.float32)
    for c in range(NCORE):
        b, cp = c // 4, c % 4
        perm = p3_perm_rows(cp)
        out[b, perm, :] = res4b.results[c]["p4_xD"].T
    return out



def _stage_idx_tab(cp):
    tab = np.zeros((128, 128), dtype=np.uint32)
    p = np.arange(128)
    blocks = []
    for m in range(8):
        blocks.extend([8 * m + cp, 8 * m + 7 - cp])
    for s_, j in enumerate(blocks):
        r, blk2 = j // 16, (j % 16) % 8
        for dc in range(8):
            tab[:, s_ * 8 + dc] = (r * 1024 + dc * 128 + p) * 8 + blk2
    return tab


def stage_fused(inputs):
    m0 = stage_p0(inputs)
    m1 = stage_p1(inputs)
    m3 = stage_p3(inputs)
    m4 = stage_p4(inputs)
    maps = []
    for c in range(NCORE):
        m = {}
        m.update(m0[c])
        m.update(m1[c])
        m.update(m3[c])
        m.update(m4[c])
        m["fx_idxtab"] = _stage_idx_tab(c % 4)
        maps.append(m)
    return maps


def build_fused():
    nc = bass.Bass("TRN2", target_bir_lowering=False)
    dram = {}
    dram.update({
        "p0_xT": nc.dram_tensor("p0_xT", [D, TK], F32, kind="ExternalInput").ap(),
        "p0_wqkT": nc.dram_tensor("p0_wqkT", [16 * D, 128], F32, kind="ExternalInput").ap(),
        "p0_wvT": nc.dram_tensor("p0_wvT", [D, 1024], F32, kind="ExternalInput").ap(),
        "p0_woutT": nc.dram_tensor("p0_woutT", [8 * D, 128], F32, kind="ExternalInput").ap(),
        "p0_cos": nc.dram_tensor("p0_cos", [128, TK], F32, kind="ExternalInput").ap(),
        "p0_sin": nc.dram_tensor("p0_sin", [128, TK], F32, kind="ExternalInput").ap(),
        "p0_perm": nc.dram_tensor("p0_perm", [128, 128], BF16, kind="ExternalInput").ap(),
        "p0_mask": nc.dram_tensor("p0_mask", [128, 12 * 512], BF16, kind="ExternalInput").ap(),
        "c_sel": nc.dram_tensor("c_sel", [1, 256], F32, kind="ExternalInput").ap(),
        "c_ones_row": nc.dram_tensor("c_ones_row", [1, 128], F32, kind="ExternalInput").ap(),
        "c_statcol": nc.dram_tensor("c_statcol", [128, 1], BF16, kind="ExternalInput").ap(),
        "c_onecol": nc.dram_tensor("c_onecol", [128, 1], BF16, kind="ExternalInput").ap(),
        "c_eps": nc.dram_tensor("c_eps", [1, 1], F32, kind="ExternalInput").ap(),
    })
    dram.update(p1_dram_decls(nc))
    dram.update(p3_dram_decls(nc))
    dram.update({
        "p4_wd1t": nc.dram_tensor("p4_wd1t", [8 * 2048, 128], F32, kind="ExternalInput").ap(),
        "p4_wd2t": nc.dram_tensor("p4_wd2t", [8 * 2048, 128], F32, kind="ExternalInput").ap(),
        "p4_bd1": nc.dram_tensor("p4_bd1", [128, 16], F32, kind="ExternalInput").ap(),
        "p4_bd2": nc.dram_tensor("p4_bd2", [128, 64], F32, kind="ExternalInput").ap(),
        "p4_wbt": nc.dram_tensor("p4_wbt", [8 * 1024, 128], F32, kind="ExternalInput").ap(),
        "p4_bb": nc.dram_tensor("p4_bb", [128, 8], F32, kind="ExternalInput").ap(),
        "p4_wad": nc.dram_tensor("p4_wad", [8 * 1024, 64], F32, kind="ExternalInput").ap(),
        "p4_bad": nc.dram_tensor("p4_bad", [8 * 64, 1], F32, kind="ExternalInput").ap(),
        "p4_wau": nc.dram_tensor("p4_wau", [8 * 64, 1024], F32, kind="ExternalInput").ap(),
        "p4_bau": nc.dram_tensor("p4_bau", [8 * 128, 8], F32, kind="ExternalInput").ap(),
        "p4_iota": nc.dram_tensor("p4_iota", [128, 1], F32, kind="ExternalInput").ap(),
        "p4_erow": nc.dram_tensor("p4_erow", [1, 8], F32, kind="ExternalInput").ap(),
        "fx_idxtab": nc.dram_tensor("fx_idxtab", [128, 128], U32, kind="ExternalInput").ap(),
    })
    xA_d = nc.dram_tensor("fx_xA", [D, ROWS], F32).ap()
    xB_d = [nc.dram_tensor(f"fx_xB{r}", [D, 1024], F32) for r in range(2)]
    kf_d = [nc.dram_tensor(f"fx_kf{r}", [D, 1024], BF16) for r in range(2)]
    vt_d = [nc.dram_tensor(f"fx_vt{r}", [1024, D], BF16) for r in range(2)]
    xbag = [nc.dram_tensor(f"fx_xbag{r}", [4 * D, 1024], F32) for r in range(2)]
    kfag = [nc.dram_tensor(f"fx_kfag{r}", [4 * D, 1024], BF16) for r in range(2)]
    vtag = [nc.dram_tensor(f"fx_vtag{r}", [4 * 1024, D], BF16) for r in range(2)]
    loss_d = nc.dram_tensor("fx_loss", [1, 8], F32)
    lossag = nc.dram_tensor("fx_lossag", [4, 8], F32)
    xC_d = nc.dram_tensor("fx_xC", [D, ROWS], F32).ap()
    out_xD = nc.dram_tensor("fx_xD", [D, ROWS], F32, kind="ExternalOutput").ap()

    groups = [[0, 1, 2, 3], [4, 5, 6, 7]]
    stack = ExitStack()
    with stack:
        A = Arena(nc, stack, "arena", 104448)
        pool_t = stack.enter_context(nc.psum_tensor("pool", [128, 4096], F32))
        P = Plan(nc, stack)

        def fence(terms):
            for e in ENGS:
                for s_ in terms:
                    P.wait(e, s_)

        rdy = P.new_sem("fxrdy")
        clr = P.new_sem("fxclr")

        def recycle(released):
            P.phase_reset()
            if not released:
                return
            for e in ENGS:
                P.step(e, rdy, lambda x: x.nop())
            tgt = P.n(rdy)
            P.wait("g", rdy, tgt)
            for nm_h, nm_n in released:
                P.emit("g", lambda g, h=nm_h, v=nm_n: g.wait_ge(h, v) if v > 0 else g.nop())
            for i, (h, n) in enumerate(released):
                if i == len(released) - 1:
                    P.step("g", clr, lambda g, h=h: g.sem_clear(h))
                else:
                    P.emit("g", lambda g, h=h: g.sem_clear(h))
            for e in ENGS:
                P.wait(e, clr, P.n(clr))

        base = A.off
        mk = P.mark()
        t0 = emit_p0(P, nc, A, pool_t, dram, xA_d)
        fence(t0)
        rel = P.release_since(mk)
        recycle(rel)
        A.off = base
        cc = [P.new_sem(f"fxcc{i}") for i in range(7)]
        mk = P.mark()

        def rt_done(rt, store_sems):
            for s_ in store_sems[0:2]:
                P.wait("g", s_, P.n(s_))
            P.step("g", cc[3 * rt + 0], lambda g, rt=rt: g.collective_compute(
                "AllGather", ALU.bypass, replica_groups=groups,
                ins=[xB_d[rt].ap().opt()], outs=[xbag[rt].ap().opt()]), 1)
            for s_ in store_sems[2:4]:
                P.wait("g", s_, P.n(s_))
            P.step("g", cc[3 * rt + 1], lambda g, rt=rt: g.collective_compute(
                "AllGather", ALU.bypass, replica_groups=groups,
                ins=[kf_d[rt].ap().opt()], outs=[kfag[rt].ap().opt()]), 1)
            for s_ in store_sems[4:6]:
                P.wait("g", s_, P.n(s_))
            P.step("g", cc[3 * rt + 2], lambda g, rt=rt: g.collective_compute(
                "AllGather", ALU.bypass, replica_groups=groups,
                ins=[vt_d[rt].ap().opt()], outs=[vtag[rt].ap().opt()]), 1)

        t1 = emit_p1(P, nc, A, pool_t, dram, xA_d,
                     [xB_d[0].ap(), xB_d[1].ap()],
                     [kf_d[0].ap(), kf_d[1].ap()],
                     [vt_d[0].ap(), vt_d[1].ap()], rt_done_cb=rt_done)
        fence(t1)
        rel = P.release_since(mk)
        recycle(rel)
        A.off = base
        mk = P.mark()
        t3 = emit_p3(P, nc, A, pool_t, dram, None,
                     [kfag[0].ap(), kfag[1].ap()], [vtag[0].ap(), vtag[1].ap()], xC_d,
                     fused=True, xb_ag=[xbag[0].ap(), xbag[1].ap()],
                     idx_tab=dram["fx_idxtab"],
                     ag_waits={"xb": [(cc[0], 1), (cc[3], 1)],
                               "kv": [(cc[1], 1), (cc[2], 1), (cc[4], 1), (cc[5], 1)]})
        fence(t3)
        rel = P.release_since(mk)
        recycle(rel)
        A.off = base
        mk = P.mark()
        t4 = emit_p4a(P, nc, A, pool_t, dram, xC_d, loss_d.ap())
        fence(t4)
        rel = P.release_since(mk)
        recycle(rel)
        P.step("g", cc[6], lambda g: g.collective_compute(
            "AllGather", ALU.bypass, replica_groups=groups,
            ins=[loss_d.ap().opt()], outs=[lossag.ap().opt()]), 1)
        A.off = base
        parts_aps = [lossag.ap()[j:j + 1, :] for j in range(4)]
        t5 = emit_p4b(P, nc, A, pool_t, dram, xC_d, parts_aps, out_xD,
                      ag_wait=(cc[6], 1))
        fence(t5)
        block = stack.enter_context(nc.Block())
        P.realize(block)
    return nc


def kernel(**inputs):
    maps = stage_fused(inputs)
    nc = _get_nc("fused", build_fused)
    res = run_bass_kernel_spmd(nc, maps, core_ids=list(range(NCORE)))
    out = np.empty((B, T, D), dtype=np.float32)
    for c in range(NCORE):
        b, cp = c // 4, c % 4
        perm = p3_perm_rows(cp)
        out[b, perm, :] = res.results[c]["fx_xD"].T
    return out


# version 14
# speedup vs baseline: 1.1005x; 1.0287x over previous
import numpy as np
import ml_dtypes
from contextlib import ExitStack

import concourse.bass as bass
import concourse.mybir as mybir
from concourse.bass_utils import run_bass_kernel_spmd
from concourse.bass import IndirectOffsetOnAxis

F32 = mybir.dt.float32
BF16 = mybir.dt.bfloat16
U32 = mybir.dt.uint32
AF = mybir.ActivationFunctionType
ALU = mybir.AluOpType
AX = mybir.AxisListType.X

BF = ml_dtypes.bfloat16

B, T, D = 2, 8192, 1024
NH, DH, WIN = 16, 64, 256
DF = 4096
NCORE = 8
ROWS = 2048
HALO = 256
TK = ROWS + HALO


ENGS = ("t", "sc", "v", "g", "s")


class Plan:

    def __init__(self, nc, stack):
        self.nc = nc
        self.stack = stack
        self.streams = {e: [] for e in ENGS}
        self.sem = {}
        self.free = []
        self.order = []

    def new_sem(self, name):
        assert name not in self.sem, name
        if self.free:
            h, n = self.free.pop()
            self.sem[name] = [h, 0]
        else:
            h = self.stack.enter_context(self.nc.semaphore(name))
            self.sem[name] = [h, 0]
        self.order.append(name)
        return name

    def new_dma_sem(self, name, kind="g"):
        assert name not in self.sem, name
        if not hasattr(self, "dma_pools"):
            self.dma_pools = {"g": [], "s": []}
            self.dma_cursor = {"g": 0, "s": 0}
        pool = self.dma_pools[kind]
        i = self.dma_cursor[kind]
        self.dma_cursor[kind] = i + 1
        if i >= len(pool):
            h = self.stack.enter_context(self.nc.semaphore(name))
            pool.append([h, 0])
        self.sem[name] = [pool[i][0], pool[i][1]]
        pool[i].append(name)
        return name

    def phase_reset(self):
        if hasattr(self, "dma_pools"):
            for kind in ("g", "s"):
                for ent in self.dma_pools[kind]:
                    while len(ent) > 2:
                        alias = ent.pop()
                        if alias in self.sem:
                            ent[1] = self.sem[alias][1]
                            del self.sem[alias]
                self.dma_cursor[kind] = 0

    def mark(self):
        return len(self.order)

    def release_since(self, mk):
        released = []
        for name in self.order[mk:]:
            ent = self.sem[name]
            released.append((ent[0], ent[1]))
            self.free.append([ent[0], 0])
            del self.sem[name]
        del self.order[mk:]
        return released

    def n(self, name):
        return self.sem[name][1]

    def emit(self, eng, fn):
        self.streams[eng].append(fn)

    def wait(self, eng, name, val=None):
        h, cnt = self.sem[name]
        v = cnt if val is None else val
        if v <= 0:
            return
        self.emit(eng, lambda e, h=h, v=v: e.wait_ge(h, v))

    def step(self, eng, name, fn, k=1):
        h, _ = self.sem[name]
        self.emit(eng, lambda e, h=h, k=k, fn=fn: fn(e).then_inc(h, k))
        self.sem[name][1] += k
        return self.sem[name][1]

    def realize(self, block):
        for ename, attr in (("t", "tensor"), ("sc", "scalar"), ("v", "vector"),
                            ("g", "gpsimd"), ("s", "sync")):
            stream = self.streams[ename]
            if not stream:
                continue
            deco = getattr(block, attr)

            def body(eng, stream=stream):
                for fn in stream:
                    fn(eng)
            deco(body)


class Arena:

    def __init__(self, nc, stack, name, n_bf16):
        self.t = stack.enter_context(nc.sbuf_tensor(name, [128, n_bf16], BF16))
        self.n = n_bf16
        self.off = 0
        self.marks = []

    def save(self):
        self.marks.append(self.off)

    def restore(self):
        self.off = self.marks.pop()

    def bf16(self, cols):
        o = self.off
        assert o + cols <= self.n, f"arena overflow {o}+{cols}>{self.n}"
        self.off = o + cols
        return self.t[:, o:o + cols]

    def f32(self, cols):
        if self.off % 2:
            self.off += 1
        ap = self.bf16(2 * cols)
        return ap.bitcast(F32)



def _rotary_tables(pos):
    inv_freq = (1.0 / 10000.0 ** (np.arange(0, DH, 2) / DH)).astype(np.float64)
    p = np.arange(128)
    freq = inv_freq[p % 32]
    ang = freq[:, None] * pos[None, :].astype(np.float64)
    cos = np.cos(ang).astype(np.float32)
    sin = np.sin(ang).astype(np.float32)
    sign = np.where(p % 64 < 32, -1.0, 1.0).astype(np.float32)
    return cos, (sin * sign[:, None]).astype(np.float32)


def _perm_swap32():
    Pm = np.zeros((128, 128), dtype=np.float32)
    for p in range(128):
        k = (p & ~63) | ((p + 32) % 64)
        Pm[k, p] = 1.0
    return Pm.astype(BF)


def _l0_masks(cp):
    out = np.zeros((128, 12 * 512), dtype=np.float32)
    kl = np.arange(128)[:, None]
    ql = np.arange(512)[None, :]
    for s in range(2):
        for ci in range(6):
            d = ql + 256 - 128 * ci - kl
            ok = (d >= 0) & (d < WIN)
            if cp == 0 and s == 0:
                kv_abs = -256 + 128 * ci + kl + 0 * ql
                ok = ok & (kv_abs >= 0)
            out[:, (s * 6 + ci) * 512:(s * 6 + ci + 1) * 512] = ok
    return out.astype(BF)


L0_RANGES = [(max(0, -256 + 128 * ci), min(512, -256 + 128 * ci + 384)) for ci in range(6)]


def _pretile2(w, blk=128):
    i, o = w.shape
    t = np.ascontiguousarray(w.reshape(i, o // blk, blk).transpose(1, 0, 2))
    return t.reshape((o // blk) * i, blk)


def stage_p0(inputs):
    x = inputs["L0_attn_x"]
    wqkvT = np.ascontiguousarray(inputs["L0_attn_w_qkv"].T)
    woutT = np.ascontiguousarray(inputs["L0_attn_w_out"].T)
    wqkT = _pretile2(wqkvT[:, :2048])
    wvT = np.ascontiguousarray(wqkvT[:, 2048:])
    woutTt = _pretile2(woutT)
    sel = np.zeros((1, 256), dtype=np.float32)
    sel[0, 0:64] = 1.0
    sel[0, 192:256] = 1.0
    ones_row = np.ones((1, 128), dtype=np.float32)
    statcol = np.full((128, 1), 1.0 / D, dtype=np.float32).astype(BF)
    onecol = np.ones((128, 1), dtype=np.float32).astype(BF)
    perm = _perm_swap32()

    maps = []
    for c in range(NCORE):
        b, cp = c // 4, c % 4
        q0 = 2048 * cp
        xT = np.zeros((D, TK), dtype=np.float32)
        lo = q0 - HALO
        src_lo = max(0, lo)
        xT[:, src_lo - lo:] = x[b, src_lo:q0 + ROWS].T
        pos = np.maximum(np.arange(lo, q0 + ROWS), 0)
        cosf, sinf = _rotary_tables(pos)
        maps.append({
            "p0_xT": np.ascontiguousarray(xT),
            "p0_wqkT": wqkT, "p0_wvT": wvT, "p0_woutT": woutTt,
            "p0_cos": cosf, "p0_sin": sinf,
            "p0_perm": perm, "p0_mask": _l0_masks(cp),
            "c_sel": sel, "c_ones_row": ones_row,
            "c_statcol": statcol, "c_onecol": onecol,
            "c_eps": np.full((1, 1), 1e-5, dtype=np.float32),
        })
    return maps



def emit_p0(P, nc, A, pool, dram, out_xA):
    st = P.stack

    gl_bulk = P.new_dma_sem("p0glb", "g")
    sl_bulk = P.new_dma_sem("p0slb", "s")
    gl_slab = P.new_dma_sem("p0gls", "g")
    gl_wout = [P.new_dma_sem("p0glw0", "g"), P.new_dma_sem("p0glw1", "g")]
    sl_x0 = [P.new_dma_sem("p0slx0", "s"), P.new_dma_sem("p0slx1", "s")]
    so_p = [P.new_dma_sem("p0soa", "s"), P.new_dma_sem("p0sob", "s")]
    tA = P.new_sem("p0tA")
    tB = P.new_sem("p0tB")
    tC = P.new_sem("p0tC")
    tD = P.new_sem("p0tD")
    tE = P.new_sem("p0tE")
    tF = P.new_sem("p0tF")
    tG = P.new_sem("p0tG")
    tH = P.new_sem("p0tH")
    tV = P.new_sem("p0tV")
    tZ = P.new_sem("p0tZ")
    scA = P.new_sem("p0scA")
    scB = P.new_sem("p0scB")
    scC = P.new_sem("p0scC")
    scD = P.new_sem("p0scD")
    scE = P.new_sem("p0scE")
    scV = P.new_sem("p0scV")
    vA = P.new_sem("p0vA")
    vB = P.new_sem("p0vB")
    vC = P.new_sem("p0vC")
    vD = P.new_sem("p0vD")
    vE = P.new_sem("p0vE")
    vF = P.new_sem("p0vF")
    vG = P.new_sem("p0vG")
    vH = P.new_sem("p0vH")
    vS = P.new_sem("p0vS")
    scS = P.new_sem("p0scS")


    def slot(i, w=1):
        return pool[:, 512 * i:512 * (i + w)]
    pj = [slot(0), slot(1)]
    pv = slot(0, 2)
    psw = [slot(2), slot(3)]
    psc = [slot(4, 2), slot(6, 2)]
    pav = [slot(2), slot(3)]
    plp_e = pool[0:1, 0:512]
    plp_o = pool[0:1, 512:1024]
    plb = pool[:, 2048:2560]
    pz = slot(0)
    pst_m = pool[0:1, 0:512]
    pst_q = pool[0:1, 512:1024]
    pbr = slot(2, 2)

    attnf = A.bf16(8 * ROWS)
    maskb = A.bf16(12 * 512)
    permb = A.bf16(128)
    selb = A.f32(256)
    onesrow = A.f32(128)
    statcol = A.bf16(2)
    onecol = A.bf16(2)
    epsr = A.f32(1)
    linv = A.f32(1024)
    lbs = A.f32(512)

    A.save()
    xbf = A.bf16(8 * TK)
    wvb = A.bf16(8 * 1024)
    vbuf = A.bf16(18 * 1024)
    qkslab = [A.bf16(8 * 128) for _ in range(2)]
    qp = A.bf16(ROWS)
    kp = A.bf16(TK)
    qc = [A.bf16(512) for _ in range(2)]
    rt1 = [A.f32(512) for _ in range(2)]
    rt2 = [A.f32(512) for _ in range(2)]
    probs = A.bf16(12 * 512)
    cosb = A.f32(TK)
    sinb = A.f32(TK)

    for i in range(8):
        P.step("g", gl_bulk, lambda g, i=i: g.dma_start(
            out=xbf[:, i * TK:(i + 1) * TK], in_=dram["p0_xT"][i * 128:(i + 1) * 128, :]), 16)
    for i in range(8):
        P.step("g", gl_bulk, lambda g, i=i: g.dma_start(
            out=wvb[:, i * 1024:(i + 1) * 1024], in_=dram["p0_wvT"][i * 128:(i + 1) * 128, :]), 16)
    P.step("g", gl_bulk, lambda g: g.dma_start(out=permb[:, :], in_=dram["p0_perm"][:, :]), 16)
    P.step("g", gl_bulk, lambda g: g.dma_start(out=maskb[:, :], in_=dram["p0_mask"][:, :]), 16)
    P.step("g", gl_bulk, lambda g: g.dma_start(out=statcol[:, 0:1], in_=dram["c_statcol"][:, :]), 16)
    P.step("g", gl_bulk, lambda g: g.dma_start(out=onecol[:, 0:1], in_=dram["c_onecol"][:, :]), 16)
    P.step("s", sl_bulk, lambda s: s.dma_start(out=cosb[:, :], in_=dram["p0_cos"][:, :]), 16)
    P.step("s", sl_bulk, lambda s: s.dma_start(out=sinb[:, :], in_=dram["p0_sin"][:, :]), 16)
    P.step("s", sl_bulk, lambda s: s.dma_start(out=selb[0:1, :], in_=dram["c_sel"][:, :]), 16)
    P.step("s", sl_bulk, lambda s: s.dma_start(out=onesrow[0:1, :], in_=dram["c_ones_row"][:, :]), 16)
    P.step("s", sl_bulk, lambda s: s.dma_start(out=epsr[0:1, :], in_=dram["c_eps"][:, :]), 16)
    for e in ("t", "v", "sc"):
        P.wait(e, gl_bulk)
        P.wait(e, sl_bulk)

    P.step("v", vS, lambda v: v.memset(probs[:, :], 0.0))

    for tc in range(18):
        if tc >= 1:
            P.wait("t", scV, tc)
        for ic in range(8):
            for h in range(2):
                fn = lambda t, tc=tc, ic=ic, h=h: t.matmul(
                    pv[:, h * 512:(h + 1) * 512],
                    xbf[:, ic * TK + tc * 128: ic * TK + tc * 128 + 128],
                    wvb[:, ic * 1024 + h * 512: ic * 1024 + (h + 1) * 512],
                    start=(ic == 0), stop=(ic == 7))
                if ic == 7 and h == 1:
                    P.step("t", tV, fn)
                else:
                    P.emit("t", fn)
        P.wait("sc", tV, P.n(tV))
        P.step("sc", scV, lambda sc, tc=tc: sc.copy(
            out=vbuf[:, tc * 1024:(tc + 1) * 1024], in_=pv[:, :]))

    P.wait("t", scV, P.n(scV))
    tile_g = 0
    chunk_g = 0
    for hp in range(8):
        if hp >= 1:
            P.wait("g", tA, 9 * hp)
        P.step("g", gl_slab, lambda g, hp=hp: g.dma_start(
            out=qkslab[0].rearrange("p (c o) -> p c o", c=8),
            in_=dram["p0_wqkT"][hp * 1024:(hp + 1) * 1024, :].rearrange("(c p) o -> p c o", p=128)), 16)
        P.step("g", gl_slab, lambda g, hp=hp: g.dma_start(
            out=qkslab[1].rearrange("p (c o) -> p c o", c=8),
            in_=dram["p0_wqkT"][(hp + 8) * 1024:(hp + 9) * 1024, :].rearrange("(c p) o -> p c o", p=128)), 16)
        P.wait("t", gl_slab)

        P.wait("t", vD, P.n(vD))
        P.wait("t", vC, P.n(vC))
        tiles = []
        for i in range(4):
            tiles.append(("q", qp, 512 * i, 512, HALO + 512 * i))
        for i in range(5):
            w = 512 if i < 4 else 256
            tiles.append(("k", kp, 512 * i, w, 512 * i))
        for (kind, dest, dcol, w, xcol) in tiles:
            par = tile_g % 2
            slabi = 0 if kind == "q" else 1
            if tile_g >= 2:
                P.wait("t", vA, tile_g - 1)
            for ic in range(8):
                fn = lambda t, par=par, slabi=slabi, ic=ic, xcol=xcol, w=w: t.matmul(
                    pj[par][:, 0:w],
                    qkslab[slabi][:, ic * 128:(ic + 1) * 128],
                    xbf[:, ic * TK + xcol: ic * TK + xcol + w],
                    start=(ic == 0), stop=(ic == 7))
                if ic == 7:
                    P.step("t", tA, fn)
                else:
                    P.emit("t", fn)
            P.wait("sc", tA, P.n(tA))
            P.step("sc", scA, lambda sc, par=par, w=w: sc.copy(
                out=qc[par][:, 0:w], in_=pj[par][:, 0:w]))
            P.wait("t", scA, P.n(scA))
            P.step("t", tB, lambda t, par=par, w=w: t.matmul(
                psw[par][:, 0:w], permb[:, :], qc[par][:, 0:w], start=True, stop=True))
            P.wait("v", tA, P.n(tA))
            P.wait("v", tB, P.n(tB))
            if tile_g >= 2:
                P.wait("v", vA, tile_g - 1)
            P.step("v", vS, lambda v, par=par, w=w, xcol=xcol: v.tensor_mul(
                rt1[par][:, 0:w], pj[par][:, 0:w], cosb[:, xcol:xcol + w]))
            P.step("v", vS, lambda v, par=par, w=w, xcol=xcol: v.tensor_mul(
                rt2[par][:, 0:w], psw[par][:, 0:w], sinb[:, xcol:xcol + w]))
            P.wait("v", vS, P.n(vS))
            P.step("v", vA, lambda v, par=par, w=w, dest=dest, dcol=dcol: v.tensor_add(
                dest[:, dcol:dcol + w], rt1[par][:, 0:w], rt2[par][:, 0:w]))
            tile_g += 1

        rot_done = P.n(vA)
        P.wait("t", vA, rot_done)

        for qt in range(4):
            P.wait("t", vD, P.n(vD) - 1)
            mset = 0 if qt == 0 else 1
            for ci in range(6):
                lo, hi = L0_RANGES[ci]
                w = hi - lo
                kcol = 512 * qt + 128 * ci
                cp2 = chunk_g % 2
                if chunk_g >= 2:
                    P.wait("t", scB, 2 * (chunk_g - 1))
                if ci == 0:
                    P.wait("t", scC, P.n(scC))
                for par in range(2):
                    fn = lambda t, cp2=cp2, par=par, lo=lo, w=w, kcol=kcol, qt=qt: t.matmul(
                        psc[cp2][:, par * 512 + lo: par * 512 + lo + w],
                        kp[64 * par:64 * par + 64, kcol:kcol + 128],
                        qp[64 * par:64 * par + 64, 512 * qt + lo: 512 * qt + lo + w],
                        start=True, stop=True)
                    if par == 1:
                        P.step("t", tC, fn)
                    else:
                        P.emit("t", fn)
                P.wait("sc", tC, P.n(tC))
                P.step("sc", scB, lambda sc, cp2=cp2, ci=ci, lo=lo, w=w: sc.activation(
                    out=probs[:, (0 * 6 + ci) * 512 + lo: (0 * 6 + ci) * 512 + lo + w],
                    in_=psc[cp2][:, 0 * 512 + lo: 0 * 512 + lo + w], func=AF.Exp, scale=0.125))
                P.step("sc", scB, lambda sc, cp2=cp2, ci=ci, lo=lo, w=w: sc.activation(
                    out=probs[:, (1 * 6 + ci) * 512 + lo: (1 * 6 + ci) * 512 + lo + w],
                    in_=psc[cp2][:, 1 * 512 + lo: 1 * 512 + lo + w], func=AF.Exp, scale=0.125))
                P.wait("v", scB, P.n(scB))
                P.step("v", vB, lambda v, ci=ci, mset=mset: v.tensor_mul(
                    probs[:, (0 * 6 + ci) * 512: (0 * 6 + ci + 1) * 512],
                    probs[:, (0 * 6 + ci) * 512: (0 * 6 + ci + 1) * 512],
                    maskb[:, (mset * 6 + ci) * 512: (mset * 6 + ci + 1) * 512]))
                P.step("v", vB, lambda v, ci=ci, mset=mset: v.tensor_mul(
                    probs[:, (1 * 6 + ci) * 512: (1 * 6 + ci + 1) * 512],
                    probs[:, (1 * 6 + ci) * 512: (1 * 6 + ci + 1) * 512],
                    maskb[:, (mset * 6 + ci) * 512: (mset * 6 + ci + 1) * 512]))
                chunk_g += 1

            P.wait("t", vB, P.n(vB))
            P.wait("t", vC, P.n(vC))
            pavq = pav[qt % 2]
            for par in range(2):
                for ci in range(6):
                    vc = 4 * qt + ci
                    P.emit("t", lambda t, pavq=pavq, par=par, ci=ci, vc=vc, hp=hp: t.matmul(
                        pavq[64 * par:64 * par + 64, :],
                        vbuf[:, vc * 1024 + hp * 128 + par * 64: vc * 1024 + hp * 128 + par * 64 + 64],
                        probs[:, (par * 6 + ci) * 512: (par * 6 + ci + 1) * 512],
                        start=(ci == 0), stop=(ci == 5)))
            for par in range(2):
                plp = plp_e if par == 0 else plp_o
                for ci in range(6):
                    fn = lambda t, plp=plp, par=par, ci=ci: t.matmul(
                        plp[0:1, :], onecol[:, 0:1],
                        probs[:, (par * 6 + ci) * 512: (par * 6 + ci + 1) * 512],
                        start=(ci == 0), stop=(ci == 5))
                    if par == 1 and ci == 5:
                        P.step("t", tD, fn)
                    else:
                        P.emit("t", fn)
            P.wait("sc", tD, P.n(tD))
            P.step("sc", vC, lambda sc: sc.activation(
                out=linv[0:1, 0:512], in_=plp_e[0:1, :], func=AF.Ln))
            P.step("sc", vC, lambda sc: sc.activation(
                out=linv[0:1, 512:1024], in_=plp_o[0:1, :], func=AF.Ln))
            P.wait("sc", vC, P.n(vC))
            P.step("sc", vC, lambda sc: sc.activation(
                out=linv[0:1, 0:512], in_=linv[0:1, 0:512], func=AF.Exp, scale=-1.0))
            P.step("sc", vC, lambda sc: sc.activation(
                out=linv[0:1, 512:1024], in_=linv[0:1, 512:1024], func=AF.Exp, scale=-1.0))
            P.wait("t", vC, P.n(vC))
            P.wait("t", scC, P.n(scC))
            P.emit("t", lambda t: t.matmul(plb[:, :], selb[0:1, 0:128], linv[0:1, 0:512],
                                           start=True, stop=False))
            P.step("t", tE, lambda t: t.matmul(plb[:, :], selb[0:1, 128:256], linv[0:1, 512:1024],
                                               start=False, stop=True))
            P.wait("sc", tE, P.n(tE))
            P.step("sc", scC, lambda sc: sc.copy(out=lbs[:, :], in_=plb[:, :]))
            P.wait("v", scC, P.n(scC))
            P.step("v", vD, lambda v, pavq=pavq, hp=hp, qt=qt: v.tensor_mul(
                attnf[:, hp * ROWS + 512 * qt: hp * ROWS + 512 * qt + 512],
                pavq[:, :], lbs[:, :]))

    A.restore()
    zs = A.f32(8 * ROWS)
    woutb = [A.bf16(8 * 128) for _ in range(2)]
    x0t = [A.f32(512) for _ in range(2)]
    zb = A.bf16(512)
    zsq = A.bf16(512)
    stat_rows = A.f32(5 * 512)
    xAt = [A.f32(512) for _ in range(2)]

    all_attn = P.n(vD)
    P.wait("t", vD, all_attn)
    for e in ("s", "g", "v", "sc"):
        for sem in (vA, vB, vC, vD, vS, scA, scB, scC, scV):
            P.wait(e, sem)
    zi = 0
    for oc in range(8):
        if oc >= 2:
            P.wait("g", tF, 4 * (oc - 1))
        P.step("g", gl_wout[oc % 2], lambda g, oc=oc: g.dma_start(
            out=woutb[oc % 2].rearrange("p (c o) -> p c o", c=8),
            in_=dram["p0_woutT"][oc * 1024:(oc + 1) * 1024, :].rearrange("(c p) o -> p c o", p=128)), 16)
        P.wait("t", gl_wout[oc % 2])
        for qt in range(4):
            par = zi % 2
            if zi >= 2:
                P.wait("s", vE, zi - 1)
            P.step("s", sl_x0[par], lambda s, oc=oc, qt=qt, par=par: s.dma_start(
                out=x0t[par][:, :],
                in_=dram["p0_xT"][oc * 128:(oc + 1) * 128, HALO + 512 * qt: HALO + 512 * qt + 512]), 16)
            if zi >= 1:
                P.wait("t", vE, zi)
            for ic in range(8):
                fn = lambda t, ic=ic, qt=qt, oc=oc: t.matmul(
                    pz[:, :], woutb[oc % 2][:, ic * 128:(ic + 1) * 128],
                    attnf[:, ic * ROWS + 512 * qt: ic * ROWS + 512 * qt + 512],
                    start=(ic == 0), stop=(ic == 7))
                if ic == 7:
                    P.step("t", tF, fn)
                else:
                    P.emit("t", fn)
            P.wait("v", tF, P.n(tF))
            P.wait("v", sl_x0[par], P.n(sl_x0[par]))
            P.step("v", vE, lambda v, oc=oc, qt=qt, par=par: v.tensor_add(
                zs[:, oc * ROWS + 512 * qt: oc * ROWS + 512 * qt + 512],
                pz[:, :], x0t[par][:, :]))
            zi += 1


    z_done = P.n(vE)
    for e in ("t", "sc", "v"):
        P.wait(e, vE, z_done)
    ap_i = 0
    for qt in range(4):
        for oc in range(8):
            if oc >= 1:
                P.wait("v", tZ, P.n(tZ))
                P.wait("sc", tZ, P.n(tZ))
            P.step("v", vF, lambda v, oc=oc, qt=qt: v.tensor_copy(
                zb[:, :], zs[:, oc * ROWS + 512 * qt: oc * ROWS + 512 * qt + 512]))
            P.step("sc", scD, lambda sc, oc=oc, qt=qt: sc.activation(
                out=zsq[:, :], in_=zs[:, oc * ROWS + 512 * qt: oc * ROWS + 512 * qt + 512],
                func=AF.Square))
            P.wait("t", vF, P.n(vF))
            P.wait("t", scD, P.n(scD))
            P.emit("t", lambda t, oc=oc: t.matmul(
                pst_m[0:1, :], statcol[:, 0:1], zb[:, :], start=(oc == 0), stop=(oc == 7)))
            P.step("t", tZ, lambda t, oc=oc: t.matmul(
                pst_q[0:1, :], statcol[:, 0:1], zsq[:, :], start=(oc == 0), stop=(oc == 7)))
        P.wait("sc", tZ, 8 * (qt + 1))
        P.wait("sc", vG, P.n(vG))
        P.step("sc", scE, lambda sc: sc.copy(out=stat_rows[0:1, 0:512], in_=pst_m[0:1, :]))
        P.step("sc", scE, lambda sc: sc.activation(
            out=stat_rows[0:1, 512:1024], in_=pst_m[0:1, :], func=AF.Square))
        P.wait("v", scE, P.n(scE))
        P.step("v", vG, lambda v: v.tensor_sub(
            stat_rows[0:1, 1024:1536], pst_q[0:1, :], stat_rows[0:1, 512:1024]))
        P.wait("sc", vG, P.n(vG))
        P.step("sc", scS, lambda sc: sc.activation(
            out=stat_rows[0:1, 1536:2048], in_=stat_rows[0:1, 1024:1536],
            func=AF.Ln, bias=epsr[0:1, 0:1]))
        P.wait("sc", scS, P.n(scS))
        P.step("sc", scE, lambda sc: sc.activation(
            out=stat_rows[0:1, 1536:2048], in_=stat_rows[0:1, 1536:2048],
            func=AF.Exp, scale=-0.5))
        P.wait("v", scE, P.n(scE))
        P.step("v", vG, lambda v: v.tensor_mul(
            stat_rows[0:1, 2048:2560], stat_rows[0:1, 0:512], stat_rows[0:1, 1536:2048]))
        P.wait("t", vH, P.n(vH))
        P.wait("t", vG, P.n(vG))
        P.emit("t", lambda t: t.matmul(pbr[:, 0:512], onesrow[0:1, :],
                                       stat_rows[0:1, 1536:2048], start=True, stop=True))
        P.step("t", tH, lambda t: t.matmul(pbr[:, 512:1024], onesrow[0:1, :],
                                           stat_rows[0:1, 2048:2560], start=True, stop=True))
        P.wait("v", tH, P.n(tH))
        for oc in range(8):
            par = ap_i % 2
            if ap_i >= 2:
                P.wait("v", so_p[par], P.n(so_p[par]))
            P.step("v", vS, lambda v, oc=oc, qt=qt, par=par: v.tensor_mul(
                xAt[par][:, :], zs[:, oc * ROWS + 512 * qt: oc * ROWS + 512 * qt + 512],
                pbr[:, 0:512]))
            P.wait("v", vS, P.n(vS))
            P.step("v", vH, lambda v, par=par: v.tensor_sub(
                xAt[par][:, :], xAt[par][:, :], pbr[:, 512:1024]))
            P.wait("s", vH, P.n(vH))
            P.step("s", so_p[par], lambda s, oc=oc, qt=qt, par=par: s.dma_start(
                out=out_xA[oc * 128:(oc + 1) * 128, 512 * qt:512 * qt + 512],
                in_=xAt[par][:, :]), 16)
            ap_i += 1
    P.wait("s", so_p[0], P.n(so_p[0]))
    P.wait("s", so_p[1], P.n(so_p[1]))
    P.emit("s", lambda s: s.nop())
    return [so_p[0], so_p[1]]


def build_p0():
    nc = bass.Bass("TRN2", target_bir_lowering=False)
    dram = {
        "p0_xT": nc.dram_tensor("p0_xT", [D, TK], F32, kind="ExternalInput").ap(),
        "p0_wqkT": nc.dram_tensor("p0_wqkT", [16 * D, 128], F32, kind="ExternalInput").ap(),
        "p0_wvT": nc.dram_tensor("p0_wvT", [D, 1024], F32, kind="ExternalInput").ap(),
        "p0_woutT": nc.dram_tensor("p0_woutT", [8 * D, 128], F32, kind="ExternalInput").ap(),
        "p0_cos": nc.dram_tensor("p0_cos", [128, TK], F32, kind="ExternalInput").ap(),
        "p0_sin": nc.dram_tensor("p0_sin", [128, TK], F32, kind="ExternalInput").ap(),
        "p0_perm": nc.dram_tensor("p0_perm", [128, 128], BF16, kind="ExternalInput").ap(),
        "p0_mask": nc.dram_tensor("p0_mask", [128, 12 * 512], BF16, kind="ExternalInput").ap(),
        "c_sel": nc.dram_tensor("c_sel", [1, 256], F32, kind="ExternalInput").ap(),
        "c_ones_row": nc.dram_tensor("c_ones_row", [1, 128], F32, kind="ExternalInput").ap(),
        "c_statcol": nc.dram_tensor("c_statcol", [128, 1], BF16, kind="ExternalInput").ap(),
        "c_onecol": nc.dram_tensor("c_onecol", [128, 1], BF16, kind="ExternalInput").ap(),
        "c_eps": nc.dram_tensor("c_eps", [1, 1], F32, kind="ExternalInput").ap(),
    }
    out_xA = nc.dram_tensor("p0_xA", [D, ROWS], F32, kind="ExternalOutput").ap()
    stack = ExitStack()
    with stack:
        A = Arena(nc, stack, "arena", 104448)
        pool_t = stack.enter_context(nc.psum_tensor("pool", [128, 4096], F32))
        P = Plan(nc, stack)
        emit_p0(P, nc, A, pool_t, dram, out_xA)
        block = stack.enter_context(nc.Block())
        P.realize(block)
    return nc


def run_p0(inputs):
    maps = stage_p0(inputs)
    nc = build_p0()
    res = run_bass_kernel_spmd(nc, maps, core_ids=list(range(NCORE)))
    return [res.results[c]["p0_xA"] for c in range(NCORE)]



def emit_ln_fm(P, name, pool, bufs, n_sub, zsrc, sink, w=512, gb=None, sems=None):
    if sems is None:
        sems = {}
    def S(nm):
        if nm not in sems:
            sems[nm] = P.new_sem(name + nm)
        return sems[nm]
    tZ, tH, scE, scS2 = S("tZ"), S("tH"), S("scE"), S("scS2")
    vF, vG, vAp, vH = S("vF"), S("vG"), S("vAp"), S("vH")
    scGB = S("scGB") if gb is not None else None
    pst_m = pool[0:1, 0:w]
    pst_q = pool[0:1, 512:512 + w]
    pbr = pool[:, 1024:2048]
    zb, zsq, statr = bufs["zb"], bufs["zsq"], bufs["statr"]
    xnt = bufs["xnt"]
    statcol, onesrow, epsr = bufs["statcol"], bufs["onesrow"], bufs["epsr"]
    guards = sems.setdefault("_guards", [None, None])
    for sub in range(n_sub):
        for oc in range(8):
            if P.n(tZ) > 0:
                P.wait("v", tZ, P.n(tZ))
                P.wait("sc", tZ, P.n(tZ))
            P.step("v", vF, lambda v, oc=oc, sub=sub: v.tensor_copy(
                zb[:, 0:w], zsrc(oc, sub)))
            P.step("sc", scE, lambda sc, oc=oc, sub=sub: sc.activation(
                out=zsq[:, 0:w], in_=zsrc(oc, sub), func=AF.Square))
            P.wait("t", vF, P.n(vF))
            P.wait("t", scE, P.n(scE))
            P.emit("t", lambda t, oc=oc: t.matmul(
                pst_m[0:1, :], statcol[:, 0:1], zb[:, 0:w], start=(oc == 0), stop=(oc == 7)))
            P.step("t", tZ, lambda t, oc=oc: t.matmul(
                pst_q[0:1, :], statcol[:, 0:1], zsq[:, 0:w], start=(oc == 0), stop=(oc == 7)))
        P.wait("sc", tZ, P.n(tZ))
        P.wait("sc", vG, P.n(vG))
        P.step("sc", scE, lambda sc: sc.copy(out=statr[0:1, 0:w], in_=pst_m[0:1, :]))
        P.step("sc", scE, lambda sc: sc.activation(
            out=statr[0:1, 512:512 + w], in_=pst_m[0:1, :], func=AF.Square))
        P.wait("v", scE, P.n(scE))
        P.step("v", vG, lambda v: v.tensor_sub(
            statr[0:1, 1024:1024 + w], pst_q[0:1, :], statr[0:1, 512:512 + w]))
        P.wait("sc", vG, P.n(vG))
        P.step("sc", scS2, lambda sc: sc.activation(
            out=statr[0:1, 1536:1536 + w], in_=statr[0:1, 1024:1024 + w],
            func=AF.Ln, bias=epsr[0:1, 0:1]))
        P.wait("sc", scS2, P.n(scS2))
        P.step("sc", scE, lambda sc: sc.activation(
            out=statr[0:1, 1536:1536 + w], in_=statr[0:1, 1536:1536 + w],
            func=AF.Exp, scale=-0.5))
        P.wait("v", scE, P.n(scE))
        P.step("v", vG, lambda v: v.tensor_mul(
            statr[0:1, 2048:2048 + w], statr[0:1, 0:w], statr[0:1, 1536:1536 + w]))
        P.wait("t", vH, P.n(vH))
        if scGB is not None:
            P.wait("t", scGB, P.n(scGB))
        P.wait("t", vG, P.n(vG))
        P.emit("t", lambda t: t.matmul(pbr[:, 0:w], onesrow[0:1, :],
                                       statr[0:1, 1536:1536 + w], start=True, stop=True))
        P.step("t", tH, lambda t: t.matmul(pbr[:, 512:512 + w], onesrow[0:1, :],
                                           statr[0:1, 2048:2048 + w], start=True, stop=True))
        P.wait("v", tH, P.n(tH))
        for oc in range(8):
            par = sems.get("_appar", 0)
            sems["_appar"] = 1 - par
            if guards[par]:
                for (sname, v_) in guards[par]:
                    P.wait("v", sname, v_)
            P.step("v", vAp, lambda v, oc=oc, sub=sub, par=par: v.tensor_mul(
                xnt[par][:, 0:w], zsrc(oc, sub), pbr[:, 0:w]))
            P.wait("v", vAp, P.n(vAp))
            P.step("v", vH, lambda v, par=par: v.tensor_sub(
                xnt[par][:, 0:w], xnt[par][:, 0:w], pbr[:, 512:512 + w]))
            if gb is not None:
                gamma_ap, beta_ap = gb
                P.wait("sc", vH, P.n(vH))
                P.step("sc", scGB, lambda sc, oc=oc, par=par: sc.activation(
                    out=bufs["gbout"][par][:, 0:w], in_=xnt[par][:, 0:w], func=AF.Copy,
                    scale=gamma_ap(oc), bias=beta_ap(oc)))
                extra = [(scGB, P.n(scGB))]
            else:
                extra = []
            P.wait("v", vH, P.n(vH))
            guards[par] = extra + sink(sub, oc, par)
    return sems


def stage_p1(inputs):
    w1t = _pretile2(np.ascontiguousarray(inputs["L0_ffn_w1_w"].T))
    w2t = _pretile2(np.ascontiguousarray(inputs["L0_ffn_w2_w"].T))
    h1t = _pretile2(np.ascontiguousarray(inputs["L0_ffn_h1_w"].T))
    h2t = _pretile2(np.ascontiguousarray(inputs["L0_ffn_h2_w"].T))
    w3t = _pretile2(np.ascontiguousarray(inputs["L0_ffn_w3_w"].T))
    s1 = np.ascontiguousarray(inputs["L0_ffn_w1_s"].reshape(32, 128).T)
    s2 = np.ascontiguousarray(inputs["L0_ffn_w2_s"].reshape(32, 128).T)
    h1s = np.ascontiguousarray(inputs["L0_ffn_h1_s"].reshape(32, 128).T)
    h2s = np.ascontiguousarray(inputs["L0_ffn_h2_s"].reshape(32, 128).T)
    s3 = np.ascontiguousarray(inputs["L0_ffn_w3_s"].reshape(8, 128).T)
    wkt = _pretile2(np.ascontiguousarray(inputs["L1_attn_Wk"]))
    wv = np.ascontiguousarray(inputs["L1_attn_Wv"])
    shared = {
        "p1_w1t": w1t, "p1_w2t": w2t, "p1_h1t": h1t, "p1_h2t": h2t, "p1_w3t": w3t,
        "p1_s1": s1, "p1_s2": s2, "p1_h1s": h1s, "p1_h2s": h2s, "p1_s3": s3,
        "p1_wkt": wkt, "p1_wv": wv,
        "c_ones_row": np.ones((1, 128), dtype=np.float32),
        "c_statcol": np.full((128, 1), 1.0 / D, dtype=np.float32).astype(BF),
        "c_eps": np.full((1, 1), 1e-5, dtype=np.float32),
    }
    return [dict(shared) for _ in range(NCORE)]


def p1_dram_decls(nc):
    return {
        "p1_w1t": nc.dram_tensor("p1_w1t", [32 * 1024, 128], F32, kind="ExternalInput").ap(),
        "p1_w2t": nc.dram_tensor("p1_w2t", [32 * 1024, 128], F32, kind="ExternalInput").ap(),
        "p1_h1t": nc.dram_tensor("p1_h1t", [32 * 4096, 128], F32, kind="ExternalInput").ap(),
        "p1_h2t": nc.dram_tensor("p1_h2t", [32 * 4096, 128], F32, kind="ExternalInput").ap(),
        "p1_w3t": nc.dram_tensor("p1_w3t", [8 * 4096, 128], F32, kind="ExternalInput").ap(),
        "p1_s1": nc.dram_tensor("p1_s1", [128, 32], F32, kind="ExternalInput").ap(),
        "p1_s2": nc.dram_tensor("p1_s2", [128, 32], F32, kind="ExternalInput").ap(),
        "p1_h1s": nc.dram_tensor("p1_h1s", [128, 32], F32, kind="ExternalInput").ap(),
        "p1_h2s": nc.dram_tensor("p1_h2s", [128, 32], F32, kind="ExternalInput").ap(),
        "p1_s3": nc.dram_tensor("p1_s3", [128, 8], F32, kind="ExternalInput").ap(),
        "p1_wkt": nc.dram_tensor("p1_wkt", [8 * 1024, 128], F32, kind="ExternalInput").ap(),
        "p1_wv": nc.dram_tensor("p1_wv", [1024, 1024], F32, kind="ExternalInput").ap(),
    }


def emit_p1(P, nc, A, pool, dram, xa_src, out_xB, out_kf, out_vt, rt_done_cb=None):
    if not isinstance(out_xB, list):
        out_xB = [out_xB[:, 0:1024], out_xB[:, 1024:2048]]
        out_kf = [out_kf[:, 0:1024], out_kf[:, 1024:2048]]
        out_vt = [out_vt[0:1024, :], out_vt[1024:2048, :]]

    def slot(i, w=1):
        return pool[:, 512 * i:512 * (i + w)]

    gW = [P.new_dma_sem("p1gW0", "g"), P.new_dma_sem("p1gW1", "g")]
    gH = [P.new_dma_sem("p1gH0", "g"), P.new_dma_sem("p1gH1", "g")]
    gW3 = [P.new_dma_sem("p1gW30", "g"), P.new_dma_sem("p1gW31", "g")]
    gK = [P.new_dma_sem(f"p1gK{i}", "g") for i in range(4)]
    gV = P.new_dma_sem("p1gV", "g")
    gX = P.new_dma_sem("p1gX", "g")
    sX = [P.new_dma_sem("p1sX0", "s"), P.new_dma_sem("p1sX1", "s")]
    soX = [P.new_dma_sem("p1soX0", "s"), P.new_dma_sem("p1soX1", "s")]
    soK = [P.new_dma_sem("p1soK0", "s"), P.new_dma_sem("p1soK1", "s")]
    soV = [P.new_dma_sem("p1soV0", "s"), P.new_dma_sem("p1soV1", "s")]
    tA1 = P.new_sem("p1tA1")
    tA2 = P.new_sem("p1tA2")
    tA3 = P.new_sem("p1tA3")
    tA4 = P.new_sem("p1tA4")
    tK = P.new_sem("p1tK")
    tV2 = P.new_sem("p1tV2")
    sc1 = P.new_sem("p1sc1")
    sc2 = P.new_sem("p1sc2")
    sc3 = P.new_sem("p1sc3")
    scK = P.new_sem("p1scK")
    scV2 = P.new_sem("p1scV2")
    v1 = P.new_sem("p1v1")
    v3 = P.new_sem("p1v3")
    vKc = P.new_sem("p1vKc")

    xabf = A.bf16(8 * 1024)
    h0b = A.bf16(32 * 1024)
    h1b = A.bf16(32 * 1024)
    wbuf = [[A.bf16(8 * 128), A.bf16(8 * 128)] for _ in range(2)]
    hbuf = [A.bf16(32 * 128) for _ in range(2)]
    w3buf = hbuf
    wkbuf = [A.bf16(8 * 128) for _ in range(4)]
    wvb8 = h1b[:, 16384:24576]
    tsil = [A.f32(1024) for _ in range(2)]
    xat = [A.f32(1024) for _ in range(2)]
    zbuf = h1b[:, 0:16384].bitcast(F32)
    xbbf = xabf
    kft = [A.bf16(1024) for _ in range(2)]
    vtt = [A.bf16(1024) for _ in range(2)]
    s1b = A.f32(32)
    s2b = A.f32(32)
    h1sb = A.f32(32)
    h2sb = A.f32(32)
    s3b = A.f32(8)
    zb = h0b[:, 0:512]
    zsq = h0b[:, 512:1024]
    statr = h0b[:, 1024:6144].bitcast(F32)
    xnt = [h0b[:, 6144:7168].bitcast(F32), h0b[:, 7168:8192].bitcast(F32)]
    statcol = A.bf16(2)
    onesrow = A.f32(128)
    epsr = A.f32(1)

    sB = P.new_dma_sem("p1sB", "s")
    for name, buf, w in (("p1_s1", s1b, 32), ("p1_s2", s2b, 32), ("p1_h1s", h1sb, 32),
                         ("p1_h2s", h2sb, 32), ("p1_s3", s3b, 8)):
        P.step("s", sB, lambda s, name=name, buf=buf, w=w: s.dma_start(
            out=buf[:, 0:w], in_=dram[name][:, :]), 16)
    P.step("s", sB, lambda s: s.dma_start(out=onesrow[0:1, :], in_=dram["c_ones_row"][:, :]), 16)
    P.step("s", sB, lambda s: s.dma_start(out=epsr[0:1, :], in_=dram["c_eps"][:, :]), 16)
    P.step("g", gV, lambda g: g.dma_start(out=statcol[:, 0:1], in_=dram["c_statcol"][:, :]), 16)
    for e in ("t", "v", "sc"):
        P.wait(e, sB)
        P.wait(e, gV)

    def slab_load(chan_sems, unit, bufap, src_ap, gate_sem, gate_val, nblk):
        par = unit % len(chan_sems)
        if gate_val > 0:
            P.wait("g", gate_sem, gate_val)
        P.step("g", chan_sems[par], lambda g, bufap=bufap, src_ap=src_ap, nblk=nblk: g.dma_start(
            out=bufap.rearrange("p (c o) -> p c o", c=nblk),
            in_=src_ap.rearrange("(c p) o -> p c o", p=128)), 16)
        return chan_sems[par], P.n(chan_sems[par])

    ln_guard_prev = None
    for rt in range(2):
        col0 = 1024 * rt
        if rt == 1:
            P.wait("g", tA1, 32 * rt)
            P.wait("g", tV2, P.n(tV2))
            P.wait("g", scK, P.n(scK))
        for ic in range(8):
            P.step("g", gX, lambda g, ic=ic, col0=col0: g.dma_start(
                out=xabf[:, ic * 1024:(ic + 1) * 1024],
                in_=xa_src[ic * 128:(ic + 1) * 128, col0:col0 + 1024]), 16)
        P.wait("t", gX)

        if rt == 1:
            P.wait("v", sc3, P.n(sc3))
            P.wait("t", scV2, P.n(scV2))
            P.wait("t", scK, P.n(scK))
            for sem in (ln_guard_prev or []) + [vKc, soX[0], soX[1]]:
                P.wait("v", sem, P.n(sem))
                P.wait("sc", sem, P.n(sem))
        for ob in range(32):
            unit = rt * 32 + ob
            par = ob % 2
            slab_load(gW, unit, wbuf[par][0],
                      dram["p1_w1t"][ob * 1024:(ob + 1) * 1024, :], tA1, unit - 1, 8)
            P.step("g", gW[par], lambda g, par=par, ob=ob: g.dma_start(
                out=wbuf[par][1].rearrange("p (c o) -> p c o", c=8),
                in_=dram["p1_w2t"][ob * 1024:(ob + 1) * 1024, :].rearrange("(c p) o -> p c o", p=128)), 16)
            P.wait("t", gW[par], P.n(gW[par]))
            if unit >= 2:
                P.wait("t", v1, unit - 1)
            pp1 = slot(4 * par, 2)
            pp2 = slot(4 * par + 2, 2)
            for w_i, pp in ((0, pp1), (1, pp2)):
                for ic in range(8):
                    for h in range(2):
                        fn = lambda t, w_i=w_i, pp=pp, ic=ic, h=h, par=par: t.matmul(
                            pp[:, h * 512:(h + 1) * 512],
                            wbuf[par][w_i][:, ic * 128:(ic + 1) * 128],
                            xabf[:, ic * 1024 + h * 512: ic * 1024 + (h + 1) * 512],
                            start=(ic == 0), stop=(ic == 7))
                        if w_i == 1 and ic == 7 and h == 1:
                            P.step("t", tA1, fn)
                        else:
                            P.emit("t", fn)
            P.wait("sc", tA1, P.n(tA1))
            P.step("sc", sc1, lambda sc, par=par, ob=ob, pp1=pp1: sc.activation(
                out=tsil[par][:, :], in_=pp1[:, :], func=AF.Silu,
                scale=s1b[:, ob:ob + 1]))
            P.wait("v", sc1, P.n(sc1))
            P.step("v", v1, lambda v, par=par, ob=ob, pp2=pp2: v.scalar_tensor_tensor(
                h0b[:, ob * 1024:(ob + 1) * 1024], pp2[:, :], s2b[:, ob:ob + 1],
                tsil[par][:, :], ALU.mult, ALU.mult))

        P.wait("t", v1, P.n(v1))
        for ob in range(32):
            unit = rt * 32 + ob
            par = ob % 2
            if ob < 2 and rt == 1:
                P.wait("g", tA3, P.n(tA3))
                _, wv_ = slab_load(gH, unit, hbuf[par],
                                   dram["p1_h1t"][ob * 4096:(ob + 1) * 4096, :], tA2, 0, 32)
            else:
                _, wv_ = slab_load(gH, unit, hbuf[par],
                                   dram["p1_h1t"][ob * 4096:(ob + 1) * 4096, :], tA2, unit - 1, 32)
            P.wait("t", gH[par], wv_)
            if unit >= 2:
                P.wait("t", sc2, unit - 1)
            ph = slot(4 * (par % 2), 2)
            for ic in range(32):
                for h in range(2):
                    fn = lambda t, ph=ph, ic=ic, h=h, par=par: t.matmul(
                        ph[:, h * 512:(h + 1) * 512],
                        hbuf[par][:, ic * 128:(ic + 1) * 128],
                        h0b[:, ic * 1024 + h * 512: ic * 1024 + (h + 1) * 512],
                        start=(ic == 0), stop=(ic == 31))
                    if ic == 31 and h == 1:
                        P.step("t", tA2, fn)
                    else:
                        P.emit("t", fn)
            P.wait("sc", tA2, P.n(tA2))
            P.step("sc", sc2, lambda sc, ob=ob, ph=ph: sc.activation(
                out=h1b[:, ob * 1024:(ob + 1) * 1024], in_=ph[:, :], func=AF.Gelu,
                scale=h1sb[:, ob:ob + 1]))

        P.wait("t", sc2, P.n(sc2))
        for ob in range(32):
            unit = rt * 32 + ob
            par = ob % 2
            if ob < 2:
                P.wait("g", tA2, P.n(tA2))
                _, wv_ = slab_load(gH, 64 + unit, hbuf[par],
                                   dram["p1_h2t"][ob * 4096:(ob + 1) * 4096, :], tA3, 0, 32)
            else:
                _, wv_ = slab_load(gH, 64 + unit, hbuf[par],
                                   dram["p1_h2t"][ob * 4096:(ob + 1) * 4096, :], tA3, unit - 1, 32)
            P.wait("t", gH[par], wv_)
            if unit >= 2:
                P.wait("t", sc3, unit - 1)
            ph = slot(4 * (par % 2), 2)
            for ic in range(32):
                for h in range(2):
                    fn = lambda t, ph=ph, ic=ic, h=h, par=par: t.matmul(
                        ph[:, h * 512:(h + 1) * 512],
                        hbuf[par][:, ic * 128:(ic + 1) * 128],
                        h1b[:, ic * 1024 + h * 512: ic * 1024 + (h + 1) * 512],
                        start=(ic == 0), stop=(ic == 31))
                    if ic == 31 and h == 1:
                        P.step("t", tA3, fn)
                    else:
                        P.emit("t", fn)
            P.wait("sc", tA3, P.n(tA3))
            P.step("sc", sc3, lambda sc, ob=ob, ph=ph: sc.activation(
                out=h0b[:, ob * 1024:(ob + 1) * 1024], in_=ph[:, :], func=AF.Gelu,
                scale=h2sb[:, ob:ob + 1]))

        P.wait("t", sc3, P.n(sc3))
        for ob in range(8):
            unit = rt * 8 + ob
            par = ob % 2
            if ob < 2:
                P.wait("g", tA3, P.n(tA3))
                _, wv3_ = slab_load(gW3, unit, w3buf[par],
                                    dram["p1_w3t"][ob * 4096:(ob + 1) * 4096, :], tA4, 0, 32)
            else:
                _, wv3_ = slab_load(gW3, unit, w3buf[par],
                                    dram["p1_w3t"][ob * 4096:(ob + 1) * 4096, :], tA4, unit - 1, 32)
            if unit >= 2:
                P.wait("s", v3, unit - 1)
            P.step("s", sX[par], lambda s, ob=ob, par=par, col0=col0: s.dma_start(
                out=xat[par][:, :], in_=xa_src[ob * 128:(ob + 1) * 128, col0:col0 + 1024]), 16)
            P.wait("t", gW3[par], wv3_)
            if unit >= 2:
                P.wait("t", v3, unit - 1)
            py = slot(4 * (par % 2), 2)
            for ic in range(32):
                for h in range(2):
                    fn = lambda t, py=py, ic=ic, h=h, par=par: t.matmul(
                        py[:, h * 512:(h + 1) * 512],
                        w3buf[par][:, ic * 128:(ic + 1) * 128],
                        h0b[:, ic * 1024 + h * 512: ic * 1024 + (h + 1) * 512],
                        start=(ic == 0), stop=(ic == 31))
                    if ic == 31 and h == 1:
                        P.step("t", tA4, fn)
                    else:
                        P.emit("t", fn)
            P.wait("v", tA4, P.n(tA4))
            P.wait("v", sX[par], P.n(sX[par]))
            P.step("v", v3, lambda v, ob=ob, par=par, py=py: v.scalar_tensor_tensor(
                zbuf[:, ob * 1024:(ob + 1) * 1024], py[:, :], s3b[:, ob:ob + 1],
                xat[par][:, :], ALU.mult, ALU.add))

        P.wait("t", v3, P.n(v3))
        P.wait("sc", v3, P.n(v3))
        P.wait("v", v3, P.n(v3))

        st_i = [0]

        def sink(sub, oc, par, rt=rt, st_i=st_i):
            P.step("v", vKc, lambda v, sub=sub, oc=oc, par=par: v.tensor_copy(
                xbbf[:, oc * 1024 + sub * 512: oc * 1024 + sub * 512 + 512],
                xnt[par][:, :]))
            P.wait("s", vKc, P.n(vKc))
            spar = st_i[0] % 2
            P.step("s", soX[spar], lambda s, sub=sub, oc=oc, par=par, rt=rt: s.dma_start(
                out=out_xB[rt][oc * 128:(oc + 1) * 128, sub * 512: sub * 512 + 512],
                in_=xnt[par][:, :]), 16)
            st_i[0] += 1
            return [(vKc, P.n(vKc)), (soX[spar], P.n(soX[spar]))]

        _lnsems = emit_ln_fm(P, f"p1ln{rt}", pool, {
            "zb": zb, "zsq": zsq, "statr": statr, "xnt": xnt,
            "statcol": statcol, "onesrow": onesrow, "epsr": epsr,
        }, 2, lambda oc, sub: zbuf[:, oc * 1024 + sub * 512: oc * 1024 + sub * 512 + 512],
            sink)
        ln_guard_prev = [v for v in _lnsems.values() if isinstance(v, str)]

        P.wait("g", tA3, P.n(tA3))
        for ic in range(8):
            P.step("g", gV, lambda g, ic=ic: g.dma_start(
                out=wvb8[:, ic * 1024:(ic + 1) * 1024],
                in_=dram["p1_wv"][ic * 128:(ic + 1) * 128, :]), 16)
        P.wait("t", gV)
        P.wait("t", vKc, P.n(vKc))
        for ob in range(8):
            unit = rt * 8 + ob
            bpar = unit % 4
            par = ob % 2
            _, wk_ = slab_load(gK, unit, wkbuf[bpar],
                               dram["p1_wkt"][ob * 1024:(ob + 1) * 1024, :], tK, unit - 3, 8)
            P.wait("t", gK[bpar], wk_)
            if unit >= 2:
                P.wait("t", scK, unit - 1)
            pk = slot(4 * par, 2)
            for ic in range(8):
                for h in range(2):
                    fn = lambda t, pk=pk, ic=ic, h=h, bpar=bpar: t.matmul(
                        pk[:, h * 512:(h + 1) * 512],
                        wkbuf[bpar][:, ic * 128:(ic + 1) * 128],
                        xbbf[:, ic * 1024 + h * 512: ic * 1024 + (h + 1) * 512],
                        start=(ic == 0), stop=(ic == 7))
                    if ic == 7 and h == 1:
                        P.step("t", tK, fn)
                    else:
                        P.emit("t", fn)
            P.wait("sc", tK, P.n(tK))
            if unit >= 2:
                P.wait("sc", soK[par], P.n(soK[par]))
            P.step("sc", scK, lambda sc, par=par, pk=pk: sc.copy(
                out=kft[par][:, :], in_=pk[:, :]))
            P.wait("s", scK, P.n(scK))
            P.step("s", soK[par], lambda s, ob=ob, par=par, rt=rt: s.dma_start(
                out=out_kf[rt][ob * 128:(ob + 1) * 128, :],
                in_=kft[par][:, :]), 16)
        P.wait("t", scK, P.n(scK))
        for tc in range(8):
            unit = rt * 8 + tc
            par = tc % 2
            if unit >= 2:
                P.wait("t", scV2, unit - 1)
            pvv = slot(4 * par, 2)
            for ic in range(8):
                for h in range(2):
                    fn = lambda t, pvv=pvv, ic=ic, h=h, tc=tc: t.matmul(
                        pvv[:, h * 512:(h + 1) * 512],
                        xbbf[:, ic * 1024 + tc * 128: ic * 1024 + tc * 128 + 128],
                        wvb8[:, ic * 1024 + h * 512: ic * 1024 + (h + 1) * 512],
                        start=(ic == 0), stop=(ic == 7))
                    if ic == 7 and h == 1:
                        P.step("t", tV2, fn)
                    else:
                        P.emit("t", fn)
            P.wait("sc", tV2, P.n(tV2))
            if unit >= 2:
                P.wait("sc", soV[par], P.n(soV[par]))
            P.step("sc", scV2, lambda sc, par=par, pvv=pvv: sc.copy(
                out=vtt[par][:, :], in_=pvv[:, :]))
            P.wait("s", scV2, P.n(scV2))
            P.step("s", soV[par], lambda s, tc=tc, par=par, rt=rt: s.dma_start(
                out=out_vt[rt][tc * 128: tc * 128 + 128, :],
                in_=vtt[par][:, :]), 16)
        if rt_done_cb is not None:
            rt_done_cb(rt, [soX[0], soX[1], soK[0], soK[1], soV[0], soV[1]])

    for sem in (soX[0], soX[1], soK[0], soK[1], soV[0], soV[1]):
        P.wait("s", sem, P.n(sem))
    P.emit("s", lambda s: s.nop())
    return [soX[0], soX[1], soK[0], soK[1], soV[0], soV[1]]



def build_p1():
    nc = bass.Bass("TRN2", target_bir_lowering=False)
    dram = p1_dram_decls(nc)
    dram["p1_xA"] = nc.dram_tensor("p1_xA", [D, ROWS], F32, kind="ExternalInput").ap()
    dram["c_ones_row"] = nc.dram_tensor("c_ones_row", [1, 128], F32, kind="ExternalInput").ap()
    dram["c_statcol"] = nc.dram_tensor("c_statcol", [128, 1], BF16, kind="ExternalInput").ap()
    dram["c_eps"] = nc.dram_tensor("c_eps", [1, 1], F32, kind="ExternalInput").ap()
    out_xB = nc.dram_tensor("p1_xB", [D, ROWS], F32, kind="ExternalOutput").ap()
    out_kf = nc.dram_tensor("p1_kf", [D, ROWS], BF16, kind="ExternalOutput").ap()
    out_vt = nc.dram_tensor("p1_vt", [ROWS, D], BF16, kind="ExternalOutput").ap()
    stack = ExitStack()
    with stack:
        A = Arena(nc, stack, "arena", 104448)
        pool_t = stack.enter_context(nc.psum_tensor("pool", [128, 4096], F32))
        P = Plan(nc, stack)
        emit_p1(P, nc, A, pool_t, dram, dram["p1_xA"], out_xB, out_kf, out_vt)
        block = stack.enter_context(nc.Block())
        P.realize(block)
    return nc


def run_p1(inputs, xA_shards):
    maps = stage_p1(inputs)
    for c in range(NCORE):
        maps[c]["p1_xA"] = np.ascontiguousarray(xA_shards[c])
    nc = build_p1()
    res = run_bass_kernel_spmd(nc, maps, core_ids=list(range(NCORE)))
    return ([res.results[c]["p1_xB"] for c in range(NCORE)],
            [res.results[c]["p1_kf"] for c in range(NCORE)],
            [res.results[c]["p1_vt"] for c in range(NCORE)])



def p3_perm_rows(cp):
    rows = []
    for m in range(8):
        for j in (8 * m + cp, 8 * m + 7 - cp):
            rows.extend(range(128 * j, 128 * j + 128))
    return np.array(rows)


def _p3_masks(cp):
    out = np.zeros((128, 64 * 128), dtype=np.float32)
    kl = np.arange(128)[:, None]
    ql = np.arange(128)[None, :]
    for m in range(8):
        for k in range(8):
            c = 8 * m + k
            j = 8 * m + cp if k < 4 else 8 * m + 7 - cp
            if c < j:
                v = np.ones((128, 128), dtype=np.float32)
            elif c == j:
                v = (ql >= kl).astype(np.float32)
            else:
                v = np.zeros((128, 128), dtype=np.float32)
            out[:, (m * 8 + k) * 128:(m * 8 + k + 1) * 128] = v
    return out.astype(BF)


def stage_p3(inputs):
    wqt = _pretile2(np.ascontiguousarray(inputs["L1_attn_Wq"]))
    wvt2 = _pretile2(np.ascontiguousarray(inputs["L1_attn_Wv"]))
    wfct = _pretile2(np.ascontiguousarray(inputs["L1_attn_Wfc"]))
    bfc = np.ascontiguousarray(inputs["L1_attn_bfc"].reshape(8, 128).T)
    gam = np.ascontiguousarray(inputs["L1_attn_gamma"].reshape(8, 128).T)
    bet = np.ascontiguousarray(inputs["L1_attn_beta"].reshape(8, 128).T)
    shared = {
        "p3_wqt": wqt, "p3_wvt": wvt2, "p3_wfct": wfct,
        "p3_bfc": bfc, "p3_gam": gam, "p3_bet": bet,
        "c_ones_row": np.ones((1, 128), dtype=np.float32),
        "c_statcol": np.full((128, 1), 1.0 / D, dtype=np.float32).astype(BF),
        "c_onecol": np.ones((128, 1), dtype=np.float32).astype(BF),
        "c_eps": np.full((1, 1), 1e-5, dtype=np.float32),
    }
    maps = []
    for c in range(NCORE):
        cp = c % 4
        m = dict(shared)
        m["p3_mask"] = _p3_masks(cp)
        maps.append(m)
    return maps


def p3_dram_decls(nc):
    d = {
        "p3_wqt": nc.dram_tensor("p3_wqt", [8 * 1024, 128], F32, kind="ExternalInput").ap(),
        "p3_wvt": nc.dram_tensor("p3_wvt", [8 * 1024, 128], F32, kind="ExternalInput").ap(),
        "p3_wfct": nc.dram_tensor("p3_wfct", [8 * 1024, 128], F32, kind="ExternalInput").ap(),
        "p3_bfc": nc.dram_tensor("p3_bfc", [128, 8], F32, kind="ExternalInput").ap(),
        "p3_gam": nc.dram_tensor("p3_gam", [128, 8], F32, kind="ExternalInput").ap(),
        "p3_bet": nc.dram_tensor("p3_bet", [128, 8], F32, kind="ExternalInput").ap(),
        "p3_mask": nc.dram_tensor("p3_mask", [128, 64 * 128], BF16, kind="ExternalInput").ap(),
    }
    return d


def emit_p3(P, nc, A, pool, dram, xbi_src, kf_src, vt_src, out_xC,
            fused=False, xb_ag=None, idx_tab=None, ag_waits=None):
    if fused:
        xb_view = [h.rearrange("a (n o) -> (a n) o", o=128) for h in xb_ag]

    def slot(i, w=1):
        return pool[:, 512 * i:512 * (i + w)]

    gB = P.new_dma_sem("p3gB", "g")
    sB = P.new_dma_sem("p3sB", "s")
    gX = P.new_dma_sem("p3gX", "g")
    gKV = [P.new_dma_sem(f"p3gKV{i}", "s") for i in range(8)]
    sxb = [P.new_dma_sem("p3sxb0", "g" if fused else "s"),
           P.new_dma_sem("p3sxb1", "g" if fused else "s")]
    soC = [P.new_dma_sem("p3soC0", "s"), P.new_dma_sem("p3soC1", "s")]
    tQ = P.new_sem("p3tQ")
    tS = P.new_sem("p3tS")
    tS2 = P.new_sem("p3tS2")
    tAv = P.new_sem("p3tAv")
    tLb = P.new_sem("p3tLb")
    tFc = P.new_sem("p3tFc")
    scQ = P.new_sem("p3scQ")
    scE2 = P.new_sem("p3scE2")
    scLb = P.new_sem("p3scLb")
    scFc = P.new_sem("p3scFc")
    vM = P.new_sem("p3vM")
    vC2 = P.new_sem("p3vC2")
    vZ = P.new_sem("p3vZ")
    vZf = P.new_sem("p3vZf")
    vZ2 = P.new_sem("p3vZ2")

    xbibf = A.bf16(8 * 2048)
    qf = A.bf16(8 * 2048)
    vf = A.bf16(8 * 2048)
    qwb = A.bf16(64 * 128)
    vwb = A.bf16(64 * 128)
    fwb = A.bf16(64 * 128)
    maskb = A.bf16(64 * 128)
    kvb = [vwb[:, i * 1024:(i + 1) * 1024] for i in range(8)]
    vvb = [qwb[:, i * 1024:(i + 1) * 1024] for i in range(8)]
    probs = [A.bf16(256) for _ in range(2)]
    linv = A.f32(256)
    linbs = A.f32(256)
    zslot = A.f32(8 * 256)
    z2slot = zslot
    lnb = A.bf16(8 * 256)
    xbt = [A.f32(256) for _ in range(2)]
    bfcb = A.f32(8)
    gamb = A.f32(8)
    betb = A.f32(8)
    zb = A.bf16(512)
    zsq = A.bf16(512)
    statr = A.f32(2304)
    xnt = [A.f32(512) for _ in range(2)]
    statcol = A.bf16(2)
    onecol = A.bf16(2)
    onesrow = A.f32(128)
    epsr = A.f32(1)

    for name, buf in (("p3_wqt", qwb), ("p3_wvt", vwb), ("p3_wfct", fwb)):
        for ob in range(8):
            P.step("g", gB, lambda g, name=name, buf=buf, ob=ob: g.dma_start(
                out=buf[:, ob * 1024:(ob + 1) * 1024].rearrange("p (c o) -> p c o", c=8),
                in_=dram[name][ob * 1024:(ob + 1) * 1024, :].rearrange("(c p) o -> p c o", p=128)), 16)
    P.step("g", gB, lambda g: g.dma_start(out=maskb[:, :], in_=dram["p3_mask"][:, :]), 16)
    P.step("g", gB, lambda g: g.dma_start(out=statcol[:, 0:1], in_=dram["c_statcol"][:, :]), 16)
    P.step("g", gB, lambda g: g.dma_start(out=onecol[:, 0:1], in_=dram["c_onecol"][:, :]), 16)
    if fused:
        idxg = A.bf16(256)
        idxgu = idxg.bitcast(U32)
        P.step("g", gX, lambda g: g.dma_start(out=idxgu[:, :], in_=idx_tab[:, :]), 16)
        P.wait("g", gX, P.n(gX))
        if ag_waits:
            for s_, v_ in ag_waits["xb"]:
                P.wait("g", s_, v_)
        for ic in range(8):
            for s_ in range(16):
                hh_ = (s_ // 2) % 2
                P.step("g", gX, lambda g, ic=ic, s_=s_, hh_=hh_: g.indirect_dma_start(
                    out=xbibf[:, ic * 2048 + s_ * 128: ic * 2048 + s_ * 128 + 128],
                    out_offset=None, in_=xb_view[hh_][:, :],
                    in_offset=IndirectOffsetOnAxis(ap=idxgu[:, (s_ * 8 + ic):(s_ * 8 + ic) + 1], axis=0)), 16)
    else:
        for ic in range(8):
            P.step("g", gX, lambda g, ic=ic: g.dma_start(
                out=xbibf[:, ic * 2048:(ic + 1) * 2048],
                in_=xbi_src[ic * 128:(ic + 1) * 128, :]), 16)
    for name, buf in (("p3_bfc", bfcb), ("p3_gam", gamb), ("p3_bet", betb)):
        P.step("s", sB, lambda s, name=name, buf=buf: s.dma_start(
            out=buf[:, 0:8], in_=dram[name][:, :]), 16)
    P.step("s", sB, lambda s: s.dma_start(out=onesrow[0:1, :], in_=dram["c_ones_row"][:, :]), 16)
    P.step("s", sB, lambda s: s.dma_start(out=epsr[0:1, :], in_=dram["c_eps"][:, :]), 16)
    for e in ("t", "v", "sc"):
        P.wait(e, gB)
        P.wait(e, gX)
        P.wait(e, sB)

    for proj, wb, dst, dstf in (("q", qwb, qf, None), ("v", vwb, None, vf)):
        for ob in range(8):
            for tt in range(4):
                unit = P.n(tQ)
                par = unit % 2
                if unit >= 2:
                    P.wait("t", scQ, unit - 1)
                pp = slot(2 * par)
                for ic in range(8):
                    fn = lambda t, pp=pp, wb=wb, ic=ic, ob=ob, tt=tt: t.matmul(
                        pp[:, :], wb[:, ob * 1024 + ic * 128: ob * 1024 + (ic + 1) * 128],
                        xbibf[:, ic * 2048 + tt * 512: ic * 2048 + tt * 512 + 512],
                        start=(ic == 0), stop=(ic == 7))
                    if ic == 7:
                        P.step("t", tQ, fn)
                    else:
                        P.emit("t", fn)
                P.wait("sc", tQ, P.n(tQ))
                if dst is not None:
                    P.step("sc", scQ, lambda sc, pp=pp, ob=ob, tt=tt: sc.copy(
                        out=qf[:, ob * 2048 + tt * 512: ob * 2048 + tt * 512 + 512],
                        in_=pp[:, :]))
                else:
                    P.step("sc", scQ, lambda sc, pp=pp, ob=ob, tt=tt: sc.copy(
                        out=vf[:, ob * 2048 + tt * 512: ob * 2048 + tt * 512 + 512],
                        in_=pp[:, :]))

    lnsems = {}
    lnbufs = {"zb": zb, "zsq": zsq, "statr": statr, "xnt": xnt,
              "statcol": statcol, "onesrow": onesrow, "epsr": epsr}
    kv_unit = [0]
    ch_g = [0]
    for m in range(8):
        if m > 0:
            for s_ in ("vH", "scE"):
                if s_ in lnsems:
                    P.wait("t", lnsems[s_], P.n(lnsems[s_]))
            P.wait("t", scLb, P.n(scLb))
            P.wait("t", vZ2, P.n(vZ2))
        nch = 8 * m + 8
        ncommon = 8 * m + 4
        pend = []

        def emit_av(ent):
            (kpar_, ppar_, c_, common_, vm_target, sce_target) = ent
            if vm_target is not None:
                P.wait("t", vM, vm_target)
            else:
                P.wait("t", scE2, sce_target)
            for j in range(4):
                for half in range(2):
                    dv = 2 * j + half
                    P.emit("t", lambda t, j=j, half=half, dv=dv, kpar=kpar_, ppar=ppar_, c=c_, nch=nch: t.matmul(
                        slot(j)[:, half * 256: half * 256 + 256],
                        vvb[kpar][:, dv * 128:(dv + 1) * 128],
                        probs[ppar][:, 0:256],
                        start=(c == 0 and half == 0), stop=(c == nch - 1 and half == 1)))
            P.step("t", tS2, lambda t, ppar=ppar_, c=c_, nch=nch: t.matmul(
                pool[0:1, 3072:3328], onecol[:, 0:1], probs[ppar][:, 0:256],
                start=(c == 0), stop=(c == nch - 1)))

        for c in range(nch):
            common = c < ncommon
            n = 256
            qoff = 256 * m
            u = kv_unit[0]
            par = u % 8
            if u < 8:
                P.wait("s", scQ, P.n(scQ))
                if fused and ag_waits and u == 0:
                    for s_, v_ in ag_waits["kv"]:
                        P.wait("s", s_, v_)
            if u >= 8:
                P.wait("s", tS2, u - 7)
            if fused:
                r_, hh_ = c // 16, (c % 16) // 8
                lc_ = (c % 16) - 8 * hh_
                P.step("s", gKV[par], lambda s, par=par, r_=r_, hh_=hh_, lc_=lc_: s.dma_start(
                    out=kvb[par].rearrange("p (c1 o) -> p c1 o", c1=8),
                    in_=kf_src[hh_][r_ * 1024:(r_ + 1) * 1024, :]
                    .rearrange("(c1 p) kv -> p c1 kv", p=128)[:, :, lc_ * 128:(lc_ + 1) * 128]), 16)
            else:
                P.step("s", gKV[par], lambda s, par=par, c=c: s.dma_start(
                    out=kvb[par].rearrange("p (c1 o) -> p c1 o", c1=8),
                    in_=kf_src.rearrange("(c1 p) kv -> p c1 kv", p=128)[:, :, c * 128:(c + 1) * 128]), 16)
            if fused:
                r_, hh_ = c // 16, (c % 16) // 8
                rih_ = 128 * (c % 16) - 1024 * hh_
                P.step("s", gKV[par], lambda s, par=par, r_=r_, hh_=hh_, rih_=rih_: s.dma_start(
                    out=vvb[par][:, :],
                    in_=vt_src[hh_][r_ * 1024 + rih_: r_ * 1024 + rih_ + 128, :]), 16)
            else:
                P.step("s", gKV[par], lambda s, par=par, c=c: s.dma_start(
                    out=vvb[par][:, :], in_=vt_src[c * 128:(c + 1) * 128, :]), 16)
            kvw = P.n(gKV[par])
            kv_unit[0] += 1
            P.wait("t", gKV[par], kvw)
            if ch_g[0] >= 2:
                P.wait("t", scE2, ch_g[0] - 1)
            psc = pool[:, 2048 + 512 * (c % 2): 2048 + 512 * (c % 2) + 256]
            for ic in range(8):
                fn = lambda t, psc=psc, par=par, ic=ic, qoff=qoff, n=n: t.matmul(
                    psc[:, 0:n], kvb[par][:, ic * 128:(ic + 1) * 128],
                    qf[:, ic * 2048 + qoff: ic * 2048 + qoff + n],
                    start=(ic == 0), stop=(ic == 7))
                if ic == 7:
                    P.step("t", tS, fn)
                else:
                    P.emit("t", fn)
            ppar = ch_g[0] % 2
            P.wait("sc", tS, P.n(tS))
            P.step("sc", scE2, lambda sc, psc=psc, ppar=ppar, n=n: sc.activation(
                out=probs[ppar][:, 0:n], in_=psc[:, 0:n], func=AF.Exp, scale=1.0 / 32.0))
            need_v = False
            if common and 8 * m <= c < 8 * m + 4:
                mi = m * 8 + (c - 8 * m)
                P.wait("v", scE2, P.n(scE2))
                P.step("v", vM, lambda v, ppar=ppar, mi=mi: v.tensor_mul(
                    probs[ppar][:, 0:128], probs[ppar][:, 0:128],
                    maskb[:, mi * 128:(mi + 1) * 128]))
                need_v = True
            elif not common:
                mi = m * 8 + 4 + (c - ncommon)
                P.wait("v", scE2, P.n(scE2))
                P.step("v", vM, lambda v, ppar=ppar: v.memset(probs[ppar][:, 0:128], 0.0))
                P.step("v", vM, lambda v, ppar=ppar, mi=mi: v.tensor_mul(
                    probs[ppar][:, 128:256], probs[ppar][:, 128:256],
                    maskb[:, mi * 128:(mi + 1) * 128]))
                need_v = True
            ent = (par, ppar, c, common, P.n(vM) if need_v else None, P.n(scE2))
            pend.append(ent)
            if len(pend) >= 2:
                emit_av(pend.pop(0))
            ch_g[0] += 1
        while pend:
            emit_av(pend.pop(0))
        P.wait("sc", tS2, P.n(tS2))
        P.step("sc", vC2, lambda sc: sc.activation(
            out=linv[0:1, :], in_=pool[0:1, 3072:3328], func=AF.Ln))
        P.wait("sc", vC2, P.n(vC2))
        P.step("sc", vC2, lambda sc: sc.activation(
            out=linv[0:1, :], in_=linv[0:1, :], func=AF.Exp, scale=-1.0))
        P.wait("t", vC2, P.n(vC2))
        P.step("t", tLb, lambda t: t.matmul(
            pool[:, 3328:3584], onesrow[0:1, :], linv[0:1, :], start=True, stop=True))
        P.wait("sc", tLb, P.n(tLb))
        P.step("sc", scLb, lambda sc: sc.copy(out=linbs[:, :], in_=pool[:, 3328:3584]))
        P.wait("v", scLb, P.n(scLb))
        for oc in range(8):
            P.step("v", vZ, lambda v, oc=oc, m=m: v.tensor_mul(
                zslot[:, oc * 256:(oc + 1) * 256],
                slot(oc // 2)[:, (oc % 2) * 256:(oc % 2) * 256 + 256], linbs[:, :]))
            P.wait("v", vZ, P.n(vZ))
            P.step("v", vZf, lambda v, oc=oc, m=m: v.tensor_add(
                zslot[:, oc * 256:(oc + 1) * 256],
                zslot[:, oc * 256:(oc + 1) * 256],
                vf[:, oc * 2048 + m * 256: oc * 2048 + m * 256 + 256]))
        def sink1b(sub, oc, par, m=m):
            P.step("v", vM, lambda v, oc=oc, par=par: v.tensor_copy(
                lnb[:, oc * 256:(oc + 1) * 256], xnt[par][:, 0:256]))
            return [(vM, P.n(vM))]

        P.wait("v", vZf, P.n(vZf))
        P.wait("t", vZf, P.n(vZf))
        P.wait("sc", vZf, P.n(vZf))
        emit_ln_fm(P, "p3ln", pool, lnbufs, 1,
                   lambda oc, sub, m=m: zslot[:, oc * 256:(oc + 1) * 256],
                   sink1b, w=256, gb=None, sems=lnsems)
        P.wait("v", vM, P.n(vM))
        for oc in range(8):
            P.step("v", vM, lambda v, oc=oc: v.tensor_scalar(
                lnb[:, oc * 256:(oc + 1) * 256], lnb[:, oc * 256:(oc + 1) * 256],
                gamb[:, oc:oc + 1], betb[:, oc:oc + 1], ALU.mult, ALU.add))
        P.wait("t", vM, P.n(vM))
        for ob in range(8):
            unit = P.n(tFc)
            par = unit % 2
            if fused:
                if unit >= 2:
                    P.wait("g", vZ2, P.n(vZ2) - 1)
                for hh in range(2):
                    P.step("g", sxb[par], lambda g, ob=ob, par=par, m=m, hh=hh: g.indirect_dma_start(
                        out=xbt[par][:, hh * 128:(hh + 1) * 128], out_offset=None,
                        in_=xb_view[m % 2][:, :],
                        in_offset=IndirectOffsetOnAxis(
                            ap=idxgu[:, ((2 * m + hh) * 8 + ob):((2 * m + hh) * 8 + ob) + 1], axis=0)), 16)
            else:
                if unit >= 2:
                    P.wait("s", vZ2, P.n(vZ2) - 1)
                P.step("s", sxb[par], lambda s, ob=ob, par=par, m=m: s.dma_start(
                    out=xbt[par][:, :],
                    in_=xbi_src[ob * 128:(ob + 1) * 128, m * 256:(m + 1) * 256]), 16)
            pfc = pool[:, 3584:3840] if (unit % 2 == 0) else pool[:, 3328:3584]
            if P.n(vZ2) >= 2:
                P.wait("t", vZ2, P.n(vZ2) - 1)
            for ic in range(8):
                fn = lambda t, pfc=pfc, ic=ic, ob=ob: t.matmul(
                    pfc[:, :], fwb[:, ob * 1024 + ic * 128: ob * 1024 + (ic + 1) * 128],
                    lnb[:, ic * 256:(ic + 1) * 256], start=(ic == 0), stop=(ic == 7))
                if ic == 7:
                    P.step("t", tFc, fn)
                else:
                    P.emit("t", fn)
            P.wait("v", tFc, P.n(tFc))
            P.wait("v", sxb[par], P.n(sxb[par]))
            P.step("v", vZ2, lambda v, ob=ob, par=par, pfc=pfc: v.scalar_tensor_tensor(
                z2slot[:, ob * 256:(ob + 1) * 256], pfc[:, :], bfcb[:, ob:ob + 1],
                xbt[par][:, :], ALU.add, ALU.add))
        P.wait("v", vZ2, P.n(vZ2))
        P.wait("t", vZ2, P.n(vZ2))
        P.wait("sc", vZ2, P.n(vZ2))

        def sink2(sub, oc, par, m=m):
            spar = (m * 8 + oc) % 2
            P.wait("s", lnsems["vH"], P.n(lnsems["vH"]))
            P.step("s", soC[spar], lambda s, oc=oc, par=par, m=m: s.dma_start(
                out=out_xC[oc * 128:(oc + 1) * 128, m * 256:(m + 1) * 256],
                in_=xnt[par][:, 0:256]), 16)
            return [(soC[spar], P.n(soC[spar]))]

        emit_ln_fm(P, "p3ln", pool, lnbufs, 1,
                   lambda oc, sub, m=m: z2slot[:, oc * 256:(oc + 1) * 256],
                   sink2, w=256, gb=None, sems=lnsems)
    P.wait("s", soC[0], P.n(soC[0]))
    P.wait("s", soC[1], P.n(soC[1]))
    P.emit("s", lambda s: s.nop())
    return [soC[0], soC[1]]


def build_p3():
    nc = bass.Bass("TRN2", target_bir_lowering=False)
    dram = p3_dram_decls(nc)
    dram["c_ones_row"] = nc.dram_tensor("c_ones_row", [1, 128], F32, kind="ExternalInput").ap()
    dram["c_statcol"] = nc.dram_tensor("c_statcol", [128, 1], BF16, kind="ExternalInput").ap()
    dram["c_onecol"] = nc.dram_tensor("c_onecol", [128, 1], BF16, kind="ExternalInput").ap()
    dram["c_eps"] = nc.dram_tensor("c_eps", [1, 1], F32, kind="ExternalInput").ap()
    xbi = nc.dram_tensor("p3_xbi", [D, ROWS], F32, kind="ExternalInput").ap()
    kf = nc.dram_tensor("p3_kf", [D, T], BF16, kind="ExternalInput").ap()
    vt = nc.dram_tensor("p3_vt", [T, D], BF16, kind="ExternalInput").ap()
    out_xC = nc.dram_tensor("p3_xC", [D, ROWS], F32, kind="ExternalOutput").ap()
    stack = ExitStack()
    with stack:
        A = Arena(nc, stack, "arena", 104448)
        pool_t = stack.enter_context(nc.psum_tensor("pool", [128, 4096], F32))
        P = Plan(nc, stack)
        emit_p3(P, nc, A, pool_t, dram, xbi, kf, vt, out_xC)
        block = stack.enter_context(nc.Block())
        P.realize(block)
    return nc


def run_p3(inputs, xbi_shards, kf_full, vt_full):
    maps = stage_p3(inputs)
    for c in range(NCORE):
        b = c // 4
        maps[c]["p3_xbi"] = np.ascontiguousarray(xbi_shards[c])
        maps[c]["p3_kf"] = np.ascontiguousarray(kf_full[b])
        maps[c]["p3_vt"] = np.ascontiguousarray(vt_full[b])
    nc = build_p3()
    res = run_bass_kernel_spmd(nc, maps, core_ids=list(range(NCORE)))
    return [res.results[c]["p3_xC"] for c in range(NCORE)]



def stage_p4(inputs):
    wd1 = inputs["L1_ffn_Wd1"]
    wd2 = inputs["L1_ffn_Wd2"]
    wd1t = np.concatenate([_pretile2(np.ascontiguousarray(wd1[e])) for e in range(8)], axis=0)
    wd2t = np.concatenate([_pretile2(np.ascontiguousarray(wd2[e])) for e in range(8)], axis=0)
    bd1 = np.ascontiguousarray(inputs["L1_ffn_bd1"].reshape(8, 2, 128).transpose(2, 0, 1).reshape(128, 16))
    bd2 = np.ascontiguousarray(inputs["L1_ffn_bd2"].reshape(8, 8, 128).transpose(2, 0, 1).reshape(128, 64))
    wbt = _pretile2(np.ascontiguousarray(inputs["L1_ffn_W_base"]))
    bb = np.ascontiguousarray(inputs["L1_ffn_b_base"].reshape(8, 128).T)
    wad = np.ascontiguousarray(inputs["L1_ffn_Wa_down"].reshape(8 * 1024, 64))
    bad = np.ascontiguousarray(inputs["L1_ffn_ba_down"].reshape(8 * 64, 1))
    wau = np.ascontiguousarray(inputs["L1_ffn_Wa_up"].reshape(8 * 64, 1024))
    bau = np.ascontiguousarray(
        inputs["L1_ffn_ba_up"].reshape(8, 8, 128).transpose(0, 2, 1).reshape(8 * 128, 8))
    iota128 = np.arange(128, dtype=np.float32).reshape(128, 1)
    erow = np.arange(8, dtype=np.float32).reshape(1, 8)
    shared = {
        "p4_wd1t": wd1t, "p4_wd2t": wd2t, "p4_bd1": bd1, "p4_bd2": bd2,
        "p4_wbt": wbt, "p4_bb": bb, "p4_wad": wad, "p4_bad": bad,
        "p4_wau": wau, "p4_bau": bau, "p4_iota": iota128, "p4_erow": erow,
        "c_ones_row": np.ones((1, 128), dtype=np.float32),
        "c_statcol": np.full((128, 1), 1.0 / D, dtype=np.float32).astype(BF),
        "c_onecol": np.ones((128, 1), dtype=np.float32).astype(BF),
        "c_eps": np.full((1, 1), 1e-5, dtype=np.float32),
    }
    return [dict(shared) for _ in range(NCORE)]


def emit_p4a(P, nc, A, pool, dram, xci_src, out_loss):
    def slot(i, w=1):
        return pool[:, 512 * i:512 * (i + w)]
    gB = P.new_dma_sem("p4gB", "g")
    sB = P.new_dma_sem("p4sB", "s")
    gW1 = [P.new_dma_sem("p4gW10", "g"), P.new_dma_sem("p4gW11", "g")]
    gW2 = [P.new_dma_sem("p4gW20", "g"), P.new_dma_sem("p4gW21", "g")]
    tH1 = P.new_sem("p4tH1")
    tR = P.new_sem("p4tR")
    tL = P.new_sem("p4tL")
    scH = P.new_sem("p4scH")
    scS = P.new_sem("p4scS")
    vD2 = P.new_sem("p4vD2")
    vL = P.new_sem("p4vL")
    so4 = P.new_dma_sem("p4so", "s")

    xcibf = A.bf16(8 * 2048)
    xcif = A.f32(8 * 2048)
    w1buf = [A.bf16(16 * 128) for _ in range(2)]
    w2buf = [A.bf16(16 * 128) for _ in range(2)]
    heb = A.bf16(2 * 2048)
    bd1b = A.f32(16)
    bd2b = A.f32(64)
    diff = [A.f32(512) for _ in range(4)]
    dsq = [A.bf16(512) for _ in range(4)]
    lrow = A.f32(8)
    onecol = A.bf16(2)

    for ic in range(8):
        P.step("g", gB, lambda g, ic=ic: g.dma_start(
            out=xcibf[:, ic * 2048:(ic + 1) * 2048],
            in_=xci_src[ic * 128:(ic + 1) * 128, :]), 16)
    P.step("g", gB, lambda g: g.dma_start(out=onecol[:, 0:1], in_=dram["c_onecol"][:, :]), 16)
    for ic in range(8):
        P.step("s", sB, lambda s, ic=ic: s.dma_start(
            out=xcif[:, ic * 2048:(ic + 1) * 2048],
            in_=xci_src[ic * 128:(ic + 1) * 128, :]), 16)
    P.step("s", sB, lambda s: s.dma_start(out=bd1b[:, 0:16], in_=dram["p4_bd1"][:, :]), 16)
    P.step("s", sB, lambda s: s.dma_start(out=bd2b[:, 0:64], in_=dram["p4_bd2"][:, :]), 16)
    for e in ("t", "v", "sc"):
        P.wait(e, gB)
        P.wait(e, sB)

    for e in range(8):
        par = e % 2
        if e >= 2:
            P.wait("g", tH1, 8 * (e - 1))
        P.step("g", gW1[par], lambda g, e=e, par=par: g.dma_start(
            out=w1buf[par].rearrange("p (c o) -> p c o", c=16),
            in_=dram["p4_wd1t"][e * 2048:(e + 1) * 2048, :].rearrange("(c p) o -> p c o", p=128)), 16)
        w1v = P.n(gW1[par])
        if e >= 2:
            P.wait("g", tR, 32 * (e - 1))
        P.step("g", gW2[par], lambda g, e=e, par=par: g.dma_start(
            out=w2buf[par].rearrange("p (c o) -> p c o", c=16),
            in_=dram["p4_wd2t"][e * 2048:(e + 1) * 2048, :].rearrange("(c p) o -> p c o", p=128)), 16)
        w2v = P.n(gW2[par])
        P.wait("t", gW1[par], w1v)
        for hb in range(2):
            for tt in range(4):
                hu = P.n(tH1)
                if hu >= 2:
                    P.wait("t", scH, hu - 1)
                ph = slot(hu % 2)
                for ic in range(8):
                    fn = lambda t, ph=ph, par=par, hb=hb, ic=ic, tt=tt: t.matmul(
                        ph[:, :], w1buf[par][:, (hb * 8 + ic) * 128:(hb * 8 + ic + 1) * 128],
                        xcibf[:, ic * 2048 + tt * 512: ic * 2048 + tt * 512 + 512],
                        start=(ic == 0), stop=(ic == 7))
                    if ic == 7:
                        P.step("t", tH1, fn)
                    else:
                        P.emit("t", fn)
                P.wait("sc", tH1, P.n(tH1))
                P.step("sc", scH, lambda sc, e=e, hb=hb, tt=tt, ph=ph: sc.activation(
                    out=heb[:, hb * 2048 + tt * 512: hb * 2048 + tt * 512 + 512],
                    in_=ph[:, :], func=AF.Relu, bias=bd1b[:, (e * 2 + hb):(e * 2 + hb) + 1]))
        P.wait("t", gW2[par], w2v)
        P.wait("t", scH, P.n(scH))
        pend4 = []

        def emit_loss(ent, e=e):
            dpar_, u32_, scs_t, first = ent
            if first:
                P.wait("t", vL, P.n(vL))
            P.wait("t", scS, scs_t)
            P.step("t", tL, lambda t, dpar=dpar_, u32=u32_: t.matmul(
                pool[0:1, 3584:4096], onecol[:, 0:1], dsq[dpar][:, :],
                start=(u32 == 0), stop=(u32 == 31)))

        for ob in range(8):
            for tt in range(4):
                ru = P.n(tR)
                u32 = ru % 32
                dpar = ru % 4
                if ru >= 4:
                    P.wait("t", vD2, ru - 3)
                    P.wait("v", scS, ru - 3)
                    P.wait("sc", tL, ru - 3)
                pr = slot(2 + dpar)
                for hc in range(2):
                    fn = lambda t, pr=pr, par=par, ob=ob, hc=hc, tt=tt: t.matmul(
                        pr[:, :], w2buf[par][:, (ob * 2 + hc) * 128:(ob * 2 + hc + 1) * 128],
                        heb[:, hc * 2048 + tt * 512: hc * 2048 + tt * 512 + 512],
                        start=(hc == 0), stop=(hc == 1))
                    if hc == 1:
                        P.step("t", tR, fn)
                    else:
                        P.emit("t", fn)
                P.wait("v", tR, P.n(tR))
                P.step("v", vD2, lambda v, e=e, ob=ob, tt=tt, pr=pr, dpar=dpar: v.scalar_tensor_tensor(
                    diff[dpar][:, :], pr[:, :], bd2b[:, (e * 8 + ob):(e * 8 + ob) + 1],
                    xcif[:, ob * 2048 + tt * 512: ob * 2048 + tt * 512 + 512],
                    ALU.add, ALU.subtract))
                P.wait("sc", vD2, P.n(vD2))
                P.step("sc", scS, lambda sc, dpar=dpar: sc.activation(
                    out=dsq[dpar][:, :], in_=diff[dpar][:, :], func=AF.Square))
                pend4.append((dpar, u32, P.n(scS), u32 == 0))
                if len(pend4) > 2:
                    emit_loss(pend4.pop(0))
        while pend4:
            emit_loss(pend4.pop(0))
        P.wait("v", tL, P.n(tL))
        P.step("v", vL, lambda v, e=e: v.reduce_sum(
            lrow[0:1, e:e + 1], pool[0:1, 3584:4096], axis=AX))
    P.wait("s", vL, P.n(vL))
    P.step("s", so4, lambda s: s.dma_start(out=out_loss[:, :], in_=lrow[0:1, 0:8]), 16)
    P.wait("s", so4, P.n(so4))
    P.emit("s", lambda s: s.nop())
    return [so4]


def emit_p4b(P, nc, A, pool, dram, xci_src, parts_src, out_xD, ag_wait=None):
    def slot(i, w=1):
        return pool[:, 512 * i:512 * (i + w)]
    gB = P.new_dma_sem("q4gB", "g")
    sB = P.new_dma_sem("q4sB", "s")
    gG = P.new_dma_sem("q4gG", "g")
    tB2 = P.new_sem("q4tB2")
    tA5 = P.new_sem("q4tA5")
    tY = P.new_sem("q4tY")
    scB2 = P.new_sem("q4scB2")
    scA5 = P.new_sem("q4scA5")
    vI = P.new_sem("q4vI")
    vY = P.new_sem("q4vY")
    so5 = [P.new_dma_sem("q4so5a", "s"), P.new_dma_sem("q4so5b", "s")]

    xcibf = A.bf16(8 * 2048)
    xcif = A.f32(8 * 2048)
    wbb = A.bf16(64 * 128)
    wadb = A.bf16(8 * 64)
    waub = A.bf16(1024)
    badb = A.f32(1)
    baub = A.f32(8)
    bbb = A.f32(8)
    cbias = A.f32(8)
    abuf = A.bf16(2048)
    parts = A.f32(32)
    lsum = A.f32(24)
    erowb = A.f32(8)
    iotab = A.f32(1)
    fxb = A.f32(1)
    idxt = A.bf16(6)
    zbuf = A.f32(8 * 2048)
    zb = A.bf16(512)
    zsq = A.bf16(512)
    statr = A.f32(2560)
    xnt = [A.f32(512) for _ in range(2)]
    statcol = A.bf16(2)
    onesrow = A.f32(128)
    epsr = A.f32(1)

    idxu = idxt.bitcast(U32)

    for ic in range(8):
        P.step("g", gB, lambda g, ic=ic: g.dma_start(
            out=xcibf[:, ic * 2048:(ic + 1) * 2048],
            in_=xci_src[ic * 128:(ic + 1) * 128, :]), 16)
    for ob in range(8):
        P.step("g", gB, lambda g, ob=ob: g.dma_start(
            out=wbb[:, ob * 1024:(ob + 1) * 1024].rearrange("p (c o) -> p c o", c=8),
            in_=dram["p4_wbt"][ob * 1024:(ob + 1) * 1024, :].rearrange("(c p) o -> p c o", p=128)), 16)
    P.step("g", gB, lambda g: g.dma_start(out=statcol[:, 0:1], in_=dram["c_statcol"][:, :]), 16)
    for ic in range(8):
        P.step("s", sB, lambda s, ic=ic: s.dma_start(
            out=xcif[:, ic * 2048:(ic + 1) * 2048],
            in_=xci_src[ic * 128:(ic + 1) * 128, :]), 16)
    P.step("s", sB, lambda s: s.dma_start(out=bbb[:, 0:8], in_=dram["p4_bb"][:, :]), 16)
    P.step("s", sB, lambda s: s.dma_start(out=erowb[0:1, 0:8], in_=dram["p4_erow"][:, :]), 16)
    P.step("s", sB, lambda s: s.dma_start(out=iotab[:, 0:1], in_=dram["p4_iota"][:, :]), 16)
    P.step("s", sB, lambda s: s.dma_start(out=onesrow[0:1, :], in_=dram["c_ones_row"][:, :]), 16)
    P.step("s", sB, lambda s: s.dma_start(out=epsr[0:1, :], in_=dram["c_eps"][:, :]), 16)
    if ag_wait is not None:
        P.wait("s", ag_wait[0], ag_wait[1])
    if isinstance(parts_src, list):
        for j in range(4):
            P.step("s", sB, lambda s, j=j: s.dma_start(
                out=parts[0:1, 8 * j:8 * j + 8], in_=parts_src[j]), 16)
    else:
        P.step("s", sB, lambda s: s.dma_start(out=parts[0:1, 0:32], in_=parts_src[:, :]), 16)
    for e in ("t", "v", "sc", "g"):
        P.wait(e, gB)
        P.wait(e, sB)

    P.step("v", vI, lambda v: v.tensor_add(lsum[0:1, 0:8], parts[0:1, 0:8], parts[0:1, 8:16]))
    P.wait("v", vI, P.n(vI))
    P.step("v", vI, lambda v: v.tensor_add(lsum[0:1, 8:16], parts[0:1, 16:24], parts[0:1, 24:32]))
    P.wait("v", vI, P.n(vI))
    P.step("v", vI, lambda v: v.tensor_add(lsum[0:1, 0:8], lsum[0:1, 0:8], lsum[0:1, 8:16]))
    P.wait("v", vI, P.n(vI))
    P.step("v", vI, lambda v: v.tensor_reduce(
        lsum[0:1, 8:9], lsum[0:1, 0:8], AX, ALU.min))
    P.wait("v", vI, P.n(vI))
    P.step("v", vI, lambda v: v.tensor_scalar(
        lsum[0:1, 16:24], lsum[0:1, 0:8], lsum[0:1, 8:9], None, ALU.is_equal))
    P.wait("v", vI, P.n(vI))
    P.step("v", vI, lambda v: v.tensor_mul(lsum[0:1, 16:24], lsum[0:1, 16:24], erowb[0:1, 0:8]))
    P.wait("v", vI, P.n(vI))
    P.step("v", vI, lambda v: v.reduce_sum(fxb[0:1, 0:1], lsum[0:1, 16:24], axis=AX))
    P.wait("t", vI, P.n(vI))
    P.step("t", tB2, lambda t: t.matmul(pool[:, 3584:3585], onesrow[0:1, :], fxb[0:1, 0:1],
                                        start=True, stop=True))
    P.wait("sc", tB2, P.n(tB2))
    P.step("sc", scB2, lambda sc: sc.copy(out=fxb[:, 0:1] if False else lsum[:, 0:1], in_=pool[:, 3584:3585]))
    P.wait("v", scB2, P.n(scB2))
    P.step("v", vI, lambda v: v.scalar_tensor_tensor(
        idxu[:, 0:1], lsum[:, 0:1], 1024.0, iotab[:, 0:1], ALU.mult, ALU.add))
    P.wait("v", vI, P.n(vI))
    P.step("v", vI, lambda v: v.scalar_tensor_tensor(
        idxu[:, 1:2], lsum[:, 0:1], 64.0, iotab[:, 0:1], ALU.mult, ALU.add))
    P.wait("v", vI, P.n(vI))
    P.step("v", vI, lambda v: v.scalar_tensor_tensor(
        idxu[:, 2:3], lsum[:, 0:1], 128.0, iotab[:, 0:1], ALU.mult, ALU.add))
    P.wait("v", vI, P.n(vI))
    P.step("v", vI, lambda v: v.memset(waub[:, :], 0.0))
    P.wait("v", vI, P.n(vI))
    P.step("v", vI, lambda v: v.memset(abuf[:, :], 0.0))
    P.wait("g", vI, P.n(vI))
    for ic in range(8):
        P.step("g", gG, lambda g, ic=ic: g.indirect_dma_start(
            out=wadb[:, ic * 64:(ic + 1) * 64], out_offset=None,
            in_=dram["p4_wad"][:, :],
            in_offset=IndirectOffsetOnAxis(ap=idxu[:, 0:1], axis=0),
            element_offset=ic * 128 * 64), 16)
    P.step("g", gG, lambda g: g.indirect_dma_start(
        out=waub[0:64, :], out_offset=None, in_=dram["p4_wau"][:, :],
        in_offset=IndirectOffsetOnAxis(ap=idxu[0:64, 1:2], axis=0)), 16)
    P.step("g", gG, lambda g: g.indirect_dma_start(
        out=badb[0:64, :], out_offset=None, in_=dram["p4_bad"][:, :],
        in_offset=IndirectOffsetOnAxis(ap=idxu[0:64, 1:2], axis=0)), 16)
    P.step("g", gG, lambda g: g.indirect_dma_start(
        out=baub[:, 0:8], out_offset=None, in_=dram["p4_bau"][:, :],
        in_offset=IndirectOffsetOnAxis(ap=idxu[:, 2:3], axis=0)), 16)
    for e in ("t", "v", "sc"):
        P.wait(e, gG)

    for tt in range(4):
        pa = slot(4 + (tt % 2))
        if tt >= 2:
            P.wait("t", scA5, tt - 1)
        for ic in range(8):
            fn = lambda t, pa=pa, ic=ic, tt=tt: t.matmul(
                pa[0:64, :], wadb[:, ic * 64:(ic + 1) * 64],
                xcibf[:, ic * 2048 + tt * 512: ic * 2048 + tt * 512 + 512],
                start=(ic == 0), stop=(ic == 7))
            if ic == 7:
                P.step("t", tA5, fn)
            else:
                P.emit("t", fn)
        P.wait("sc", tA5, P.n(tA5))
        P.step("sc", scA5, lambda sc, tt=tt, pa=pa: sc.activation(
            out=abuf[0:64, tt * 512:(tt + 1) * 512], in_=pa[0:64, :], func=AF.Relu,
            bias=badb[0:64, 0:1]))
    P.wait("v", sB, P.n(sB))
    P.step("v", vI, lambda v: v.tensor_add(cbias[:, 0:8], bbb[:, 0:8], baub[:, 0:8]))
    P.wait("t", scA5, P.n(scA5))
    P.wait("t", vI, P.n(vI))
    for ob in range(8):
        for tt in range(4):
            yu = P.n(tY)
            py = slot(yu % 4)
            if yu >= 4:
                P.wait("t", vY, yu - 3)
            for ic in range(8):
                P.emit("t", lambda t, py=py, ic=ic, ob=ob, tt=tt: t.matmul(
                    py[:, :], wbb[:, ob * 1024 + ic * 128: ob * 1024 + (ic + 1) * 128],
                    xcibf[:, ic * 2048 + tt * 512: ic * 2048 + tt * 512 + 512],
                    start=(ic == 0), stop=False))
            fn = lambda t, py=py, ob=ob, tt=tt: t.matmul(
                py[:, :], waub[:, ob * 128:(ob + 1) * 128],
                abuf[:, tt * 512:(tt + 1) * 512], start=False, stop=True)
            P.step("t", tY, fn)
            P.wait("v", tY, P.n(tY))
            P.step("v", vY, lambda v, py=py, ob=ob, tt=tt: v.scalar_tensor_tensor(
                zbuf[:, ob * 2048 + tt * 512: ob * 2048 + tt * 512 + 512],
                py[:, :], cbias[:, ob:ob + 1],
                xcif[:, ob * 2048 + tt * 512: ob * 2048 + tt * 512 + 512],
                ALU.add, ALU.add))
    for e in ("t", "v", "sc"):
        P.wait(e, vY, P.n(vY))
    lnb2 = {"zb": zb, "zsq": zsq, "statr": statr, "xnt": xnt,
            "statcol": statcol, "onesrow": onesrow, "epsr": epsr}

    st5 = [0]

    def sink(sub, oc, par):
        spar = st5[0] % 2
        st5[0] += 1
        P.wait("s", "p4lnvH", P.n("p4lnvH"))
        P.step("s", so5[spar], lambda s, oc=oc, sub=sub, par=par: s.dma_start(
            out=out_xD[oc * 128:(oc + 1) * 128, sub * 512: sub * 512 + 512],
            in_=xnt[par][:, 0:512]), 16)
        return [(so5[spar], P.n(so5[spar]))]

    emit_ln_fm(P, "p4ln", pool, lnb2, 4,
               lambda oc, sub: zbuf[:, oc * 2048 + sub * 512: oc * 2048 + sub * 512 + 512],
               sink, w=512)
    P.wait("s", so5[0], P.n(so5[0]))
    P.wait("s", so5[1], P.n(so5[1]))
    P.emit("s", lambda s: s.nop())
    return [so5[0], so5[1]]


def build_p4a():
    nc = bass.Bass("TRN2", target_bir_lowering=False)
    dram = {
        "p4_wd1t": nc.dram_tensor("p4_wd1t", [8 * 2048, 128], F32, kind="ExternalInput").ap(),
        "p4_wd2t": nc.dram_tensor("p4_wd2t", [8 * 2048, 128], F32, kind="ExternalInput").ap(),
        "p4_bd1": nc.dram_tensor("p4_bd1", [128, 16], F32, kind="ExternalInput").ap(),
        "p4_bd2": nc.dram_tensor("p4_bd2", [128, 64], F32, kind="ExternalInput").ap(),
        "c_onecol": nc.dram_tensor("c_onecol", [128, 1], BF16, kind="ExternalInput").ap(),
    }
    xci = nc.dram_tensor("p4_xci", [D, ROWS], F32, kind="ExternalInput").ap()
    out_loss = nc.dram_tensor("p4_loss", [1, 8], F32, kind="ExternalOutput").ap()
    stack = ExitStack()
    with stack:
        A = Arena(nc, stack, "arena", 104448)
        pool_t = stack.enter_context(nc.psum_tensor("pool", [128, 4096], F32))
        P = Plan(nc, stack)
        emit_p4a(P, nc, A, pool_t, dram, xci, out_loss)
        block = stack.enter_context(nc.Block())
        P.realize(block)
    return nc


def build_p4b():
    nc = bass.Bass("TRN2", target_bir_lowering=False)
    dram = {
        "p4_wbt": nc.dram_tensor("p4_wbt", [8 * 1024, 128], F32, kind="ExternalInput").ap(),
        "p4_bb": nc.dram_tensor("p4_bb", [128, 8], F32, kind="ExternalInput").ap(),
        "p4_wad": nc.dram_tensor("p4_wad", [8 * 1024, 64], F32, kind="ExternalInput").ap(),
        "p4_bad": nc.dram_tensor("p4_bad", [8 * 64, 1], F32, kind="ExternalInput").ap(),
        "p4_wau": nc.dram_tensor("p4_wau", [8 * 64, 1024], F32, kind="ExternalInput").ap(),
        "p4_bau": nc.dram_tensor("p4_bau", [8 * 128, 8], F32, kind="ExternalInput").ap(),
        "p4_iota": nc.dram_tensor("p4_iota", [128, 1], F32, kind="ExternalInput").ap(),
        "p4_erow": nc.dram_tensor("p4_erow", [1, 8], F32, kind="ExternalInput").ap(),
        "c_ones_row": nc.dram_tensor("c_ones_row", [1, 128], F32, kind="ExternalInput").ap(),
        "c_statcol": nc.dram_tensor("c_statcol", [128, 1], BF16, kind="ExternalInput").ap(),
        "c_eps": nc.dram_tensor("c_eps", [1, 1], F32, kind="ExternalInput").ap(),
    }
    xci = nc.dram_tensor("p4_xci", [D, ROWS], F32, kind="ExternalInput").ap()
    parts = nc.dram_tensor("p4_parts", [1, 32], F32, kind="ExternalInput").ap()
    out_xD = nc.dram_tensor("p4_xD", [D, ROWS], F32, kind="ExternalOutput").ap()
    stack = ExitStack()
    with stack:
        A = Arena(nc, stack, "arena", 104448)
        pool_t = stack.enter_context(nc.psum_tensor("pool", [128, 4096], F32))
        P = Plan(nc, stack)
        emit_p4b(P, nc, A, pool_t, dram, xci, parts, out_xD)
        block = stack.enter_context(nc.Block())
        P.realize(block)
    return nc



_NC_CACHE = {}


def _get_nc(name, builder):
    if name not in _NC_CACHE:
        _NC_CACHE[name] = builder()
    return _NC_CACHE[name]


def kernel_unfused(**inputs):
    maps0 = stage_p0(inputs)
    nc0 = _get_nc("p0", build_p0)
    res0 = run_bass_kernel_spmd(nc0, maps0, core_ids=list(range(NCORE)))
    xA = [res0.results[c]["p0_xA"] for c in range(NCORE)]

    maps1 = stage_p1(inputs)
    for c in range(NCORE):
        maps1[c]["p1_xA"] = np.ascontiguousarray(xA[c])
    nc1 = _get_nc("p1", build_p1)
    res1 = run_bass_kernel_spmd(nc1, maps1, core_ids=list(range(NCORE)))
    xB = [res1.results[c]["p1_xB"] for c in range(NCORE)]
    kf = [res1.results[c]["p1_kf"] for c in range(NCORE)]
    vt = [res1.results[c]["p1_vt"] for c in range(NCORE)]

    kf_full = [np.ascontiguousarray(np.concatenate(kf[4 * b:4 * b + 4], axis=1)) for b in range(B)]
    vt_full = [np.ascontiguousarray(np.concatenate(vt[4 * b:4 * b + 4], axis=0)) for b in range(B)]
    xB_full = [np.concatenate(xB[4 * b:4 * b + 4], axis=1) for b in range(B)]

    maps3 = stage_p3(inputs)
    for c in range(NCORE):
        b, cp = c // 4, c % 4
        perm = p3_perm_rows(cp)
        maps3[c]["p3_xbi"] = np.ascontiguousarray(xB_full[b][:, perm])
        maps3[c]["p3_kf"] = kf_full[b]
        maps3[c]["p3_vt"] = vt_full[b]
    nc3 = _get_nc("p3", build_p3)
    res3 = run_bass_kernel_spmd(nc3, maps3, core_ids=list(range(NCORE)))
    xC = [res3.results[c]["p3_xC"] for c in range(NCORE)]

    maps4 = stage_p4(inputs)
    maps4a = []
    for c in range(NCORE):
        m = {k: maps4[c][k] for k in ("p4_wd1t", "p4_wd2t", "p4_bd1", "p4_bd2", "c_onecol")}
        m["p4_xci"] = np.ascontiguousarray(xC[c])
        maps4a.append(m)
    nc4a = _get_nc("p4a", build_p4a)
    res4a = run_bass_kernel_spmd(nc4a, maps4a, core_ids=list(range(NCORE)))
    parts = [res4a.results[c]["p4_loss"] for c in range(NCORE)]

    maps4b = []
    for c in range(NCORE):
        b = c // 4
        m = {k: maps4[c][k] for k in ("p4_wbt", "p4_bb", "p4_wad", "p4_bad", "p4_wau",
                                      "p4_bau", "p4_iota", "p4_erow", "c_ones_row",
                                      "c_statcol", "c_eps")}
        m["p4_xci"] = np.ascontiguousarray(xC[c])
        m["p4_parts"] = np.ascontiguousarray(
            np.concatenate([parts[4 * b + j] for j in range(4)], axis=1))
        maps4b.append(m)
    nc4b = _get_nc("p4b", build_p4b)
    res4b = run_bass_kernel_spmd(nc4b, maps4b, core_ids=list(range(NCORE)))

    out = np.empty((B, T, D), dtype=np.float32)
    for c in range(NCORE):
        b, cp = c // 4, c % 4
        perm = p3_perm_rows(cp)
        out[b, perm, :] = res4b.results[c]["p4_xD"].T
    return out



def _stage_idx_tab(cp):
    tab = np.zeros((128, 128), dtype=np.uint32)
    p = np.arange(128)
    blocks = []
    for m in range(8):
        blocks.extend([8 * m + cp, 8 * m + 7 - cp])
    for s_, j in enumerate(blocks):
        r, blk2 = j // 16, (j % 16) % 8
        for dc in range(8):
            tab[:, s_ * 8 + dc] = (r * 1024 + dc * 128 + p) * 8 + blk2
    return tab


def stage_fused(inputs):
    m0 = stage_p0(inputs)
    m1 = stage_p1(inputs)
    m3 = stage_p3(inputs)
    m4 = stage_p4(inputs)
    maps = []
    for c in range(NCORE):
        m = {}
        m.update(m0[c])
        m.update(m1[c])
        m.update(m3[c])
        m.update(m4[c])
        m["fx_idxtab"] = _stage_idx_tab(c % 4)
        maps.append(m)
    return maps


def build_fused():
    nc = bass.Bass("TRN2", target_bir_lowering=False)
    dram = {}
    dram.update({
        "p0_xT": nc.dram_tensor("p0_xT", [D, TK], F32, kind="ExternalInput").ap(),
        "p0_wqkT": nc.dram_tensor("p0_wqkT", [16 * D, 128], F32, kind="ExternalInput").ap(),
        "p0_wvT": nc.dram_tensor("p0_wvT", [D, 1024], F32, kind="ExternalInput").ap(),
        "p0_woutT": nc.dram_tensor("p0_woutT", [8 * D, 128], F32, kind="ExternalInput").ap(),
        "p0_cos": nc.dram_tensor("p0_cos", [128, TK], F32, kind="ExternalInput").ap(),
        "p0_sin": nc.dram_tensor("p0_sin", [128, TK], F32, kind="ExternalInput").ap(),
        "p0_perm": nc.dram_tensor("p0_perm", [128, 128], BF16, kind="ExternalInput").ap(),
        "p0_mask": nc.dram_tensor("p0_mask", [128, 12 * 512], BF16, kind="ExternalInput").ap(),
        "c_sel": nc.dram_tensor("c_sel", [1, 256], F32, kind="ExternalInput").ap(),
        "c_ones_row": nc.dram_tensor("c_ones_row", [1, 128], F32, kind="ExternalInput").ap(),
        "c_statcol": nc.dram_tensor("c_statcol", [128, 1], BF16, kind="ExternalInput").ap(),
        "c_onecol": nc.dram_tensor("c_onecol", [128, 1], BF16, kind="ExternalInput").ap(),
        "c_eps": nc.dram_tensor("c_eps", [1, 1], F32, kind="ExternalInput").ap(),
    })
    dram.update(p1_dram_decls(nc))
    dram.update(p3_dram_decls(nc))
    dram.update({
        "p4_wd1t": nc.dram_tensor("p4_wd1t", [8 * 2048, 128], F32, kind="ExternalInput").ap(),
        "p4_wd2t": nc.dram_tensor("p4_wd2t", [8 * 2048, 128], F32, kind="ExternalInput").ap(),
        "p4_bd1": nc.dram_tensor("p4_bd1", [128, 16], F32, kind="ExternalInput").ap(),
        "p4_bd2": nc.dram_tensor("p4_bd2", [128, 64], F32, kind="ExternalInput").ap(),
        "p4_wbt": nc.dram_tensor("p4_wbt", [8 * 1024, 128], F32, kind="ExternalInput").ap(),
        "p4_bb": nc.dram_tensor("p4_bb", [128, 8], F32, kind="ExternalInput").ap(),
        "p4_wad": nc.dram_tensor("p4_wad", [8 * 1024, 64], F32, kind="ExternalInput").ap(),
        "p4_bad": nc.dram_tensor("p4_bad", [8 * 64, 1], F32, kind="ExternalInput").ap(),
        "p4_wau": nc.dram_tensor("p4_wau", [8 * 64, 1024], F32, kind="ExternalInput").ap(),
        "p4_bau": nc.dram_tensor("p4_bau", [8 * 128, 8], F32, kind="ExternalInput").ap(),
        "p4_iota": nc.dram_tensor("p4_iota", [128, 1], F32, kind="ExternalInput").ap(),
        "p4_erow": nc.dram_tensor("p4_erow", [1, 8], F32, kind="ExternalInput").ap(),
        "fx_idxtab": nc.dram_tensor("fx_idxtab", [128, 128], U32, kind="ExternalInput").ap(),
    })
    xA_d = nc.dram_tensor("fx_xA", [D, ROWS], F32).ap()
    xB_d = [nc.dram_tensor(f"fx_xB{r}", [D, 1024], F32) for r in range(2)]
    kf_d = [nc.dram_tensor(f"fx_kf{r}", [D, 1024], BF16) for r in range(2)]
    vt_d = [nc.dram_tensor(f"fx_vt{r}", [1024, D], BF16) for r in range(2)]
    xbag = [nc.dram_tensor(f"fx_xbag{r}", [4 * D, 1024], F32) for r in range(2)]
    kfag = [nc.dram_tensor(f"fx_kfag{r}", [4 * D, 1024], BF16) for r in range(2)]
    vtag = [nc.dram_tensor(f"fx_vtag{r}", [4 * 1024, D], BF16) for r in range(2)]
    loss_d = nc.dram_tensor("fx_loss", [1, 8], F32)
    lossag = nc.dram_tensor("fx_lossag", [4, 8], F32)
    xC_d = nc.dram_tensor("fx_xC", [D, ROWS], F32).ap()
    out_xD = nc.dram_tensor("fx_xD", [D, ROWS], F32, kind="ExternalOutput").ap()

    groups = [[0, 1, 2, 3], [4, 5, 6, 7]]
    stack = ExitStack()
    with stack:
        A = Arena(nc, stack, "arena", 104448)
        pool_t = stack.enter_context(nc.psum_tensor("pool", [128, 4096], F32))
        P = Plan(nc, stack)

        def fence(terms):
            for e in ENGS:
                for s_ in terms:
                    P.wait(e, s_)

        rdy = P.new_sem("fxrdy")
        clr = P.new_sem("fxclr")

        def recycle(released):
            P.phase_reset()
            if not released:
                return
            for e in ENGS:
                P.step(e, rdy, lambda x: x.nop())
            tgt = P.n(rdy)
            P.wait("g", rdy, tgt)
            for nm_h, nm_n in released:
                P.emit("g", lambda g, h=nm_h, v=nm_n: g.wait_ge(h, v) if v > 0 else g.nop())
            for i, (h, n) in enumerate(released):
                if i == len(released) - 1:
                    P.step("g", clr, lambda g, h=h: g.sem_clear(h))
                else:
                    P.emit("g", lambda g, h=h: g.sem_clear(h))
            for e in ENGS:
                P.wait(e, clr, P.n(clr))

        base = A.off
        mk = P.mark()
        t0 = emit_p0(P, nc, A, pool_t, dram, xA_d)
        fence(t0)
        rel = P.release_since(mk)
        recycle(rel)
        A.off = base
        cc = [P.new_sem(f"fxcc{i}") for i in range(7)]
        mk = P.mark()

        def rt_done(rt, store_sems):
            for s_ in store_sems[0:2]:
                P.wait("g", s_, P.n(s_))
            P.step("g", cc[3 * rt + 0], lambda g, rt=rt: g.collective_compute(
                "AllGather", ALU.bypass, replica_groups=groups,
                ins=[xB_d[rt].ap().opt()], outs=[xbag[rt].ap().opt()]), 1)
            for s_ in store_sems[2:4]:
                P.wait("g", s_, P.n(s_))
            P.step("g", cc[3 * rt + 1], lambda g, rt=rt: g.collective_compute(
                "AllGather", ALU.bypass, replica_groups=groups,
                ins=[kf_d[rt].ap().opt()], outs=[kfag[rt].ap().opt()]), 1)
            for s_ in store_sems[4:6]:
                P.wait("g", s_, P.n(s_))
            P.step("g", cc[3 * rt + 2], lambda g, rt=rt: g.collective_compute(
                "AllGather", ALU.bypass, replica_groups=groups,
                ins=[vt_d[rt].ap().opt()], outs=[vtag[rt].ap().opt()]), 1)

        t1 = emit_p1(P, nc, A, pool_t, dram, xA_d,
                     [xB_d[0].ap(), xB_d[1].ap()],
                     [kf_d[0].ap(), kf_d[1].ap()],
                     [vt_d[0].ap(), vt_d[1].ap()], rt_done_cb=rt_done)
        fence(t1)
        rel = P.release_since(mk)
        recycle(rel)
        A.off = base
        mk = P.mark()
        t3 = emit_p3(P, nc, A, pool_t, dram, None,
                     [kfag[0].ap(), kfag[1].ap()], [vtag[0].ap(), vtag[1].ap()], xC_d,
                     fused=True, xb_ag=[xbag[0].ap(), xbag[1].ap()],
                     idx_tab=dram["fx_idxtab"],
                     ag_waits={"xb": [(cc[0], 1), (cc[3], 1)],
                               "kv": [(cc[1], 1), (cc[2], 1), (cc[4], 1), (cc[5], 1)]})
        fence(t3)
        rel = P.release_since(mk)
        recycle(rel)
        A.off = base
        mk = P.mark()
        t4 = emit_p4a(P, nc, A, pool_t, dram, xC_d, loss_d.ap())
        fence(t4)
        rel = P.release_since(mk)
        recycle(rel)
        P.step("g", cc[6], lambda g: g.collective_compute(
            "AllGather", ALU.bypass, replica_groups=groups,
            ins=[loss_d.ap().opt()], outs=[lossag.ap().opt()]), 1)
        A.off = base
        parts_aps = [lossag.ap()[j:j + 1, :] for j in range(4)]
        t5 = emit_p4b(P, nc, A, pool_t, dram, xC_d, parts_aps, out_xD,
                      ag_wait=(cc[6], 1))
        fence(t5)
        block = stack.enter_context(nc.Block())
        P.realize(block)
    return nc


def kernel(**inputs):
    maps = stage_fused(inputs)
    nc = _get_nc("fused", build_fused)
    res = run_bass_kernel_spmd(nc, maps, core_ids=list(range(NCORE)))
    out = np.empty((B, T, D), dtype=np.float32)
    for c in range(NCORE):
        b, cp = c // 4, c % 4
        perm = p3_perm_rows(cp)
        out[b, perm, :] = res.results[c]["fx_xD"].T
    return out
